# version 10
# speedup vs baseline: 1.0558x; 1.0242x over previous
import contextlib
import math
import numpy as np
import ml_dtypes
import concourse.bass as bass
import concourse.mybir as mybir
from concourse.bass_utils import run_bass_kernel_spmd

F32 = mybir.dt.float32
BF16 = mybir.dt.bfloat16
AF = mybir.ActivationFunctionType
ALU = mybir.AluOpType
AX = mybir.AxisListType

NCORE = 8
BB, SS, DD = 2, 16384, 1024
TPC = 4096
NCH = 8
G = 512
NG = TPC // G
NEG = -3.0e38
SW = 513

_PROG_CACHE = {}


class Prog:

    def __init__(self, name):
        self.nc = bass.Bass("TRN2", target_bir_lowering=False, name=name)
        self.es = contextlib.ExitStack()
        self.block = None
        self.cnt = {}
        self.handles = {}
        self._scopes = []
        self._sid = 0
        self._pend = {"t": [], "v": [], "s": [], "g": [], "y": []}

    def sem(self, name):
        h = self.es.enter_context(self.nc.semaphore(name))
        self.cnt[name] = 0
        self.handles[name] = h
        return h

    def std_sems(self):
        for nm in ("qL0", "qL1", "qS0", "qS1", "qW", "sG", "sT", "sV", "sS", "sB"):
            self.sem(nm)

    def inc(self, name, n=1):
        self.cnt[name] += n
        return self.cnt[name]

    def push_scope(self):
        self._sid += 1
        self._scopes.append(contextlib.ExitStack())

    def pop_scope(self):
        self.flush()
        self._scopes.pop().close()

    def flush(self):
        emitters = {"t": lambda b: self.block.tensor(b),
                    "v": lambda b: self.block.vector(b),
                    "s": lambda b: self.block.scalar(b),
                    "g": lambda b: self.block.gpsimd(b),
                    "y": lambda b: self.block.sync(b)}

        def run_all(eng, bs):
            live = self.cnt
            try:
                for body, snap in bs:
                    self.cnt = snap
                    body(eng)
            finally:
                self.cnt = live

        for k, emit in emitters.items():
            bodies = self._pend[k]
            if bodies:
                self._pend[k] = []
                emit(lambda eng, bs=bodies: run_all(eng, bs))

    def sb(self, name, shape, dt):
        st = self._scopes[-1] if self._scopes else self.es
        nm = f"g{self._sid}_{name}" if self._scopes else name
        return st.enter_context(self.nc.sbuf_tensor(nm, shape, dt))

    def ps(self, name, shape, dt=F32):
        st = self._scopes[-1] if self._scopes else self.es
        nm = f"g{self._sid}_{name}" if self._scopes else name
        return st.enter_context(self.nc.psum_tensor(nm, shape, dt))

    def open(self):
        self.block = self.es.enter_context(self.nc.Block())
        return self.block

    def close(self):
        self.flush()
        self.es.close()

    def emit_barrier(self):
        self.flush()
        sB = self.handles["sB"]
        dma_tgts = [(self.handles[nm], self.cnt[nm])
                    for nm in ("qL0", "qL1", "qS0", "qS1", "qW")]

        def g_b(g_):
            for h, v in dma_tgts:
                g_.wait_ge(h, v)
            g_.engine_nop().then_inc(sB, 1)
        self.gsec(g_b)

        def t_b(t):
            t.nop().then_inc(sB, 1)
        self.tsec(t_b)

        def v_b(v):
            v.drain()
            v.engine_nop().then_inc(sB, 1)
        self.vsec(v_b)

        def s_b(s):
            s.drain()
            s.nop().then_inc(sB, 1)
        self.ssec(s_b)

        def y_b(y):
            y.nop().then_inc(sB, 1)
        self.ysec(y_b)

        tgt = self.inc("sB", 5)

        def g_w(g_):
            g_.wait_ge(sB, tgt)
        self.gsec(g_w)

        def t_w(t):
            t.wait_ge(sB, tgt)
        self.tsec(t_w)

        def v_w(v):
            v.wait_ge(sB, tgt)
        self.vsec(v_w)

        def s_w(s):
            s.wait_ge(sB, tgt)
        self.ssec(s_w)

        def y_w(y):
            y.wait_ge(sB, tgt)
        self.ysec(y_w)

    def tsec(self, body):
        self._pend["t"].append((body, dict(self.cnt)))

    def vsec(self, body):
        self._pend["v"].append((body, dict(self.cnt)))

    def ssec(self, body):
        self._pend["s"].append((body, dict(self.cnt)))

    def gsec(self, body):
        self._pend["g"].append((body, dict(self.cnt)))

    def ysec(self, body):
        self._pend["y"].append((body, dict(self.cnt)))


def _chsl(ch, width=G, base=0):
    return slice(base + ch * width, base + (ch + 1) * width)


def host_constants():
    c = {}
    c["ident_f32"] = np.eye(128, dtype=np.float32)
    ut = np.triu(np.ones((128, 128), np.float32))
    c["ut_f32"] = ut
    c["ut_bf"] = ut.astype(ml_dtypes.bfloat16)
    p = np.arange(128)[:, None]
    r = np.arange(128)[None, :]
    c["mask_b0"] = np.where(r >= p, 0.0, NEG).astype(np.float32)
    c["lneg_bf"] = np.where(p > r, NEG, 0.0).astype(ml_dtypes.bfloat16)
    c["mask_b4"] = np.where(r <= p, 0.0, NEG).astype(np.float32)
    m64 = np.zeros((64, 64), np.float32)
    for d in range(32):
        m64[d, d + 32] = -1.0
        m64[d + 32, d] = 1.0
    m128 = np.zeros((128, 128), np.float32)
    m128[0:64, 0:64] = m64
    m128[64:128, 64:128] = m64
    c["rot_lhsT_bf"] = np.ascontiguousarray(m128.T).astype(ml_dtypes.bfloat16)
    hsel = np.zeros((128, 4 * 8), np.float32)
    for cc in range(4):
        hsel[0:64, cc * 8 + 2 * cc] = 1.0
        hsel[64:128, cc * 8 + 2 * cc + 1] = 1.0
    c["hsel"] = hsel
    c["nhsel"] = -hsel
    dsel = np.zeros((8, 8 * 128), np.float32)
    for h in range(8):
        dsel[h, h * 128:(h + 1) * 128] = 1.0
    c["dsel"] = dsel
    DH, WINDOW, n = 64, 512, 16384
    half = np.arange(0, DH, 2, dtype=np.float64)
    inv_freq = 1.0 / 10000.0 ** (half / DH)
    t = np.arange(n, dtype=np.float64)
    freqs = t[:, None] * inv_freq[None, :]
    freqs = np.concatenate([freqs, freqs], axis=-1)
    base = (half + 0.4 * DH) / (1.4 * DH)
    power = (t - n // 2) / float(WINDOW // 2)
    scale = base[None, :] ** power[:, None]
    scale = np.concatenate([scale, scale], axis=-1)
    cos, sin = np.cos(freqs), np.sin(freqs)
    qc = (cos * scale).T
    qs = (sin * scale).T
    kc = (cos / scale).T
    ks = (sin / scale).T
    for nm, tab in [("tab_qcos", qc), ("tab_qsin", qs), ("tab_kcos", kc), ("tab_ksin", ks)]:
        full = np.concatenate([tab, tab], axis=0).astype(np.float32)
        c[nm] = full.astype(ml_dtypes.bfloat16)
    return c


def to_fm(x_bt_d):
    t = np.ascontiguousarray(x_bt_d.T)
    return t.reshape(NCH, 128, -1)


def from_fm(fm):
    d, t = DD, fm.shape[2]
    return np.ascontiguousarray(fm.reshape(d, t).T)


def emit_ln_group(P, bufs, W, g, soff=None, sstride=None, doff=0, dstride=None,
                  want_uf=False, affine=False, skip_cast=False):
    xt, x_bf, x2_bf, u_bf = bufs["xt"], bufs["x_bf"], bufs["x2_bf"], bufs["u_bf"]
    Bmu, Bvar, Brstd, tA = bufs["Bmu"], bufs["Bvar"], bufs["Brstd"], bufs["tA"]
    tB = bufs["tB"]
    ones_bf, psB = bufs["ones_bf"], bufs["psB"]
    sV, sS, sT = P.handles["sV"], P.handles["sS"], P.handles["sT"]
    if sstride is None:
        sstride = W
    if dstride is None:
        dstride = W
    if soff is None:
        soff = 0
    offf = soff if callable(soff) else (lambda ch: soff)

    def ssl(ch):
        o = ch * sstride + offf(ch)
        return slice(o, o + W)

    if not skip_cast:
        def v_cast(v):
            v.tensor_copy(x_bf[:, 0:NCH * sstride], xt[:, 0:NCH * sstride]).then_inc(sV, 1)
        P.vsec(v_cast)
        vcast = P.inc("sV")

        def s_sq(s):
            s.wait_ge(sV, vcast)
            s.activation(x2_bf[:, 0:NCH * sstride], x_bf[:, 0:NCH * sstride],
                         AF.Square).then_inc(sS, 1)
        P.ssec(s_sq)
        ssq = P.inc("sS")
    else:
        ssq = bufs["cast_ready"]

    if bufs.get("seq_stats"):
        def t_st1(t):
            t.wait_ge(sS, ssq)
            for ch in range(NCH):
                i = t.matmul(psB[:, 0:W], ones_bf[:, :], x_bf[:, ssl(ch)],
                             start=(ch == 0), stop=(ch == NCH - 1))
            i.then_inc(sT, 1)
        P.tsec(t_st1)
        tst1 = P.inc("sT")

        def s_ev1(s):
            s.wait_ge(sT, tst1)
            s.activation(Bmu[:, 0:W], psB[:, 0:W], AF.Identity, scale=1.0 / DD).then_inc(sS, 1)
        P.ssec(s_ev1)
        sev1 = P.inc("sS")

        def t_st2(t):
            t.wait_ge(sS, sev1)
            for ch in range(NCH):
                i = t.matmul(psB[:, 0:W], ones_bf[:, :], x2_bf[:, ssl(ch)],
                             start=(ch == 0), stop=(ch == NCH - 1))
            i.then_inc(sT, 1)
        P.tsec(t_st2)
        tstat = P.inc("sT")

        def s_evac(s):
            s.wait_ge(sT, tstat)
            s.activation(Bvar[:, 0:W], psB[:, 0:W], AF.Identity, scale=1.0 / DD).then_inc(sS, 1)
        P.ssec(s_evac)
        sevac = P.inc("sS")
    else:
        def t_stats(t):
            t.wait_ge(sS, ssq)
            for ch in range(NCH):
                t.matmul(psB[:, 0:W], ones_bf[:, :], x_bf[:, ssl(ch)],
                         start=(ch == 0), stop=(ch == NCH - 1))
            for ch in range(NCH):
                i = t.matmul(psB[:, G:G + W], ones_bf[:, :], x2_bf[:, ssl(ch)],
                             start=(ch == 0), stop=(ch == NCH - 1))
            i.then_inc(sT, 1)
        P.tsec(t_stats)
        tstat = P.inc("sT")

        def s_evac(s):
            s.wait_ge(sT, tstat)
            s.activation(Bmu[:, 0:W], psB[:, 0:W], AF.Identity, scale=1.0 / DD)
            s.activation(Bvar[:, 0:W], psB[:, G:G + W], AF.Identity, scale=1.0 / DD).then_inc(sS, 1)
        P.ssec(s_evac)
        sevac = P.inc("sS")

    def v_var(v):
        v.wait_ge(sS, sevac)
        v.tensor_tensor(tA[:, 0:W], Bmu[:, 0:W], Bmu[:, 0:W], ALU.mult)
        v.drain()
        v.tensor_tensor(Bvar[:, 0:W], Bvar[:, 0:W], tA[:, 0:W], ALU.subtract).then_inc(sV, 1)
    P.vsec(v_var)
    vvar = P.inc("sV")

    def s_sd(s):
        s.wait_ge(sV, vvar)
        s.activation(Bvar[:, 0:W], Bvar[:, 0:W], AF.Sqrt, bias=bufs["epscol"][:, 0:1]).then_inc(sS, 1)
    P.ssec(s_sd)
    ssd = P.inc("sS")

    def v_apply(v):
        v.wait_ge(sS, ssd)
        v.reciprocal(Brstd[:, 0:W], Bvar[:, 0:W])
        v.drain()
        for ch in range(NCH):
            sc = tA if (ch % 2 == 0) else tB
            v.tensor_tensor(sc[:, 0:W], xt[:, ssl(ch)], Bmu[:, 0:W], ALU.subtract)
            v.drain()
            dsl = slice(ch * dstride + doff, ch * dstride + doff + W)
            if affine:
                v.tensor_tensor(sc[:, 0:W], sc[:, 0:W], Brstd[:, 0:W], ALU.mult)
                v.drain()
                i = v.tensor_scalar(u_bf[:, dsl], sc[:, 0:W],
                                    bufs["lg"][:, ch:ch + 1], bufs["lb"][:, ch:ch + 1],
                                    ALU.mult, ALU.add)
                if want_uf:
                    v.drain()
                    i = v.tensor_copy(bufs["uf"][:, ch * W:(ch + 1) * W], u_bf[:, dsl])
            else:
                if want_uf:
                    i = v.tensor_tensor(bufs["uf"][:, ch * W:(ch + 1) * W], sc[:, 0:W],
                                        Brstd[:, 0:W], ALU.mult)
                    v.drain()
                    i = v.tensor_copy(u_bf[:, dsl], bufs["uf"][:, ch * W:(ch + 1) * W])
                else:
                    i = v.tensor_tensor(u_bf[:, dsl], sc[:, 0:W], Brstd[:, 0:W], ALU.mult)
        i.then_inc(sV, 1)
    P.vsec(v_apply)
    return P.inc("sV")


def emit_l1ffn(P, ext):
    nc = P.nc
    xin = ext["xin"]
    w1 = ext["w1"]
    w2 = ext["w2"]
    b1t = ext["b1t"]
    b2t = ext["b2t"]
    xout = ext["xout"]

    HALF = 16

    P.push_scope()
    sw1 = P.sb("sw1", [128, NCH * HALF * 128], BF16)
    sw2 = P.sb("sw2", [128, HALF * 1024], BF16)
    sb1 = P.sb("sb1", [128, 32], F32)
    sb2h = P.sb("sb2h", [128, 8], F32)
    ones_bf = P.sb("ones_bf", [128, 128], BF16)
    xt = [P.sb(f"xt{i}", [128, TPC], F32) for i in range(2)]
    xtln = [P.sb(f"xtln{i}", [128, TPC], F32) for i in range(2)]
    x_bf = P.sb("xbf", [128, TPC], BF16)
    x2_bf = P.sb("x2bf", [128, TPC], BF16)
    u_bf = [P.sb(f"ubf{i}", [128, TPC], BF16) for i in range(2)]
    mid = P.sb("mid", [128, HALF * G], BF16)
    Bmu = P.sb("Bmu", [128, G], F32)
    Bvar = P.sb("Bvar", [128, G], F32)
    Brstd = P.sb("Brstd", [128, G], F32)
    tA = P.sb("tA", [128, G], F32)
    tB = P.sb("tB", [128, G], F32)
    tO = [P.sb(f"tO{i}", [128, G], F32) for i in range(2)]
    epscol = P.sb("epscol", [128, 1], F32)

    psB = P.ps("psB", [128, 2 * G])
    psA = [P.ps(f"psA{i}", [128, G]) for i in range(2)]
    psO = [P.ps(f"psO{i}", [128, G]) for i in range(2)]

    qL = [P.handles["qL0"], P.handles["qL1"]]
    qS = [P.handles["qS0"], P.handles["qS1"]]
    qW = P.handles["qW"]
    sG = P.handles["sG"]
    sT = P.handles["sT"]; sV = P.handles["sV"]; sS = P.handles["sS"]

    def g_pro(g_):
        g_.dma_start(sb1[:, :], b1t[:, :]).then_inc(qW, 16)
        g_.dma_start(sb2h[:, :], b2t[:, :]).then_inc(qW, 16)
        g_.memset(ones_bf[:, :], 1.0)
        g_.memset(epscol[:, :], 1e-5).then_inc(sG, 1)
    P.gsec(g_pro)
    bready = P.inc("qW", 32)
    gready = P.inc("sG")

    load_done = {}
    store_done = {}
    mid_evac = {}
    out_evac = {}
    amm_done = {}
    w2mm_done = {}
    xadd_done = {}
    uapply = {}
    wready = {}

    for half in range(2):
        def g_w(g_, half=half):
            if half == 1:
                g_.wait_ge(P.handles["sT"], w2mm_done[(0, NG - 1, NCH - 1)])
            for a in range(NCH):
                g_.dma_start(sw1[:, a * HALF * 128:(a + 1) * HALF * 128],
                             w1[a * 128:(a + 1) * 128,
                                half * HALF * 128:(half + 1) * HALF * 128]
                             ).then_inc(qW, 16)
            for a4 in range(4):
                g_.dma_start(
                    sw2[:, a4 * 4096:(a4 + 1) * 4096].rearrange("p (a n) -> p a n", a=4),
                    w2[half * 2048 + a4 * 512: half * 2048 + (a4 + 1) * 512, :]
                    .rearrange("(a p) n -> p a n", p=128)).then_inc(qW, 16)
        P.gsec(g_w)
        wready[half] = P.inc("qW", 12 * 16)

        if half == 0:
            def v_pro(v):
                v.wait_ge(qW, wready[0])
                v.wait_ge(sG, gready)
                v.tensor_scalar(sb2h[:, :], sb2h[:, :], 0.5, None, ALU.mult).then_inc(sV, 1)
            P.vsec(v_pro)
            b2ready = P.inc("sV")

        for g in range(NG):
            pb = g % 2
            def y_load(y, half=half, g=g, pb=pb):
                if (half, g - 2) in store_done:
                    y.wait_ge(*store_done[(half, g - 2)])
                elif half == 1:
                    y.wait_ge(*store_done[(0, NG - 2 + g)])
                if half == 0:
                    y.dma_start(xt[pb][:, :].rearrange("p (c t) -> p c t", c=NCH),
                                xin[:, :, g * G:(g + 1) * G].rearrange("c p t -> p c t")
                                ).then_inc(qL[g % 2], 16)
                else:
                    y.wait_ge(*store_done[(0, g)])
                    y.dma_start(xt[pb][:, :].rearrange("p (c t) -> p c t", c=NCH),
                                xout[:, :, g * G:(g + 1) * G].rearrange("c p t -> p c t")
                                ).then_inc(qL[g % 2], 16)
                    y.dma_start(xtln[pb][:, :].rearrange("p (c t) -> p c t", c=NCH),
                                xin[:, :, g * G:(g + 1) * G].rearrange("c p t -> p c t")
                                ).then_inc(qL[g % 2], 16)
            P.ysec(y_load)
            load_done[(half, g)] = (qL[g % 2], P.inc("qL%d" % (g % 2), 16 * (1 if half == 0 else 2)))

            def v_wait(v, half=half, g=g):
                v.wait_ge(*load_done[(half, g)])
                v.wait_ge(qW, wready[half])
                key2 = (half, g - 2) if g >= 2 else ((half - 1, NG - 2 + g) if half == 1 else None)
                if key2 is not None:
                    v.wait_ge(sT, amm_done[(key2[0], key2[1], HALF - 1)])
            P.vsec(v_wait)

            lnsrc = xt[pb] if half == 0 else xtln[pb]
            bufs = dict(xt=lnsrc, x_bf=x_bf, x2_bf=x2_bf, u_bf=u_bf[pb],
                        Bmu=Bmu, Bvar=Bvar, Brstd=Brstd, tA=tA, tB=tB,
                        ones_bf=ones_bf, psB=psB, epscol=epscol)
            uapply[(half, g)] = emit_ln_group(P, bufs, G, 0)

            for hc in range(HALF):
                hcg = half * HALF + hc
                ap = hc % 2

                def t_amm(t, half=half, g=g, hc=hc, ap=ap, pb=pb):
                    if hc == 0:
                        t.wait_ge(sV, uapply[(half, g)])
                    if hc >= 2:
                        key2 = (half, g, hc - 2)
                    elif (half, g - 1, HALF - 2 + hc) in mid_evac:
                        key2 = (half, g - 1, HALF - 2 + hc)
                    elif half == 1:
                        key2 = (0, NG - 1, HALF - 2 + hc)
                    else:
                        key2 = None
                    if key2 is not None:
                        t.wait_ge(sS, mid_evac[key2])
                    for kc in range(NCH):
                        i = t.matmul(psA[ap][:, :],
                                     sw1[:, kc * HALF * 128 + hc * 128:
                                         kc * HALF * 128 + hc * 128 + 128],
                                     u_bf[pb][:, _chsl(kc)],
                                     start=(kc == 0), stop=(kc == NCH - 1))
                    i.then_inc(sT, 1)
                P.tsec(t_amm)
                amm_done[(half, g, hc)] = P.inc("sT")

                def s_mid(s, half=half, g=g, hc=hc, hcg=hcg, ap=ap):
                    s.wait_ge(sT, amm_done[(half, g, hc)])
                    if hc == 0 and (half, g - 1, NCH - 1) in w2mm_done:
                        s.wait_ge(sT, w2mm_done[(half, g - 1, NCH - 1)])
                    elif hc == 0 and half == 1:
                        s.wait_ge(sT, w2mm_done[(0, NG - 1, NCH - 1)])
                    s.activation(mid[:, hc * G:(hc + 1) * G], psA[ap][:, :],
                                 AF.Relu, bias=sb1[:, hcg:hcg + 1]).then_inc(sS, 1)
                P.ssec(s_mid)
                mid_evac[(half, g, hc)] = P.inc("sS")

            for oc in range(NCH):
                op = oc % 2

                def t_w2(t, half=half, g=g, oc=oc, op=op):
                    t.wait_ge(sS, mid_evac[(half, g, HALF - 1)])
                    if oc >= 2:
                        key2 = (half, g, oc - 2)
                    elif (half, g - 1, NCH - 2 + oc) in out_evac:
                        key2 = (half, g - 1, NCH - 2 + oc)
                    elif half == 1:
                        key2 = (0, NG - 1, NCH - 2 + oc)
                    else:
                        key2 = None
                    if key2 is not None:
                        t.wait_ge(sS, out_evac[key2])
                    for hc in range(HALF):
                        i = t.matmul(psO[op][:, :],
                                     sw2[:, hc * 1024 + oc * 128: hc * 1024 + oc * 128 + 128],
                                     mid[:, hc * G:(hc + 1) * G],
                                     start=(hc == 0), stop=(hc == HALF - 1))
                    i.then_inc(sT, 1)
                P.tsec(t_w2)
                w2mm_done[(half, g, oc)] = P.inc("sT")

                def s_out(s, half=half, g=g, oc=oc, op=op):
                    s.wait_ge(sT, w2mm_done[(half, g, oc)])
                    if oc >= 2:
                        key2 = (half, g, oc - 2)
                    elif (half, g - 1, NCH - 2 + oc) in xadd_done:
                        key2 = (half, g - 1, NCH - 2 + oc)
                    elif half == 1:
                        key2 = (0, NG - 1, NCH - 2 + oc)
                    else:
                        key2 = None
                    if key2 is not None:
                        s.wait_ge(sV, xadd_done[key2])
                    else:
                        s.wait_ge(sV, b2ready)
                    if half == 0:
                        s.activation(tO[op][:, :], psO[op][:, :], AF.Identity,
                                     scale=0.5, bias=sb2h[:, oc:oc + 1]).then_inc(sS, 1)
                    else:
                        s.activation(tO[op][:, :], psO[op][:, :], AF.Identity,
                                     scale=0.5).then_inc(sS, 1)
                P.ssec(s_out)
                out_evac[(half, g, oc)] = P.inc("sS")

                def v_add(v, half=half, g=g, oc=oc, op=op, pb=pb):
                    v.wait_ge(sS, out_evac[(half, g, oc)])
                    v.tensor_tensor(xt[pb][:, _chsl(oc)], xt[pb][:, _chsl(oc)],
                                    tO[op][:, :], ALU.add).then_inc(sV, 1)
                P.vsec(v_add)
                xadd_done[(half, g, oc)] = P.inc("sV")

            def g_store(g_, half=half, g=g, pb=pb):
                g_.wait_ge(sV, xadd_done[(half, g, NCH - 1)])
                g_.dma_start(xout[:, :, g * G:(g + 1) * G].rearrange("c p t -> p c t"),
                             xt[pb][:, :].rearrange("p (c t) -> p c t", c=NCH)
                             ).then_inc(qS[g % 2], 16)
            P.gsec(g_store)
            store_done[(half, g)] = (qS[g % 2], P.inc("qS%d" % (g % 2), 16))

    def g_fin(g_):
        g_.wait_ge(qS[0], P.cnt["qS0"])
        g_.wait_ge(qS[1], P.cnt["qS1"])
    P.gsec(g_fin)


    P.pop_scope()
    P.emit_barrier()


def build_l1ffn():
    key = "l1ffn"
    if key in _PROG_CACHE:
        return _PROG_CACHE[key]
    P = Prog(key)
    nc = P.nc
    ext = {}
    ext["xin"] = nc.dram_tensor("xin", [NCH, 128, TPC], F32, kind="ExternalInput")
    ext["w1"] = nc.dram_tensor("w1", [DD, 4096], F32, kind="ExternalInput")
    ext["w2"] = nc.dram_tensor("w2", [4096, DD], F32, kind="ExternalInput")
    ext["b1t"] = nc.dram_tensor("b1t", [128, 32], F32, kind="ExternalInput")
    ext["b2t"] = nc.dram_tensor("b2t", [128, 8], F32, kind="ExternalInput")
    ext["xout"] = nc.dram_tensor("xout", [NCH, 128, TPC], F32, kind="ExternalOutput")
    P.std_sems()
    P.open()
    emit_l1ffn(P, ext)
    P.close()
    _PROG_CACHE[key] = P
    return P


def run_l1ffn(xs, w1, w2, b1, b2):
    P = build_l1ffn()
    b1t = np.ascontiguousarray(b1.reshape(32, 128).T)
    b2t = np.ascontiguousarray(b2.reshape(8, 128).T)
    in_maps = [{"xin": xs[c], "w1": w1, "w2": w2, "b1t": b1t, "b2t": b2t}
               for c in range(NCORE)]
    res = run_bass_kernel_spmd(P.nc, in_maps, core_ids=list(range(NCORE)))
    return [res.results[c]["xout"] for c in range(NCORE)]


def emit_l1attn_p1(P, ext):
    nc = P.nc
    xin = ext["xin"]
    wqT = ext["wqT"]
    wkT = ext["wkT"]
    wvT = ext["wvT"]
    qt_out = ext["qt_out"]
    kvp = ext["kvp"]

    P.push_scope()
    swq = P.sb("swq", [128, NCH * 1024], BF16)
    swk = P.sb("swk", [128, NCH * 1024], BF16)
    swv = P.sb("swv", [128, NCH * 1024], BF16)
    ones_bf = P.sb("ones_bf", [128, 128], BF16)
    epscol = P.sb("epscol", [128, 1], F32)
    xt = [P.sb(f"xt{i}", [128, TPC], F32) for i in range(2)]
    x_bf = P.sb("xbf", [128, TPC], BF16)
    x2_bf = P.sb("x2bf", [128, TPC], BF16)
    u_bf = [P.sb(f"ubf{i}", [128, TPC], BF16) for i in range(2)]
    qt = [P.sb(f"qt{i}", [128, TPC], BF16) for i in range(2)]
    kt_bf = P.sb("kt_bf", [128, 2 * 512], BF16)
    vt_bf = P.sb("vt_bf", [128, 2 * 512], BF16)
    kv_acc = P.sb("kv_acc", [64, 1024], F32)
    kv_out_s = P.sb("kv_out_s", [64, 1024], F32)
    Bmu = P.sb("Bmu", [128, G], F32)
    Bvar = P.sb("Bvar", [128, G], F32)
    Brstd = P.sb("Brstd", [128, G], F32)
    tA = P.sb("tA", [128, G], F32)
    tB = P.sb("tB", [128, G], F32)

    psB = P.ps("psB", [128, 2 * G])
    psQ = [P.ps(f"psQ{i}", [128, G]) for i in range(2)]
    psK = P.ps("psK", [128, 512])
    psV = P.ps("psV", [128, 512])
    psKV = P.ps("psKV", [64, 1024])

    qL = [P.handles["qL0"], P.handles["qL1"]]
    qS = [P.handles["qS0"], P.handles["qS1"]]
    qW = P.handles["qW"]
    sG = P.handles["sG"]
    sT = P.handles["sT"]; sV = P.handles["sV"]; sS = P.handles["sS"]

    def g_pro(g_):
        for a in range(NCH):
            g_.dma_start(swq[:, a * 1024:(a + 1) * 1024], wqT[a * 128:(a + 1) * 128, :]).then_inc(qW, 16)
            g_.dma_start(swk[:, a * 1024:(a + 1) * 1024], wkT[a * 128:(a + 1) * 128, :]).then_inc(qW, 16)
            g_.dma_start(swv[:, a * 1024:(a + 1) * 1024], wvT[a * 128:(a + 1) * 128, :]).then_inc(qW, 16)
        g_.memset(ones_bf[:, :], 1.0)
        g_.memset(epscol[:, :], 1e-5)
        g_.memset(kv_acc[:, :], 0.0).then_inc(sG, 1)
    P.gsec(g_pro)
    wready = P.inc("qW", 24 * 16)
    gready = P.inc("sG")

    load_done = {}
    qstore_done = {}
    uapply = {}
    qev = {}
    kvacc_done = {}
    kvmm_done = {}
    kvev_done = {}

    for g in range(NG):
        pb = g % 2

        def y_load(y, g=g, pb=pb):
            if g >= 2:
                y.wait_ge(*qstore_done[g - 2])
            y.dma_start(xt[pb][:, :].rearrange("p (c t) -> p c t", c=NCH),
                        xin[:, :, g * G:(g + 1) * G].rearrange("c p t -> p c t")
                        ).then_inc(qL[g % 2], 16)
        P.ysec(y_load)
        load_done[g] = (qL[g % 2], P.inc("qL%d" % (g % 2), 16))

        def v_wait(v, g=g):
            v.wait_ge(*load_done[g])
            v.wait_ge(qW, wready)
            v.wait_ge(sG, gready)
            if g >= 2:
                v.wait_ge(sT, kvmm_done[((g - 2), 3, 1, "proj")])
        P.vsec(v_wait)

        bufs = dict(xt=xt[pb], x_bf=x_bf, x2_bf=x2_bf, u_bf=u_bf[pb],
                    Bmu=Bmu, Bvar=Bvar, Brstd=Brstd, tA=tA, tB=tB,
                    ones_bf=ones_bf, psB=psB, epscol=epscol)
        uapply[g] = emit_ln_group(P, bufs, G, 0)

        for oc in range(NCH):
            op = oc % 2

            def t_q(t, g=g, oc=oc, op=op, pb=pb):
                if oc == 0:
                    t.wait_ge(sV, uapply[g])
                if oc >= 2:
                    t.wait_ge(sS, qev[(g, oc - 2)])
                elif g >= 1:
                    t.wait_ge(sS, qev[(g - 1, NCH - 2 + oc)])
                for kc in range(NCH):
                    i = t.matmul(psQ[op][:, :],
                                 swq[:, kc * 1024 + oc * 128: kc * 1024 + oc * 128 + 128],
                                 u_bf[pb][:, _chsl(kc)],
                                 start=(kc == 0), stop=(kc == NCH - 1))
                i.then_inc(sT, 1)
            P.tsec(t_q)
            tq = P.inc("sT")

            def s_qev(s, g=g, oc=oc, op=op, pb=pb, tq=tq):
                s.wait_ge(sT, tq)
                if g >= 2 and oc == 0:
                    s.wait_ge(*qstore_done[g - 2])
                s.activation(qt[pb][:, _chsl(oc)], psQ[op][:, :], AF.Identity).then_inc(sS, 1)
            P.ssec(s_qev)
            qev[(g, oc)] = P.inc("sS")

        for tl in range(4):
            for hh in range(2):
                tp = (2 * tl + hh) % 2

                def t_kv(t, g=g, tl=tl, hh=hh, tp=tp, pb=pb):
                    if tl == 0 and hh == 0:
                        t.wait_ge(sV, uapply[g])
                    idx = g * 8 + tl * 2 + hh
                    if idx >= 1:
                        t.wait_ge(sS, kvev_done[idx - 1])
                    for kc in range(NCH):
                        t.matmul(psK[:, :],
                                 u_bf[pb][:, kc * G + tl * 128: kc * G + tl * 128 + 128],
                                 swk[:, kc * 1024 + hh * 512: kc * 1024 + hh * 512 + 512],
                                 start=(kc == 0), stop=(kc == NCH - 1))
                    for kc in range(NCH):
                        i = t.matmul(psV[:, :],
                                     u_bf[pb][:, kc * G + tl * 128: kc * G + tl * 128 + 128],
                                     swv[:, kc * 1024 + hh * 512: kc * 1024 + hh * 512 + 512],
                                     start=(kc == 0), stop=(kc == NCH - 1))
                    i.then_inc(sT, 1)
                P.tsec(t_kv)
                tkv = P.inc("sT")
                kvmm_done[(g, tl, hh, "proj")] = tkv

                def s_kvev(s, tp=tp, tkv=tkv, g=g, tl=tl, hh=hh):
                    s.wait_ge(sT, tkv)
                    idx = g * 8 + tl * 2 + hh
                    if idx >= 2:
                        s.wait_ge(sT, kvmm_done[idx - 2])
                    s.activation(kt_bf[:, tp * 512:(tp + 1) * 512], psK[:, :], AF.Identity)
                    s.activation(vt_bf[:, tp * 512:(tp + 1) * 512], psV[:, :], AF.Identity).then_inc(sS, 1)
                P.ssec(s_kvev)
                idx = g * 8 + tl * 2 + hh
                kvev_done[idx] = P.inc("sS")

                def t_kvmm(t, g=g, tl=tl, hh=hh, tp=tp, idx=idx):
                    t.wait_ge(sS, kvev_done[idx])
                    if idx >= 1:
                        t.wait_ge(sV, kvacc_done[idx - 1])
                    for h8 in range(8):
                        h = hh * 8 + h8
                        i = t.matmul(psKV[:, h * 64:(h + 1) * 64],
                                     kt_bf[:, tp * 512 + h8 * 64: tp * 512 + (h8 + 1) * 64],
                                     vt_bf[:, tp * 512 + h8 * 64: tp * 512 + (h8 + 1) * 64],
                                     start=True, stop=True)
                    i.then_inc(sT, 1)
                P.tsec(t_kvmm)
                kvmm_done[idx] = P.inc("sT")

                def v_kvacc(v, hh=hh, idx=idx):
                    v.wait_ge(sT, kvmm_done[idx])
                    v.tensor_tensor(kv_acc[:, hh * 512:(hh + 1) * 512],
                                    kv_acc[:, hh * 512:(hh + 1) * 512],
                                    psKV[:, hh * 512:(hh + 1) * 512], ALU.add).then_inc(sV, 1)
                P.vsec(v_kvacc)
                kvacc_done[idx] = P.inc("sV")

        def g_qstore(g_, g=g, pb=pb):
            g_.wait_ge(sS, qev[(g, NCH - 1)])
            g_.dma_start(qt_out[:, :, g * G:(g + 1) * G].rearrange("c p t -> p c t"),
                         qt[pb][:, :].rearrange("p (c t) -> p c t", c=NCH)
                         ).then_inc(qS[g % 2], 16)
        P.gsec(g_qstore)
        qstore_done[g] = (qS[g % 2], P.inc("qS%d" % (g % 2), 16))

    def v_fin(v):
        v.wait_ge(sV, kvacc_done[NG * 8 - 1])
        v.tensor_copy(kv_out_s[:, :], kv_acc[:, :]).then_inc(sV, 1)
    P.vsec(v_fin)
    vfin = P.inc("sV")

    def g_fin(g_):
        g_.wait_ge(sV, vfin)
        g_.dma_start(kvp[:, :], kv_out_s[:, :]).then_inc(qW, 16)
        g_.wait_ge(qW, P.cnt["qW"] + 16)
        g_.wait_ge(qS[0], P.cnt["qS0"])
        g_.wait_ge(qS[1], P.cnt["qS1"])
    P.gsec(g_fin)
    P.inc("qW", 16)


    P.pop_scope()
    P.emit_barrier()


def build_l1attn_p1():
    key = "l1attn1"
    if key in _PROG_CACHE:
        return _PROG_CACHE[key]
    P = Prog(key)
    nc = P.nc
    ext = {}
    ext["xin"] = nc.dram_tensor("xin", [NCH, 128, TPC], F32, kind="ExternalInput")
    ext["wqT"] = nc.dram_tensor("wqT", [DD, DD], F32, kind="ExternalInput")
    ext["wkT"] = nc.dram_tensor("wkT", [DD, DD], F32, kind="ExternalInput")
    ext["wvT"] = nc.dram_tensor("wvT", [DD, DD], F32, kind="ExternalInput")
    ext["qt_out"] = nc.dram_tensor("qt_out", [NCH, 128, TPC], BF16, kind="ExternalOutput")
    ext["kvp"] = nc.dram_tensor("kvp", [64, 1024], F32, kind="ExternalOutput")
    P.std_sems()
    P.open()
    emit_l1attn_p1(P, ext)
    P.close()
    _PROG_CACHE[key] = P
    return P


def emit_l1attn_p2(P, ext, fused_kv=False):
    nc = P.nc
    xin = ext["xin"]
    qt_in = ext["qt_in"]
    kvp4 = ext["kvp4"]
    woT = ext["woT"]
    xout = ext["xout"]

    P.push_scope()
    swo = P.sb("swo", [128, NCH * 1024], BF16)
    kv4 = P.sb("kv4", [64, 4 * 1024], F32)
    kv_bf = P.sb("kv_bf", [64, 1024], BF16)
    kv2 = P.sb("kv2", [128, 512], BF16)
    xt = [P.sb(f"xt{i}", [128, TPC], F32) for i in range(2)]
    qt = [P.sb(f"qt{i}", [128, TPC], BF16) for i in range(2)]
    at_bf = P.sb("at_bf", [128, TPC], BF16)
    tO = [P.sb(f"tO{i}", [128, G], F32) for i in range(2)]

    psA = [P.ps(f"psA{i}", [128, G]) for i in range(2)]
    psO = [P.ps(f"psO{i}", [128, G]) for i in range(2)]

    qL = [P.handles["qL0"], P.handles["qL1"]]
    qS = [P.handles["qS0"], P.handles["qS1"]]
    qW = P.handles["qW"]
    sT = P.handles["sT"]; sV = P.handles["sV"]; sS = P.handles["sS"]

    def g_pro(g_):
        for a in range(NCH):
            g_.dma_start(swo[:, a * 1024:(a + 1) * 1024], woT[a * 128:(a + 1) * 128, :]).then_inc(qW, 16)
        if fused_kv:
            g_.dma_start(kv4[:, 0:1024], kvp4[:, :]).then_inc(qW, 16)
        else:
            g_.dma_start(kv4[:, :].rearrange("p (a n) -> p a n", a=4),
                         kvp4[:, :, :].rearrange("a p n -> p a n")).then_inc(qW, 16)
    P.gsec(g_pro)
    wready = P.inc("qW", 9 * 16)

    def v_kv(v):
        v.wait_ge(qW, wready)
        if not fused_kv:
            v.tensor_tensor(kv4[:, 0:1024], kv4[:, 0:1024], kv4[:, 1024:2048], ALU.add)
            v.tensor_tensor(kv4[:, 2048:3072], kv4[:, 2048:3072], kv4[:, 3072:4096], ALU.add)
            v.drain()
            v.tensor_tensor(kv4[:, 0:1024], kv4[:, 0:1024], kv4[:, 2048:3072], ALU.add)
            v.drain()
        v.tensor_copy(kv_bf[:, :], kv4[:, 0:1024]).then_inc(sV, 1)
    P.vsec(v_kv)
    kvready0 = P.inc("sV")

    def g_kv2(g_):
        g_.wait_ge(sV, kvready0)
        for par in range(2):
            g_.dma_start(
                kv2[par * 64:(par + 1) * 64, :].rearrange("d (hp e) -> d hp e", e=64),
                kv_bf[:, :].rearrange("d (hp par e) -> d par hp e", par=2, e=64)[:, par, :, :]
            ).then_inc(qW, 16)
    P.gsec(g_kv2)
    kvready = P.inc("qW", 32)

    load_done = {}
    store_done = {}
    atev = {}
    atmm = {}
    omm = {}
    oev = {}
    xadd = {}

    for g in range(NG):
        pb = g % 2

        def y_load(y, g=g, pb=pb):
            if g >= 2:
                y.wait_ge(*store_done[g - 2])
            y.dma_start(xt[pb][:, :].rearrange("p (c t) -> p c t", c=NCH),
                        xin[:, :, g * G:(g + 1) * G].rearrange("c p t -> p c t")
                        ).then_inc(qL[g % 2], 16)
            y.dma_start(qt[pb][:, :].rearrange("p (c t) -> p c t", c=NCH),
                        qt_in[:, :, g * G:(g + 1) * G].rearrange("c p t -> p c t")
                        ).then_inc(qL[g % 2], 16)
        P.ysec(y_load)
        load_done[g] = (qL[g % 2], P.inc("qL%d" % (g % 2), 32))

        for hp in range(8):
            ap = hp % 2

            def t_at(t, g=g, hp=hp, ap=ap, pb=pb):
                if g == 0 and hp == 0:
                    t.wait_ge(qW, kvready)
                t.wait_ge(*load_done[g])
                if hp >= 2:
                    t.wait_ge(sS, atev[(g, hp - 2)])
                elif g >= 1:
                    t.wait_ge(sS, atev[(g - 1, 6 + hp)])
                if hp == 0 and g >= 1:
                    t.wait_ge(sT, omm[(g - 1, NCH - 1)])
                for sub in range(2):
                    h = hp * 2 + sub
                    oc = h // 2
                    pbase = 64 * (h % 2)
                    i = t.matmul(psA[ap][pbase:pbase + 64, :],
                                 kv2[pbase:pbase + 64, oc * 64:(oc + 1) * 64],
                                 qt[pb][pbase:pbase + 64, _chsl(oc)],
                                 start=True, stop=True)
                i.then_inc(sT, 1)
            P.tsec(t_at)
            atmm[(g, hp)] = P.inc("sT")

            def s_atev(s, g=g, hp=hp, ap=ap, pb=pb):
                s.wait_ge(sT, atmm[(g, hp)])
                s.activation(at_bf[:, hp * G:(hp + 1) * G], psA[ap][:, :],
                             AF.Identity, scale=0.125).then_inc(sS, 1)
            P.ssec(s_atev)
            atev[(g, hp)] = P.inc("sS")

        for oc in range(NCH):
            op = oc % 2

            def t_o(t, g=g, oc=oc, op=op):
                t.wait_ge(sS, atev[(g, 7)])
                if oc >= 2:
                    t.wait_ge(sS, oev[(g, oc - 2)])
                elif g >= 1:
                    t.wait_ge(sS, oev[(g - 1, NCH - 2 + oc)])
                for kc in range(NCH):
                    i = t.matmul(psO[op][:, :],
                                 swo[:, kc * 1024 + oc * 128: kc * 1024 + oc * 128 + 128],
                                 at_bf[:, _chsl(kc)],
                                 start=(kc == 0), stop=(kc == NCH - 1))
                i.then_inc(sT, 1)
            P.tsec(t_o)
            omm[(g, oc)] = P.inc("sT")

            def s_oev(s, g=g, oc=oc, op=op):
                s.wait_ge(sT, omm[(g, oc)])
                if oc >= 2:
                    key2 = (g, oc - 2)
                elif g >= 1:
                    key2 = (g - 1, NCH - 2 + oc)
                else:
                    key2 = None
                if key2 is not None:
                    s.wait_ge(sV, xadd[key2])
                s.activation(tO[op][:, :], psO[op][:, :], AF.Identity).then_inc(sS, 1)
            P.ssec(s_oev)
            oev[(g, oc)] = P.inc("sS")

            def v_add(v, g=g, oc=oc, op=op, pb=pb):
                v.wait_ge(sS, oev[(g, oc)])
                v.tensor_tensor(xt[pb][:, _chsl(oc)], xt[pb][:, _chsl(oc)],
                                tO[op][:, :], ALU.add).then_inc(sV, 1)
            P.vsec(v_add)
            xadd[(g, oc)] = P.inc("sV")

        def g_store(g_, g=g, pb=pb):
            g_.wait_ge(sV, xadd[(g, NCH - 1)])
            g_.dma_start(xout[:, :, g * G:(g + 1) * G].rearrange("c p t -> p c t"),
                         xt[pb][:, :].rearrange("p (c t) -> p c t", c=NCH)
                         ).then_inc(qS[g % 2], 16)
        P.gsec(g_store)
        store_done[g] = (qS[g % 2], P.inc("qS%d" % (g % 2), 16))

    def g_fin(g_):
        g_.wait_ge(qS[0], P.cnt["qS0"])
        g_.wait_ge(qS[1], P.cnt["qS1"])
    P.gsec(g_fin)


    P.pop_scope()
    P.emit_barrier()


def build_l1attn_p2():
    key = "l1attn2"
    if key in _PROG_CACHE:
        return _PROG_CACHE[key]
    P = Prog(key)
    nc = P.nc
    ext = {}
    ext["xin"] = nc.dram_tensor("xin", [NCH, 128, TPC], F32, kind="ExternalInput")
    ext["qt_in"] = nc.dram_tensor("qt_in", [NCH, 128, TPC], BF16, kind="ExternalInput")
    ext["kvp4"] = nc.dram_tensor("kvp4", [4, 64, 1024], F32, kind="ExternalInput")
    ext["woT"] = nc.dram_tensor("woT", [DD, DD], F32, kind="ExternalInput")
    ext["xout"] = nc.dram_tensor("xout", [NCH, 128, TPC], F32, kind="ExternalOutput")
    P.std_sems()
    P.open()
    emit_l1attn_p2(P, ext)
    P.close()
    _PROG_CACHE[key] = P
    return P


def run_l1attn(xs, W_q, W_k, W_v, W_o):
    P1 = build_l1attn_p1()
    wqT = np.ascontiguousarray(W_q.T)
    wkT = np.ascontiguousarray(W_k.T)
    wvT = np.ascontiguousarray(W_v.T)
    woT = np.ascontiguousarray(W_o.T)
    in_maps = [{"xin": xs[c], "wqT": wqT, "wkT": wkT, "wvT": wvT} for c in range(NCORE)]
    res = run_bass_kernel_spmd(P1.nc, in_maps, core_ids=list(range(NCORE)))
    qts = [res.results[c]["qt_out"] for c in range(NCORE)]
    kvps = [res.results[c]["kvp"] for c in range(NCORE)]
    kv_b0 = np.stack(kvps[0:4]).astype(np.float32)
    kv_b1 = np.stack(kvps[4:8]).astype(np.float32)
    P2 = build_l1attn_p2()
    in_maps = [{"xin": xs[c], "qt_in": qts[c],
                "kvp4": (kv_b0 if c < 4 else kv_b1), "woT": woT}
               for c in range(NCORE)]
    res = run_bass_kernel_spmd(P2.nc, in_maps, core_ids=list(range(NCORE)))
    return [res.results[c]["xout"] for c in range(NCORE)]


def emit_l0attn(P, ext, l):
    nc = P.nc
    SW = 513
    xin = ext["xin"]
    xhalo = ext["xhalo"]
    wqkv = ext["wqkv"]
    wo = ext["wo"]
    qscol = ext["qscol"]
    kscol = ext["kscol"]
    tqc = ext["tqc"]
    tqs = ext["tqs"]
    tkc = ext["tkc"]
    tks = ext["tks"]
    smask = ext["smask"]
    cm0 = ext["cm0"]
    cm4 = ext["cm4"]
    cid128b = ext["cid128b"]
    chs = ext["chs"]
    chsT = ext["chsT"]
    crot = ext["crot"]
    xout = ext["xout"]

    P.push_scope()
    s_wqkv = P.sb("s_wqkv", [128, NCH * 1536], BF16)
    s_wo = P.sb("s_wo", [128, 4 * 1024], BF16)
    s_qs = P.sb("s_qs", [128, 1], F32)
    s_ks = P.sb("s_ks", [128, 1], F32)
    s_sm = P.sb("s_sm", [128, 16 * 128], BF16)
    s_m0 = P.sb("s_m0", [128, 128], BF16)
    s_m4 = P.sb("s_m4", [128, 128], BF16)
    s_idb = P.sb("s_idb", [128, 128], BF16)
    s_hs = P.sb("s_hs", [128, 32], F32)
    s_hsT = P.sb("s_hsT", [8, 512], F32)
    s_rot = P.sb("s_rot", [128, 128], BF16)
    ones_bf = P.sb("ones_bf", [128, 128], BF16)
    epscol = P.sb("epscol", [128, 1], F32)
    tb = {nm: [P.sb(f"tb_{nm}{i}", [128, 512], BF16) for i in range(2)]
          for nm in ("qc", "qs", "kc", "ks")}
    xt = [P.sb(f"xt{i}", [128, NCH * SW], F32) for i in range(2)]
    xh = xt[1]
    u_bf = [P.sb(f"ubf{i}", [128, NCH * SW], BF16) for i in range(2)]
    uh_bf = u_bf[1]
    uf = [P.sb(f"uf{i}", [128, TPC], F32) for i in range(2)] if l == 0 else None
    x_bf = P.sb("xbf", [128, NCH * SW], BF16)
    x2_bf = P.sb("x2bf", [128, NCH * SW], BF16)
    qf = P.sb("qf", [128, G], F32)
    sqv = P.sb("sqv", [128, G], F32)
    nr = P.sb("nr", [8, G], F32)
    rr = P.sb("rr", [8, G], F32)
    qsc = P.sb("qsc", [128, G], BF16)
    t1b = P.sb("t1b", [128, G], BF16)
    qT_f = [P.sb(f"qTf{i}", [128, 2048], BF16) for i in range(2)]
    kT_c = P.sb("kT_c", [128, 4096], BF16)
    v_c = P.sb("v_c", [128, 4096], BF16)
    attn_bf = P.sb("attn_bf", [128, 640], BF16)
    dgr = P.sb("dgr", [128, 128], BF16)
    attnT_bf = P.sb("attnT_bf", [128, 640], BF16)
    negmax = P.sb("negmax", [128, 1], F32)
    rsum = P.sb("rsum", [128, 1], F32)
    rrec = P.sb("rrec", [128, 1], F32)
    oT_g = P.sb("oT_g", [128, 2048], BF16)
    tO = [P.sb(f"tO{i}", [128, G], F32) for i in range(2)]
    Bmu = P.sb("Bmu", [128, G], F32)
    Bvar = P.sb("Bvar", [128, G], F32)
    Brstd = P.sb("Brstd", [128, G], F32)
    tA = P.sb("tA", [128, G], F32)
    tB = P.sb("tB", [128, G], F32)

    psQN = P.ps("psQN", [128, G])
    psB = psQN
    psSim = [P.ps(f"psSim{i}", [128, 640]) for i in range(2)]
    psAV = P.ps("psAV", [128, 1024])

    qL = [P.handles["qL0"], P.handles["qL1"]]
    qS = [P.handles["qS0"], P.handles["qS1"]]
    qW = P.handles["qW"]
    sG = P.handles["sG"]
    sT = P.handles["sT"]; sV = P.handles["sV"]; sS = P.handles["sS"]

    def g_pro(g_):
        for a in range(NCH):
            g_.dma_start(s_wqkv[:, a * 1536:(a + 1) * 1536], wqkv[a * 128:(a + 1) * 128, :]).then_inc(qW, 16)
        for a in range(4):
            g_.dma_start(s_wo[:, a * 1024:(a + 1) * 1024], wo[a * 128:(a + 1) * 128, :]).then_inc(qW, 16)
        g_.dma_start(s_sm[:, :], smask[:, :]).then_inc(qW, 16)
        for nm, dst in [(qscol, s_qs), (kscol, s_ks), (cm0, s_m0), (cm4, s_m4),
                        (cid128b, s_idb), (chs, s_hs), (chsT, s_hsT), (crot, s_rot)]:
            g_.dma_start(dst[:, :], nm[:, :]).then_inc(qW, 16)
        g_.dma_start(xh[:, :].rearrange("p (c t) -> p c t", c=NCH),
                     xhalo[:, :, :].rearrange("c p t -> p c t")).then_inc(qW, 16)
        g_.memset(ones_bf[:, :], 1.0)
        g_.memset(epscol[:, :], 1e-5).then_inc(sG, 1)
    P.gsec(g_pro)
    wready = P.inc("qW", 22 * 16)
    gready = P.inc("sG")

    def v_qs8(v):
        v.wait_ge(qW, wready)
        v.tensor_scalar(s_qs[:, :], s_qs[:, :], 8.0, None, ALU.mult).then_inc(sV, 1)
        v.drain()
    P.vsec(v_qs8)
    P.inc("sV")

    load_done = {}
    store_done = {}
    mm = {"qev": {}, "rope": {}, "vev": {}, "vmm": {}, "qkmm": {}, "nb": {},
          "sim": {}, "exp": {}, "scl": {}, "trp": {}, "atev": {}, "avmm": {},
          "oev": {}, "wo": {}, "woev": {}, "xadd": {}, "uready": {}}

    def shoff(kc, base):
        return base + (1 if kc < 4 else 0)

    for gi in range(-1, NG):
        pb = gi % 2
        halo = gi < 0
        ubuf = uh_bf if halo else u_bf[pb]
        srcx = xh if halo else xt[pb]

        def y_load(y, gi=gi, pb=pb, halo=halo):
            if not halo:
                if gi >= 2:
                    y.wait_ge(*store_done[gi - 2])
                elif gi == 1:
                    y.wait_ge(sV, mm["uready"][-1])
                y.dma_start(xt[pb][:, 0:TPC].rearrange("p (c t) -> p c t", c=NCH),
                            xin[:, :, gi * G:(gi + 1) * G].rearrange("c p t -> p c t")
                            ).then_inc(qL[pb], 16)
            if (gi - 2, "k", 3) in mm["rope"]:
                y.wait_ge(sT, mm["rope"][(gi - 2, "k", 3)])
            tcol = 512 + gi * 512
            for nm, t_ in [("qc", tqc), ("qs", tqs), ("kc", tkc), ("ks", tks)]:
                y.dma_start(tb[nm][pb][:, :], t_[:, tcol:tcol + 512]).then_inc(qL[pb], 16)
        P.ysec(y_load)
        load_done[gi] = (qL[pb], P.inc("qL%d" % pb, 16 * (4 if halo else 5)))

        def v_w(v, gi=gi):
            v.wait_ge(*load_done[gi])
            v.wait_ge(qW, wready)
            v.wait_ge(sG, gready)
            if (gi - 2, 3) in mm["vmm"]:
                v.wait_ge(sT, mm["vmm"][(gi - 2, 3)])
        P.vsec(v_w)

        def t_bguard(t, gi=gi):
            if (gi - 1, NCH - 1) in mm["woev"]:
                t.wait_ge(sS, mm["woev"][(gi - 1, NCH - 1)])
            elif (gi - 1, 3) in mm["vev"]:
                t.wait_ge(sS, mm["vev"][(gi - 1, 3)])
        P.tsec(t_bguard)

        if l == 0:
            bufs = dict(xt=srcx, x_bf=x_bf, x2_bf=x2_bf, u_bf=ubuf,
                        Bmu=Bmu, Bvar=Bvar, Brstd=Brstd, tA=tA, tB=tB,
                        ones_bf=ones_bf, psB=psB, epscol=epscol, seq_stats=True)
            if halo:
                emit_ln_group(P, bufs, 512, 0, soff=0, sstride=SW, doff=0, dstride=SW)
                ua = emit_ln_group(P, bufs, 1, 0, soff=512, sstride=SW, doff=512, dstride=SW)
            else:
                bufs["uf"] = uf[pb]
                ua = emit_ln_group(P, bufs, 512, 0, soff=0, sstride=512,
                                   doff=1, dstride=SW, want_uf=True)
            mm["uready"][gi] = ua
        else:
            def v_cast(v, gi=gi, halo=halo, srcx=srcx, ubuf=ubuf):
                W2 = SW if halo else 512
                do = 0 if halo else 1
                ss = SW if halo else 512
                for ch in range(NCH):
                    i = v.tensor_copy(ubuf[:, ch * SW + do: ch * SW + do + W2],
                                      srcx[:, ch * ss: ch * ss + W2])
                i.then_inc(sV, 1)
            P.vsec(v_cast)
            mm["uready"][gi] = P.inc("sV")

        if not halo:
            def v_c0(v, gi=gi, pb=pb, ubuf=ubuf):
                prev = uh_bf if gi == 0 else u_bf[1 - pb]
                v.drain()
                for ch in range(NCH):
                    i = v.tensor_copy(ubuf[:, ch * SW: ch * SW + 1],
                                      prev[:, ch * SW + 512: ch * SW + 513])
                i.then_inc(sV, 1)
            P.vsec(v_c0)
            mm["uready"][gi] = P.inc("sV")
        uready = mm["uready"][gi]

        seq = [("q", s_qs, "qc", "qs", oc) for oc in range(4)] + \
              [("k", s_ks, "kc", "ks", oc) for oc in range(4)]
        previx = (gi - 1, "k", 3)
        for six, (which, scol, tcn, tsn, oc) in enumerate(seq):
            ix = (gi, which, oc)
            pix = (gi, seq[six - 1][0], seq[six - 1][4]) if six > 0 else \
                  (previx if previx in mm["qev"] else None)
            wcol = oc * 128 + (0 if which == "q" else 512)

            def t_qk(t, ix=ix, pix=pix, wcol=wcol, ubuf=ubuf, uready=uready):
                if ix[1] == "q" and ix[2] == 0:
                    t.wait_ge(sV, uready)
                if pix is not None:
                    t.wait_ge(sS, mm["qev"][pix])
                for kc in range(NCH):
                    i = t.matmul(psQN[:, :],
                                 s_wqkv[:, kc * 1536 + wcol: kc * 1536 + wcol + 128],
                                 ubuf[:, ix[0] * 0 + kc * SW + shoff(kc, 0): kc * SW + shoff(kc, 0) + 512],
                                 start=(kc == 0), stop=(kc == NCH - 1))
                i.then_inc(sT, 1)
            P.tsec(t_qk)
            mm["qkmm"][ix] = tqk = P.inc("sT")

            def s_qev(s, tqk=tqk, pix=pix):
                s.wait_ge(sT, tqk)
                if pix is not None and pix in mm["rope"]:
                    s.wait_ge(sT, mm["rope"][pix])
                s.activation(qf[:, :], psQN[:, :], AF.Identity).then_inc(sS, 1)
            P.ssec(s_qev)
            mm["qev"][ix] = qev = P.inc("sS")

            def v_sq(v, qev=qev, pix=pix):
                v.wait_ge(sS, qev)
                if pix is not None and pix in mm["nb"]:
                    v.wait_ge(sT, mm["nb"][pix])
                v.tensor_tensor(sqv[:, :], qf[:, :], qf[:, :], ALU.mult).then_inc(sV, 1)
            P.vsec(v_sq)
            vsq = P.inc("sV")

            def t_nr(t, oc=oc, vsq=vsq):
                t.wait_ge(sV, vsq)
                t.matmul(psQN[0:8, :], s_hs[:, oc * 8:(oc + 1) * 8], sqv[:, :],
                         start=True, stop=True).then_inc(sT, 1)
            P.tsec(t_nr)
            tnr = P.inc("sT")

            def v_nr(v, tnr=tnr):
                v.wait_ge(sT, tnr)
                v.tensor_scalar(nr[:, :], psQN[0:8, :], 1e-24, None, ALU.max).then_inc(sV, 1)
            P.vsec(v_nr)
            vnr = P.inc("sV")

            def s_nr(s, vnr=vnr, pix=pix):
                s.wait_ge(sV, vnr)
                if pix is not None and pix in mm["nb"]:
                    s.wait_ge(sT, mm["nb"][pix])
                s.activation(rr[:, :], nr[:, :], AF.Sqrt).then_inc(sS, 1)
            P.ssec(s_nr)
            snr = P.inc("sS")

            def v_rr(v, snr=snr):
                v.wait_ge(sS, snr)
                v.reciprocal(rr[:, :], rr[:, :])
                v.drain().then_inc(sV, 1)
            P.vsec(v_rr)
            vrr = P.inc("sV")

            def t_nb(t, oc=oc, vrr=vrr):
                t.wait_ge(sV, vrr)
                t.matmul(psQN[:, :], s_hsT[:, oc * 128:(oc + 1) * 128], rr[:, :],
                         start=True, stop=True).then_inc(sT, 1)
            P.tsec(t_nb)
            mm["nb"][ix] = tnb = P.inc("sT")

            def v_stt(v, scol=scol, tnb=tnb):
                v.wait_ge(sT, tnb)
                v.scalar_tensor_tensor(qsc[:, :], qf[:, :], scol[:, 0:1], psQN[:, :],
                                       ALU.mult, ALU.mult).then_inc(sV, 1)
                v.drain()
            P.vsec(v_stt)
            vstt = P.inc("sV")

            def t_rot(t, vstt=vstt):
                t.wait_ge(sV, vstt)
                t.matmul(psQN[:, :], s_rot[:, :], qsc[:, :], start=True, stop=True).then_inc(sT, 1)
            P.tsec(t_rot)
            trot = P.inc("sT")

            def v_rope(v, gi=gi, pb=pb, which=which, oc=oc, tcn=tcn, tsn=tsn, trot=trot):
                v.wait_ge(sT, trot)
                if which == "q" and (gi - 2, 3, 7) in mm["sim"]:
                    v.wait_ge(sT, mm["sim"][(gi - 2, 3, 7)])
                if which == "k" and (gi - 1, 3, 7) in mm["sim"]:
                    v.wait_ge(sT, mm["sim"][(gi - 1, 3, 7)])
                v.tensor_tensor(t1b[:, :], qsc[:, :], tb[tcn][pb][:, :], ALU.mult)
                v.drain()
                v.tensor_tensor(qsc[:, :], psQN[:, :], tb[tsn][pb][:, :], ALU.mult)
                v.drain()
                if which == "q":
                    dst = qT_f[pb][:, oc * 512:(oc + 1) * 512]
                else:
                    dst = kT_c[:, oc * 1024 + pb * 512: oc * 1024 + pb * 512 + 512]
                v.tensor_tensor(dst, t1b[:, :], qsc[:, :], ALU.add).then_inc(sV, 1)
                v.drain()
            P.vsec(v_rope)
            rope_v = P.inc("sV")

            def t_rd(t, rope_v=rope_v):
                t.wait_ge(sV, rope_v)
                t.nop().then_inc(sT, 1)
            P.tsec(t_rd)
            mm["rope"][ix] = P.inc("sT")

        for tl in range(4):
            slot = pb * 4 + tl

            def t_v(t, gi=gi, tl=tl, ubuf=ubuf, uready=uready):
                if tl == 0:
                    t.wait_ge(sV, uready)
                    t.wait_ge(sS, mm["woev"].get(gi - 1, 0))
                if (gi, tl - 1) in mm["vev"]:
                    t.wait_ge(sS, mm["vev"][(gi, tl - 1)])
                elif (gi - 1, 3) in mm["vev"]:
                    t.wait_ge(sS, mm["vev"][(gi - 1, 3)])
                for kc in range(NCH):
                    i = t.matmul(psB[:, 0:512],
                                 ubuf[:, kc * SW + shoff(kc, tl * 128): kc * SW + shoff(kc, tl * 128) + 128],
                                 s_wqkv[:, kc * 1536 + 1024: kc * 1536 + 1536],
                                 start=(kc == 0), stop=(kc == NCH - 1))
                i.then_inc(sT, 1)
            P.tsec(t_v)
            mm["vmm"][(gi, tl)] = P.inc("sT")

            def s_vev(s, gi=gi, tl=tl, slot=slot):
                s.wait_ge(sT, mm["vmm"][(gi, tl)])
                if (gi - 1, 3, 7) in mm["avmm"]:
                    s.wait_ge(sT, mm["avmm"][(gi - 1, 3, 7)])
                s.activation(v_c[:, slot * 512:(slot + 1) * 512], psB[:, 0:512],
                             AF.Identity).then_inc(sS, 1)
            P.ssec(s_vev)
            mm["vev"][(gi, tl)] = P.inc("sS")

        if halo:
            continue

        for tl in range(4):
            mglob = gi * 4 + tl
            for h in range(8):
                oc = h // 2
                p64 = 64 * (h % 2)
                sp = h % 2
                if h > 0:
                    prevk = (gi, tl, h - 1)
                elif tl > 0:
                    prevk = (gi, tl - 1, 7)
                elif (gi - 1, 3, 7) in mm["exp"]:
                    prevk = (gi - 1, 3, 7)
                else:
                    prevk = None
                prevk2 = None
                if prevk is not None:
                    g2, t2, h2 = prevk
                    if h2 > 0:
                        prevk2 = (g2, t2, h2 - 1)
                    elif t2 > 0:
                        prevk2 = (g2, t2 - 1, 7)
                    elif (g2 - 1, 3, 7) in mm["exp"]:
                        prevk2 = (g2 - 1, 3, 7)

                def t_sim(t, gi=gi, tl=tl, h=h, oc=oc, p64=p64, pb=pb, sp=sp,
                          mglob=mglob, prevk2=prevk2):
                    if h == 0 and tl == 0:
                        t.wait_ge(sS, mm["vev"][(gi, 3)])
                    if prevk2 is not None:
                        t.wait_ge(sV, mm["atev"][prevk2])
                    i = None
                    for b in range(5):
                        kglob = mglob - 4 + b
                        mask = None
                        if mglob < 4 and b < 4:
                            mask = s_sm[:, (tl * 4 + b) * 128:(tl * 4 + b + 1) * 128]
                        elif b == 0:
                            mask = s_m0[:, :]
                        elif b == 4:
                            mask = s_m4[:, :]
                        slot = kglob % 8
                        rhs = kT_c[p64:p64 + 64,
                                   oc * 1024 + (slot // 4) * 512 + (slot % 4) * 128:
                                   oc * 1024 + (slot // 4) * 512 + (slot % 4) * 128 + 128]
                        lq = qT_f[pb][p64:p64 + 64, oc * 512 + tl * 128: oc * 512 + tl * 128 + 128]
                        if mask is not None:
                            t.matmul(psSim[sp][:, b * 128:(b + 1) * 128], s_idb[:, :], mask,
                                     start=True, stop=False)
                            i = t.matmul(psSim[sp][:, b * 128:(b + 1) * 128], lq, rhs,
                                         start=False, stop=True)
                        else:
                            i = t.matmul(psSim[sp][:, b * 128:(b + 1) * 128], lq, rhs,
                                         start=True, stop=True)
                    i.then_inc(sT, 1)
                P.tsec(t_sim)
                mm["sim"][(gi, tl, h)] = P.inc("sT")

                def v_max(v, gi=gi, tl=tl, h=h, sp=sp, prevk=prevk):
                    v.wait_ge(sT, mm["sim"][(gi, tl, h)])
                    v.tensor_reduce(negmax[:, :], psSim[sp][:, :], AX.X, ALU.max,
                                    negate=True).then_inc(sV, 1)
                P.vsec(v_max)
                vmax = P.inc("sV")

                def s_exp(s, vmax=vmax, gi=gi, tl=tl, h=h, sp=sp, prevk=prevk):
                    s.wait_ge(sV, vmax)
                    if prevk is not None:
                        s.wait_ge(sT, mm["trp"][prevk])
                        s.wait_ge(sV, mm["scl"][prevk])
                    s.activation(attn_bf[:, :], psSim[sp][:, :], AF.Exp, bias=negmax[:, 0:1],
                                 accum_out=rsum[:, 0:1]).then_inc(sS, 1)
                P.ssec(s_exp)
                mm["exp"][(gi, tl, h)] = P.inc("sS")

                def v_scl(v, gi=gi, tl=tl, h=h, prevk=prevk):
                    v.wait_ge(sS, mm["exp"][(gi, tl, h)])
                    if prevk is not None:
                        v.wait_ge(sT, mm["trp"][prevk])
                    v.reciprocal(rrec[:, :], rsum[:, :])
                    v.drain()
                    v.tensor_scalar(dgr[:, :], s_idb[:, :], rrec[:, 0:1], None,
                                    ALU.mult).then_inc(sV, 1)
                    v.drain()
                P.vsec(v_scl)
                mm["scl"][(gi, tl, h)] = P.inc("sV")

                def t_trp(t, gi=gi, tl=tl, h=h, sp=sp):
                    t.wait_ge(sV, mm["scl"][(gi, tl, h)])
                    for b in range(5):
                        i = t.matmul(psSim[sp][:, b * 128:(b + 1) * 128],
                                     attn_bf[:, b * 128:(b + 1) * 128], dgr[:, :],
                                     start=True, stop=True)
                    i.then_inc(sT, 1)
                P.tsec(t_trp)
                mm["trp"][(gi, tl, h)] = P.inc("sT")

                def v_atev(v, gi=gi, tl=tl, h=h, sp=sp, prevk=prevk):
                    v.wait_ge(sT, mm["trp"][(gi, tl, h)])
                    if prevk is not None:
                        v.wait_ge(sT, mm["avmm"][prevk])
                    v.tensor_copy(attnT_bf[:, :], psSim[sp][:, :]).then_inc(sV, 1)
                P.vsec(v_atev)
                mm["atev"][(gi, tl, h)] = P.inc("sV")

                def t_av(t, gi=gi, tl=tl, h=h, p64=p64, mglob=mglob):
                    t.wait_ge(sV, mm["atev"][(gi, tl, h)])
                    if h == 0:
                        if (gi, tl - 1) in mm["oev"]:
                            t.wait_ge(sS, mm["oev"][(gi, tl - 1)])
                        elif (gi - 1, 3) in mm["oev"]:
                            t.wait_ge(sS, mm["oev"][(gi - 1, 3)])
                    cbase = 512 * (h % 2) + (h // 2) * 128
                    for b in range(5):
                        slot = (mglob - 4 + b) % 8
                        i = t.matmul(psAV[p64:p64 + 64, cbase: cbase + 128],
                                     v_c[:, slot * 512 + h * 64: slot * 512 + h * 64 + 64],
                                     attnT_bf[:, b * 128:(b + 1) * 128],
                                     start=(h <= 1 and b == 0), stop=(h >= 6 and b == 4))
                    i.then_inc(sT, 1)
                P.tsec(t_av)
                mm["avmm"][(gi, tl, h)] = P.inc("sT")

            def s_oev(s, gi=gi, tl=tl):
                s.wait_ge(sT, mm["avmm"][(gi, tl, 7)])
                if tl == 0 and (gi - 1, NCH - 1) in mm["wo"]:
                    s.wait_ge(sT, mm["wo"][(gi - 1, NCH - 1)])
                s.activation(
                    oT_g[0:64, :].rearrange("p (pr i) -> p pr i", pr=4)[:, :, tl * 128:(tl + 1) * 128],
                    psAV[0:64, 0:512].rearrange("p (pr i) -> p pr i", i=128),
                    AF.Identity)
                s.activation(
                    oT_g[64:128, :].rearrange("p (pr i) -> p pr i", pr=4)[:, :, tl * 128:(tl + 1) * 128],
                    psAV[64:128, 512:1024].rearrange("p (pr i) -> p pr i", i=128),
                    AF.Identity).then_inc(sS, 1)
            P.ssec(s_oev)
            mm["oev"][(gi, tl)] = P.inc("sS")

        for oc in range(NCH):
            op = oc % 2

            def t_wo(t, gi=gi, oc=oc):
                t.wait_ge(sS, mm["oev"][(gi, 3)])
                if oc == 0:
                    t.wait_ge(sS, mm["vev"][(gi, 3)])
                else:
                    t.wait_ge(sS, mm["woev"][(gi, oc - 1)])
                for pr in range(4):
                    i = t.matmul(psB[:, 0:512],
                                 s_wo[:, pr * 1024 + oc * 128: pr * 1024 + oc * 128 + 128],
                                 oT_g[:, pr * 512:(pr + 1) * 512],
                                 start=(pr == 0), stop=(pr == 3))
                i.then_inc(sT, 1)
            P.tsec(t_wo)
            mm["wo"][(gi, oc)] = P.inc("sT")

            def s_woev(s, gi=gi, oc=oc, op=op):
                s.wait_ge(sT, mm["wo"][(gi, oc)])
                if (gi, oc - 2) in mm["xadd"]:
                    s.wait_ge(sV, mm["xadd"][(gi, oc - 2)])
                elif (gi - 1, NCH - 2 + oc) in mm["xadd"]:
                    s.wait_ge(sV, mm["xadd"][(gi - 1, NCH - 2 + oc)])
                s.activation(tO[op][:, :], psB[:, 0:512], AF.Identity).then_inc(sS, 1)
            P.ssec(s_woev)
            mm["woev"][(gi, oc)] = P.inc("sS")
            if oc == NCH - 1:
                mm["woev"][gi] = mm["woev"][(gi, oc)]

            def v_add(v, gi=gi, oc=oc, op=op, pb=pb):
                v.wait_ge(sS, mm["woev"][(gi, oc)])
                if l == 0:
                    v.tensor_tensor(xt[pb][:, _chsl(oc)], uf[pb][:, _chsl(oc)],
                                    tO[op][:, :], ALU.add).then_inc(sV, 1)
                else:
                    v.tensor_tensor(xt[pb][:, _chsl(oc)], xt[pb][:, _chsl(oc)],
                                    tO[op][:, :], ALU.add).then_inc(sV, 1)
            P.vsec(v_add)
            mm["xadd"][(gi, oc)] = P.inc("sV")

        def g_store(g_, gi=gi, pb=pb):
            g_.wait_ge(sV, mm["xadd"][(gi, NCH - 1)])
            g_.dma_start(xout[:, :, gi * G:(gi + 1) * G].rearrange("c p t -> p c t"),
                         xt[pb][:, 0:TPC].rearrange("p (c t) -> p c t", c=NCH)
                         ).then_inc(qS[pb], 16)
        P.gsec(g_store)
        store_done[gi] = (qS[pb], P.inc("qS%d" % pb, 16))

    def g_fin(g_):
        g_.wait_ge(qS[0], P.cnt["qS0"])
        g_.wait_ge(qS[1], P.cnt["qS1"])
    P.gsec(g_fin)


    P.pop_scope()
    P.emit_barrier()


def build_l0attn(l):
    key = f"l0attn{l}"
    if key in _PROG_CACHE:
        return _PROG_CACHE[key]
    P = Prog(key)
    nc = P.nc
    ext = {}
    ext["xin"] = nc.dram_tensor("xin", [NCH, 128, TPC], F32, kind="ExternalInput")
    ext["xhalo"] = nc.dram_tensor("xhalo", [NCH, 128, SW], F32, kind="ExternalInput")
    ext["wqkv"] = nc.dram_tensor("wqkv", [DD, 1536], F32, kind="ExternalInput")
    ext["wo"] = nc.dram_tensor("wo", [512, DD], F32, kind="ExternalInput")
    ext["qscol"] = nc.dram_tensor("qscol", [128, 1], F32, kind="ExternalInput")
    ext["kscol"] = nc.dram_tensor("kscol", [128, 1], F32, kind="ExternalInput")
    ext["tqc"] = nc.dram_tensor("tqc", [128, 4608], BF16, kind="ExternalInput")
    ext["tqs"] = nc.dram_tensor("tqs", [128, 4608], BF16, kind="ExternalInput")
    ext["tkc"] = nc.dram_tensor("tkc", [128, 4608], BF16, kind="ExternalInput")
    ext["tks"] = nc.dram_tensor("tks", [128, 4608], BF16, kind="ExternalInput")
    ext["smask"] = nc.dram_tensor("smask", [128, 16 * 128], BF16, kind="ExternalInput")
    ext["cm0"] = nc.dram_tensor("cm0", [128, 128], BF16, kind="ExternalInput")
    ext["cm4"] = nc.dram_tensor("cm4", [128, 128], BF16, kind="ExternalInput")
    ext["cid128b"] = nc.dram_tensor("cid128b", [128, 128], BF16, kind="ExternalInput")
    ext["chs"] = nc.dram_tensor("chs", [128, 32], F32, kind="ExternalInput")
    ext["chsT"] = nc.dram_tensor("chsT", [8, 512], F32, kind="ExternalInput")
    ext["crot"] = nc.dram_tensor("crot", [128, 128], BF16, kind="ExternalInput")
    ext["xout"] = nc.dram_tensor("xout", [NCH, 128, TPC], F32, kind="ExternalOutput")
    P.std_sems()
    P.open()
    emit_l0attn(P, ext, l)
    P.close()
    _PROG_CACHE[key] = P
    return P


HID = 2730
HHALF = 1408

def _hchunks(half):
    if half == 0:
        return [(c * 128, 128) for c in range(11)]
    out = [(HHALF + c * 128, 128) for c in range(10)]
    out.append((HHALF + 1280, 42))
    return out

def emit_l0ffn(P, ext, final):
    nc = P.nc
    SW = 513
    xin = ext["xin"]
    xh1 = ext["xh1"]
    w1 = ext["w1"]
    w2 = ext["w2"]
    b1at = ext["b1at"]
    b1gt = ext["b1gt"]
    b2t = ext["b2t"]
    lgt = ext["lgt"]
    lbt = ext["lbt"]
    if final:
        xres = ext["xres"]
    xout = ext["xout"]

    P.push_scope()
    s_w1 = P.sb("s_w1", [128, NCH * 2816], BF16)
    s_w2 = P.sb("s_w2", [128, 11 * 1024], BF16)
    s_b1a = P.sb("s_b1a", [128, 22], F32)
    s_b1g = P.sb("s_b1g", [128, 22], F32)
    s_b2 = P.sb("s_b2", [128, 8], F32)
    s_lg = P.sb("s_lg", [128, 8], F32)
    s_lb = P.sb("s_lb", [128, 8], F32)
    ones_bf = P.sb("ones_bf", [128, 128], BF16)
    epscol = P.sb("epscol", [128, 1], F32)
    xth_s = P.sb("xth_s", [128, NCH * SW], F32)
    xth = [xth_s, xth_s]
    tOx = [P.sb(f"tOx{i}", [128, TPC], F32) for i in range(2)]
    xtr_s = P.sb("xtr_s", [128, TPC], F32) if final else None
    xtr = [xtr_s, xtr_s] if final else None
    x_bf = P.sb("xbf", [128, NCH * SW], BF16)
    x2_bf = P.sb("x2bf", [128, NCH * SW], BF16)
    h_bf = [P.sb(f"hbf{i}", [128, TPC], BF16) for i in range(2)]
    mid = P.sb("mid", [128, 11 * G], BF16)
    ascr = P.sb("ascr", [128, G], F32)
    gscr = P.sb("gscr", [128, G], F32)
    tO = [P.sb(f"tO{i}", [128, G], F32) for i in range(2)]
    Bmu = P.sb("Bmu", [128, G], F32)
    Bvar = P.sb("Bvar", [128, G], F32)
    Brstd = P.sb("Brstd", [128, G], F32)
    tA = P.sb("tA", [128, G], F32)
    tB = P.sb("tB", [128, G], F32)

    psB = P.ps("psB", [128, 2 * G])
    psA = P.ps("psA", [128, G])
    psG = P.ps("psG", [128, G])
    psO = [P.ps(f"psO{i}", [128, G]) for i in range(2)]

    qL = [P.handles["qL0"], P.handles["qL1"]]
    qS = [P.handles["qS0"], P.handles["qS1"]]
    qW = P.handles["qW"]
    sG = P.handles["sG"]
    sT = P.handles["sT"]; sV = P.handles["sV"]; sS = P.handles["sS"]

    def g_pro(g_):
        for nm, dst in [(b1at, s_b1a), (b1gt, s_b1g), (b2t, s_b2), (lgt, s_lg), (lbt, s_lb)]:
            g_.dma_start(dst[:, :], nm[:, :]).then_inc(qW, 16)
        g_.memset(ones_bf[:, :], 1.0)
        g_.memset(epscol[:, :], 1e-5).then_inc(sG, 1)
    P.gsec(g_pro)
    bready = P.inc("qW", 5 * 16)
    gready = P.inc("sG")

    load_done = {}
    store_done = {}
    uapply = {}
    wready = {}
    mm = {"amm": {}, "aev": {}, "gmm": {}, "gev": {}, "midv": {}, "w2": {},
          "oev": {}, "xadd": {}, "fin": {}}

    def shoff(kc, base):
        return base + (1 if kc < 4 else 0)

    for half in range(2):
        chunks = _hchunks(half)

        hw = 1408 if half == 0 else 1322

        def g_w(g_, half=half, hw=hw, chunks=chunks):
            if half == 1:
                g_.wait_ge(P.handles["sT"], mm["w2"][(0, NG - 1, NCH - 1)])
            for a in range(NCH):
                g_.dma_start(s_w1[:, a * 2816: a * 2816 + hw],
                             w1[a * 128:(a + 1) * 128, half * HHALF: half * HHALF + hw]
                             ).then_inc(qW, 16)
                g_.dma_start(s_w1[:, a * 2816 + 1408: a * 2816 + 1408 + hw],
                             w1[a * 128:(a + 1) * 128, HID + half * HHALF: HID + half * HHALF + hw]
                             ).then_inc(qW, 16)
            for c, (off, w) in enumerate(chunks):
                g_.dma_start(s_w2[0:w, c * 1024:(c + 1) * 1024],
                             w2[off: off + w, :]).then_inc(qW, 16)
        P.gsec(g_w)
        wready[half] = P.inc("qW", (2 * NCH + len(chunks)) * 16)

        for g in range(NG):
            pb = g % 2

            def y_load(y, half=half, g=g, pb=pb):
                if (half, g - 2) in store_done:
                    y.wait_ge(*store_done[(half, g - 2)])
                elif half == 1:
                    y.wait_ge(*store_done[(0, NG - 2 + g)])
                if (half, g - 1, NCH - 1) in mm["xadd"]:
                    y.wait_ge(sV, mm["xadd"][(half, g - 1, NCH - 1)])
                elif half == 1 and (0, NG - 1, NCH - 1) in mm["xadd"]:
                    y.wait_ge(sV, mm["xadd"][(0, NG - 1, NCH - 1)])
                if final and half == 1 and (1, g - 1) in store_done:
                    y.wait_ge(*store_done[(1, g - 1)])
                src = xin if half == 0 else xout
                if g == 0:
                    y.dma_start(xth[pb][:, 0:NCH * SW].rearrange("p (c t) -> p c t", t=SW)[:, :, 0:1],
                                xh1[:, :, :].rearrange("c p t -> p c t"),
                                allow_slow_non_contiguous=True).then_inc(qL[pb], 16)
                    y.dma_start(xth[pb][:, :].rearrange("p (c t) -> p c t", t=SW)[:, :, 1:SW],
                                xin[:, :, 0:G].rearrange("c p t -> p c t")).then_inc(qL[pb], 16)
                else:
                    y.dma_start(xth[pb][:, :].rearrange("p (c t) -> p c t", t=SW),
                                xin[:, :, g * G - 1:(g + 1) * G].rearrange("c p t -> p c t")
                                ).then_inc(qL[pb], 16)
                if half == 1:
                    y.wait_ge(*store_done[(0, g)])
                    y.dma_start(tOx[pb][:, :].rearrange("p (c t) -> p c t", c=NCH),
                                xout[:, :, g * G:(g + 1) * G].rearrange("c p t -> p c t")
                                ).then_inc(qL[pb], 16)
                if final and half == 1:
                    y.dma_start(xtr[pb][:, :].rearrange("p (c t) -> p c t", c=NCH),
                                xres[:, :, g * G:(g + 1) * G].rearrange("c p t -> p c t")
                                ).then_inc(qL[pb], 16)
            P.ysec(y_load)
            nload = (2 if g == 0 else 1) + (1 if half == 1 else 0) + (1 if (final and half == 1) else 0)
            load_done[(half, g)] = (qL[pb], P.inc("qL%d" % pb, 16 * nload))

            def v_wait(v, half=half, g=g):
                v.wait_ge(*load_done[(half, g)])
                v.wait_ge(qW, wready[half])
                v.wait_ge(sG, gready)
                key2 = (half, g - 2) if g >= 2 else ((0, NG - 2 + g) if half == 1 else None)
                if key2 is not None:
                    v.wait_ge(sT, mm["amm"][(key2[0], key2[1], 10)])
            P.vsec(v_wait)

            bufs = dict(xt=xth[pb], x_bf=x_bf, x2_bf=x2_bf, u_bf=h_bf[pb],
                        Bmu=Bmu, Bvar=Bvar, Brstd=Brstd, tA=tA, tB=tB,
                        ones_bf=ones_bf, psB=psB, epscol=epscol,
                        lg=s_lg, lb=s_lb)
            uapply[(half, g)] = emit_ln_group(P, bufs, 512, 0,
                                             soff=lambda ch: (1 if ch < 4 else 0),
                                             sstride=SW, doff=0, dstride=512,
                                             affine=True)

            for c, (off, w) in enumerate(chunks):
                woff = off - half * HHALF

                def t_ag(t, half=half, g=g, c=c, w=w, woff=woff, pb=pb):
                    if c == 0:
                        t.wait_ge(sV, uapply[(half, g)])
                    if (half, g, c - 1) in mm["gev"]:
                        t.wait_ge(sS, mm["gev"][(half, g, c - 1)])
                        t.wait_ge(sV, mm["midv"][(half, g, c - 1)])
                    elif (half, g - 1, 10) in mm["gev"]:
                        t.wait_ge(sS, mm["gev"][(half, g - 1, 10)])
                        t.wait_ge(sV, mm["midv"][(half, g - 1, 10)])
                    elif half == 1 and (0, NG - 1, 10) in mm["gev"]:
                        t.wait_ge(sS, mm["gev"][(0, NG - 1, 10)])
                        t.wait_ge(sV, mm["midv"][(0, NG - 1, 10)])
                    for kc in range(NCH):
                        t.matmul(psA[0:w, :],
                                 s_w1[:, kc * 2816 + woff: kc * 2816 + woff + w],
                                 h_bf[pb][:, _chsl(kc)],
                                 start=(kc == 0), stop=(kc == NCH - 1))
                    for kc in range(NCH):
                        i = t.matmul(psG[0:w, :],
                                     s_w1[:, kc * 2816 + 1408 + woff: kc * 2816 + 1408 + woff + w],
                                     h_bf[pb][:, _chsl(kc)],
                                     start=(kc == 0), stop=(kc == NCH - 1))
                    i.then_inc(sT, 1)
                P.tsec(t_ag)
                mm["amm"][(half, g, c)] = tag = P.inc("sT")

                def s_ag(s, half=half, g=g, c=c, w=w, tag=tag):
                    s.wait_ge(sT, tag)
                    if (half, g, c - 1) in mm["midv"]:
                        s.wait_ge(sV, mm["midv"][(half, g, c - 1)])
                    elif (half, g - 1, 10) in mm["midv"]:
                        s.wait_ge(sV, mm["midv"][(half, g - 1, 10)])
                    elif half == 1 and (0, NG - 1, 10) in mm["midv"]:
                        s.wait_ge(sV, mm["midv"][(0, NG - 1, 10)])
                    bcol = half * 11 + c
                    s.activation(gscr[0:w, :], psG[0:w, :], AF.Gelu, bias=s_b1g[0:w, bcol:bcol + 1]).then_inc(sS, 1)
                P.ssec(s_ag)
                mm["aev"][(half, g, c)] = mm["gev"][(half, g, c)] = P.inc("sS")

                def v_mid(v, half=half, g=g, c=c, w=w):
                    v.wait_ge(sS, mm["aev"][(half, g, c)])
                    if (half, g - 1, c) in mm["w2"]:
                        v.wait_ge(sT, mm["w2"][(half, g - 1, NCH - 1)])
                    elif half == 1 and c == 0 and g == 0:
                        v.wait_ge(sT, mm["w2"][(0, NG - 1, NCH - 1)])
                    bcol = half * 11 + c
                    v.scalar_tensor_tensor(mid[0:w, c * G:(c + 1) * G], psA[0:w, :],
                                           s_b1a[0:w, bcol:bcol + 1], gscr[0:w, :],
                                           ALU.add, ALU.mult).then_inc(sV, 1)
                P.vsec(v_mid)
                mm["midv"][(half, g, c)] = P.inc("sV")

            for oc in range(NCH):
                op = oc % 2

                def t_w2(t, half=half, g=g, oc=oc, op=op, chunks=chunks):
                    t.wait_ge(sV, mm["midv"][(half, g, 10)])
                    if oc >= 2:
                        t.wait_ge(sV, mm["xadd"][(half, g, oc - 2)])
                    elif (half, g - 1, NCH - 2 + oc) in mm["xadd"]:
                        t.wait_ge(sV, mm["xadd"][(half, g - 1, NCH - 2 + oc)])
                    elif half == 1 and (0, NG - 1, NCH - 2 + oc) in mm["xadd"]:
                        t.wait_ge(sV, mm["xadd"][(0, NG - 1, NCH - 2 + oc)])
                    for c, (off, w) in enumerate(chunks):
                        i = t.matmul(psO[op][:, :],
                                     s_w2[0:w, c * 1024 + oc * 128: c * 1024 + oc * 128 + 128],
                                     mid[0:w, c * G:(c + 1) * G],
                                     start=(c == 0), stop=(c == len(chunks) - 1))
                    i.then_inc(sT, 1)
                P.tsec(t_w2)
                mm["w2"][(half, g, oc)] = P.inc("sT")

                def v_add(v, half=half, g=g, oc=oc, op=op, pb=pb):
                    v.wait_ge(sT, mm["w2"][(half, g, oc)])
                    if (half, g - 2) in store_done:
                        v.wait_ge(*store_done[(half, g - 2)])
                    elif half == 1 and (0, NG - 2 + g) in store_done:
                        v.wait_ge(*store_done[(0, NG - 2 + g)])
                    if half == 0:
                        v.scalar_tensor_tensor(
                            tOx[pb][:, _chsl(oc)], psO[op][:, :], s_b2[:, oc:oc + 1],
                            xth[pb][:, :].rearrange("p (c t) -> p c t", t=SW)[:, oc, 1:SW].rearrange("p t -> p t"),
                            ALU.add, ALU.add).then_inc(sV, 1)
                    else:
                        i = v.tensor_tensor(tOx[pb][:, _chsl(oc)], tOx[pb][:, _chsl(oc)],
                                            psO[op][:, :], ALU.add)
                        if final:
                            v.drain()
                            i = v.scalar_tensor_tensor(xtr[pb][:, _chsl(oc)],
                                                       tOx[pb][:, _chsl(oc)], 0.5,
                                                       xtr[pb][:, _chsl(oc)],
                                                       ALU.mult, ALU.add)
                        i.then_inc(sV, 1)
                        if final:
                            v.drain()
                P.vsec(v_add)
                mm["xadd"][(half, g, oc)] = P.inc("sV")

            def g_store(g_, half=half, g=g, pb=pb):
                g_.wait_ge(sV, mm["xadd"][(half, g, NCH - 1)])
                src = xtr[pb] if (final and half == 1) else tOx[pb]
                g_.dma_start(xout[:, :, g * G:(g + 1) * G].rearrange("c p t -> p c t"),
                             src[:, :].rearrange("p (c t) -> p c t", c=NCH)
                             ).then_inc(qS[pb], 16)
            P.gsec(g_store)
            store_done[(half, g)] = (qS[pb], P.inc("qS%d" % pb, 16))

    def g_fin(g_):
        g_.wait_ge(qS[0], P.cnt["qS0"])
        g_.wait_ge(qS[1], P.cnt["qS1"])
    P.gsec(g_fin)


    P.pop_scope()
    P.emit_barrier()


def build_l0ffn(final):
    key = f"l0ffn{int(final)}"
    if key in _PROG_CACHE:
        return _PROG_CACHE[key]
    P = Prog(key)
    nc = P.nc
    ext = {}
    ext["xin"] = nc.dram_tensor("xin", [NCH, 128, TPC], F32, kind="ExternalInput")
    ext["xh1"] = nc.dram_tensor("xh1", [NCH, 128, 1], F32, kind="ExternalInput")
    ext["w1"] = nc.dram_tensor("w1", [DD, 2 * HID], F32, kind="ExternalInput")
    ext["w2"] = nc.dram_tensor("w2", [HID, DD], F32, kind="ExternalInput")
    ext["b1at"] = nc.dram_tensor("b1at", [128, 22], F32, kind="ExternalInput")
    ext["b1gt"] = nc.dram_tensor("b1gt", [128, 22], F32, kind="ExternalInput")
    ext["b2t"] = nc.dram_tensor("b2t", [128, 8], F32, kind="ExternalInput")
    ext["lgt"] = nc.dram_tensor("lgt", [128, 8], F32, kind="ExternalInput")
    ext["lbt"] = nc.dram_tensor("lbt", [128, 8], F32, kind="ExternalInput")
    if final:
        ext["xres"] = nc.dram_tensor("xres", [NCH, 128, TPC], F32, kind="ExternalInput")
    ext["xout"] = nc.dram_tensor("xout", [NCH, 128, TPC], F32, kind="ExternalOutput")
    P.std_sems()
    P.open()
    emit_l0ffn(P, ext, final)
    P.close()
    _PROG_CACHE[key] = P
    return P


def emit_ssm_p1(P, ext):
    nc = P.nc
    xin = ext["xin"]
    wqT = ext["wqT"]
    wkT = ext["wkT"]
    waT = ext["waT"]
    wvT = ext["wvT"]
    abt = ext["abt"]
    cut = ext["cut"]
    chs = ext["chs"]
    cnhs = ext["cnhs"]
    cid8 = ext["cid8"]
    cid8b = ext["cid8b"]
    sloc = ext["sloc"]
    abT_out = ext["abT_out"]

    P.push_scope()
    swq = P.sb("swq", [128, NCH * 512], BF16)
    swk = P.sb("swk", [128, NCH * 512], BF16)
    swa = P.sb("swa", [128, NCH * 512], BF16)
    swv = P.sb("swv", [128, NCH * 1024], BF16)
    s_abt = P.sb("s_abt", [128, 4], F32)
    s_ut = P.sb("s_ut", [128, 128], F32)
    s_hs = P.sb("s_hs", [128, 32], F32)
    s_nhs = P.sb("s_nhs", [128, 32], F32)
    s_id8 = P.sb("s_id8", [8, 8], F32)
    s_id8b = P.sb("s_id8b", [8, 8], BF16)
    ones_bf = P.sb("ones_bf", [128, 128], BF16)
    ones_col = P.sb("ones_col", [128, 1], BF16)
    onecol_f = P.sb("onecol_f", [128, 1], F32)
    epscol = P.sb("epscol", [128, 1], F32)
    xt = [P.sb(f"xt{i}", [128, TPC], F32) for i in range(2)]
    x_bf = P.sb("xbf", [128, TPC], BF16)
    x2_bf = P.sb("x2bf", [128, TPC], BF16)
    u_bf = [P.sb(f"ubf{i}", [128, TPC], BF16) for i in range(2)]
    qf = P.sb("qf", [128, G], F32)
    tqk = P.sb("tqk", [128, G], F32)
    r_sp = P.sb("r_sp", [128, G], F32)
    aab = P.sb("aab", [128, G], F32)
    e_sp = P.sb("e_sp", [128, G], F32)
    l_sp = P.sb("l_sp", [128, G], F32)
    sp = P.sb("sp", [128, G], F32)
    v_bf = P.sb("v_bf", [128, 1024], BF16)
    aT_s = P.sb("aT_s", [8, TPC], F32)
    bT_s = P.sb("bT_s", [8, TPC], F32)
    aTok = P.sb("aTok", [128, 8], F32)
    ATr = P.sb("ATr", [8, 128], F32)
    bmA_T = P.sb("bmA_T", [8, 128], F32)
    wT_bf = P.sb("wT_bf", [8, 128], BF16)
    w_bf = P.sb("w_bf", [128, 8], BF16)
    eal = P.sb("eal", [8, 1], F32)
    diagE = P.sb("diagE", [8, 8], F32)
    S_sb = P.sb("S_sb", [8, 1024], F32)
    Z_sb = P.sb("Z_sb", [8, 1], F32)
    logD = P.sb("logD", [8, 1], F32)
    outbuf = P.sb("outbuf", [8, 1026], F32)
    Bmu = P.sb("Bmu", [128, G], F32)
    Bvar = P.sb("Bvar", [128, G], F32)
    Brstd = P.sb("Brstd", [128, G], F32)
    tA = P.sb("tA", [128, G], F32)
    tB = P.sb("tB", [128, G], F32)

    psB = P.ps("psB", [128, 2 * G])
    psQ = P.ps("psQ", [128, G])
    psK = P.ps("psK", [128, G])
    psSh = P.ps("psSh", [8, 1024])
    psM = P.ps("psM", [128, 138])
    psW = P.ps("psW", [128, 8], BF16)

    qL = [P.handles["qL0"], P.handles["qL1"]]
    qW = P.handles["qW"]
    sG = P.handles["sG"]
    sT = P.handles["sT"]; sV = P.handles["sV"]; sS = P.handles["sS"]

    def g_pro(g_):
        for a in range(NCH):
            g_.dma_start(swq[:, a * 512:(a + 1) * 512], wqT[a * 128:(a + 1) * 128, :]).then_inc(qW, 16)
            g_.dma_start(swk[:, a * 512:(a + 1) * 512], wkT[a * 128:(a + 1) * 128, :]).then_inc(qW, 16)
            g_.dma_start(swa[:, a * 512:(a + 1) * 512], waT[a * 128:(a + 1) * 128, :]).then_inc(qW, 16)
            g_.dma_start(swv[:, a * 1024:(a + 1) * 1024], wvT[a * 128:(a + 1) * 128, :]).then_inc(qW, 16)
        for nm, dst in [(abt, s_abt), (cut, s_ut), (chs, s_hs), (cnhs, s_nhs),
                        (cid8, s_id8), (cid8b, s_id8b)]:
            g_.dma_start(dst[:, :], nm[:, :]).then_inc(qW, 16)
        g_.memset(ones_bf[:, :], 1.0)
        g_.memset(ones_col[:, :], 1.0)
        g_.memset(onecol_f[:, :], 1.0)
        g_.memset(epscol[:, :], 1e-5)
        g_.memset(S_sb[:, :], 0.0)
        g_.memset(Z_sb[:, :], 0.0)
        g_.memset(logD[:, :], 0.0).then_inc(sG, 1)
    P.gsec(g_pro)
    wready = P.inc("qW", 38 * 16)
    gready = P.inc("sG")

    load_done = {}
    uapply = {}
    mile = {"qkev": {}, "tqk": {}, "bmm": {}, "amm": {}, "spv": {}, "spev": {},
            "vev": {}, "scan": {}, "batev": {}, "vmm": {}, "sscan": {}}

    for g in range(NG):
        pb = g % 2

        def y_load(y, g=g, pb=pb):
            if g >= 2:
                y.wait_ge(sV, uapply[g - 2])
            y.dma_start(xt[pb][:, :].rearrange("p (c t) -> p c t", c=NCH),
                        xin[:, :, g * G:(g + 1) * G].rearrange("c p t -> p c t")
                        ).then_inc(qL[g % 2], 16)
        P.ysec(y_load)
        load_done[g] = (qL[g % 2], P.inc("qL%d" % (g % 2), 16))

        def v_wait(v, g=g):
            v.wait_ge(*load_done[g])
            v.wait_ge(qW, wready)
            v.wait_ge(sG, gready)
            if g >= 2:
                v.wait_ge(sT, mile["vmm"][(g - 2, 3)])
        P.vsec(v_wait)

        bufs = dict(xt=xt[pb], x_bf=x_bf, x2_bf=x2_bf, u_bf=u_bf[pb],
                    Bmu=Bmu, Bvar=Bvar, Brstd=Brstd, tA=tA, tB=tB,
                    ones_bf=ones_bf, psB=psB, epscol=epscol)
        uapply[g] = emit_ln_group(P, bufs, G, 0)

        for oc in range(4):
            def t_qk(t, g=g, oc=oc, pb=pb):
                if oc == 0:
                    t.wait_ge(sV, uapply[g])
                    if g >= 1:
                        t.wait_ge(sS, mile["sscan"][(g - 1, 3)])
                else:
                    t.wait_ge(sS, mile["qkev"][(g, oc - 1)])
                    t.wait_ge(sV, mile["tqk"][(g, oc - 1)])
                for kc in range(NCH):
                    t.matmul(psQ[:, :], swq[:, kc * 512 + oc * 128: kc * 512 + oc * 128 + 128],
                             u_bf[pb][:, _chsl(kc)], start=(kc == 0), stop=(kc == NCH - 1))
                for kc in range(NCH):
                    i = t.matmul(psK[:, :], swk[:, kc * 512 + oc * 128: kc * 512 + oc * 128 + 128],
                                 u_bf[pb][:, _chsl(kc)], start=(kc == 0), stop=(kc == NCH - 1))
                i.then_inc(sT, 1)
            P.tsec(t_qk)
            tqkmm = P.inc("sT")

            def s_qev(s, tqkmm=tqkmm):
                s.wait_ge(sT, tqkmm)
                s.activation(qf[:, :], psQ[:, :], AF.Identity).then_inc(sS, 1)
            P.ssec(s_qev)
            mile["qkev"][(g, oc)] = P.inc("sS")

            def v_tqk(v, g=g, oc=oc):
                v.wait_ge(sS, mile["qkev"][(g, oc)])
                if (g, oc - 1) in mile["bmm"]:
                    v.wait_ge(sT, mile["bmm"][(g, oc - 1)])
                v.tensor_tensor(tqk[:, :], qf[:, :], psK[:, :], ALU.mult).then_inc(sV, 1)
            P.vsec(v_tqk)
            mile["tqk"][(g, oc)] = P.inc("sV")

            def t_bmm(t, g=g, oc=oc):
                t.wait_ge(sV, mile["tqk"][(g, oc)])
                t.matmul(psSh[:, 0:512], s_hs[:, oc * 8:(oc + 1) * 8], tqk[:, :],
                         start=(oc == 0), stop=(oc == 3)).then_inc(sT, 1)
            P.tsec(t_bmm)
            mile["bmm"][(g, oc)] = P.inc("sT")

            def t_amm(t, g=g, oc=oc, pb=pb):
                t.wait_ge(sS, mile["qkev"][(g, oc)])
                if (g, oc - 1) in mile["spev"]:
                    t.wait_ge(sS, mile["spev"][(g, oc - 1)])
                for kc in range(NCH):
                    i = t.matmul(psQ[:, :], swa[:, kc * 512 + oc * 128: kc * 512 + oc * 128 + 128],
                                 u_bf[pb][:, _chsl(kc)], start=(kc == 0), stop=(kc == NCH - 1))
                i.then_inc(sT, 1)
            P.tsec(t_amm)
            tamm = P.inc("sT")

            def s_sp(s, g=g, oc=oc, tamm=tamm):
                s.wait_ge(sT, tamm)
                if (g, oc - 1) in mile["spv"]:
                    s.wait_ge(sV, mile["spv"][(g, oc - 1)])
                s.activation(r_sp[:, :], psQ[:, :], AF.Relu, bias=s_abt[:, oc:oc + 1])
                s.activation(aab[:, :], psQ[:, :], AF.Abs, bias=s_abt[:, oc:oc + 1])
                s.drain()
                s.activation(e_sp[:, :], aab[:, :], AF.Exp, scale=-1.0)
                s.drain()
                s.activation(l_sp[:, :], e_sp[:, :], AF.Ln, bias=onecol_f[:, 0:1]).then_inc(sS, 1)
            P.ssec(s_sp)
            mile["spev"][(g, oc)] = P.inc("sS")

            def v_sp(v, g=g, oc=oc):
                v.wait_ge(sS, mile["spev"][(g, oc)])
                if (g, oc - 1) in mile["amm"]:
                    v.wait_ge(sT, mile["amm"][(g, oc - 1)])
                v.tensor_tensor(sp[:, :], r_sp[:, :], l_sp[:, :], ALU.add).then_inc(sV, 1)
            P.vsec(v_sp)
            mile["spv"][(g, oc)] = P.inc("sV")

            def t_arow(t, g=g, oc=oc):
                t.wait_ge(sV, mile["spv"][(g, oc)])
                t.matmul(psSh[:, 512:1024], s_nhs[:, oc * 8:(oc + 1) * 8], sp[:, :],
                         start=(oc == 0), stop=(oc == 3)).then_inc(sT, 1)
            P.tsec(t_arow)
            mile["amm"][(g, oc)] = P.inc("sT")

        def s_bat(s, g=g):
            s.wait_ge(sT, mile["amm"][(g, 3)])
            s.activation(bT_s[:, g * G:(g + 1) * G], psSh[:, 0:512], AF.Identity, scale=0.125)
            s.activation(aT_s[:, g * G:(g + 1) * G], psSh[:, 512:1024], AF.Identity).then_inc(sS, 1)
        P.ssec(s_bat)
        mile["batev"][g] = P.inc("sS")

        for tl in range(4):
            def t_v(t, g=g, tl=tl, pb=pb):
                if tl == 0:
                    t.wait_ge(sV, mile["tqk"][(g, 3)])
                    t.wait_ge(sS, mile["spev"][(g, 3)])
                else:
                    t.wait_ge(sS, mile["vev"][(g, tl - 1)])
                for hh in range(2):
                    for kc in range(NCH):
                        i = t.matmul(psK[:, :] if hh == 0 else psQ[:, :],
                                     u_bf[pb][:, kc * G + tl * 128: kc * G + tl * 128 + 128],
                                     swv[:, kc * 1024 + hh * 512: kc * 1024 + hh * 512 + 512],
                                     start=(kc == 0), stop=(kc == NCH - 1))
                i.then_inc(sT, 1)
            P.tsec(t_v)
            mile["vmm"][(g, tl)] = P.inc("sT")

            def s_vev(s, g=g, tl=tl):
                s.wait_ge(sT, mile["vmm"][(g, tl)])
                if (g, tl - 1) in mile["scan"]:
                    s.wait_ge(sT, mile["scan"][(g, tl - 1)])
                elif g >= 1 and tl == 0:
                    s.wait_ge(sT, mile["scan"][(g - 1, 3)])
                s.activation(v_bf[:, 0:512], psK[:, :], AF.Identity)
                s.activation(v_bf[:, 512:1024], psQ[:, :], AF.Identity).then_inc(sS, 1)
            P.ssec(s_vev)
            mile["vev"][(g, tl)] = P.inc("sS")

            sl = slice(g * G + tl * 128, g * G + tl * 128 + 128)

            def t_tr1(t, g=g, tl=tl, sl=sl):
                t.wait_ge(sS, mile["batev"][g])
                key2 = (g, tl - 1) if tl >= 1 else ((g - 1, 3) if g >= 1 else None)
                if key2 is not None:
                    t.wait_ge(sS, mile["sscan"][key2])
                t.transpose(psM[:, 0:8], aT_s[:, sl], s_id8[:, :]).then_inc(sT, 1)
            P.tsec(t_tr1)
            ttr1 = P.inc("sT")

            def s_tr1(s, ttr1=ttr1):
                s.wait_ge(sT, ttr1)
                s.activation(aTok[:, :], psM[:, 0:8], AF.Identity).then_inc(sS, 1)
            P.ssec(s_tr1)
            str1 = P.inc("sS")

            def t_pre(t, str1=str1):
                t.wait_ge(sS, str1)
                t.matmul(psM[0:8, 8:136], aTok[:, :], s_ut[:, :], start=True, stop=True).then_inc(sT, 1)
            P.tsec(t_pre)
            tpre = P.inc("sT")

            def s_pre(s, tpre=tpre):
                s.wait_ge(sT, tpre)
                s.activation(ATr[:, :], psM[0:8, 8:136], AF.Identity).then_inc(sS, 1)
            P.ssec(s_pre)
            spre = P.inc("sS")

            def v_bma(v, g=g, sl=sl, spre=spre):
                v.wait_ge(sS, spre)
                v.tensor_tensor(bmA_T[:, :], bT_s[:, sl], ATr[:, :], ALU.subtract).then_inc(sV, 1)
            P.vsec(v_bma)
            vbma = P.inc("sV")

            def s_w(s, vbma=vbma):
                s.wait_ge(sV, vbma)
                s.activation(wT_bf[:, :], bmA_T[:, :], AF.Exp, bias=ATr[:, 127:128])
                s.drain()
                s.activation(eal[:, :], ATr[:, 127:128], AF.Exp).then_inc(sS, 1)
            P.ssec(s_w)
            sw_ = P.inc("sS")

            def t_trw(t, sw_=sw_):
                t.wait_ge(sS, sw_)
                t.transpose(psW[:, :], wT_bf[:, :], s_id8b[:, :]).then_inc(sT, 1)
            P.tsec(t_trw)
            ttrw = P.inc("sT")

            def s_trw(s, ttrw=ttrw):
                s.wait_ge(sT, ttrw)
                s.activation(w_bf[:, :], psW[:, :], AF.Identity).then_inc(sS, 1)
            P.ssec(s_trw)
            strw = P.inc("sS")

            def v_diag(v, g=g, tl=tl, strw=strw):
                v.wait_ge(sS, strw)
                key2 = (g, tl - 1) if tl >= 1 else ((g - 1, 3) if g >= 1 else None)
                if key2 is not None:
                    v.wait_ge(sT, mile["scan"][key2])
                v.tensor_scalar(diagE[:, :], s_id8[:, :], eal[:, 0:1], None, ALU.mult)
                v.drain()
                v.tensor_tensor(logD[:, :], logD[:, :], ATr[:, 127:128], ALU.add).then_inc(sV, 1)
                v.drain()
            P.vsec(v_diag)
            vdiag = P.inc("sV")

            def t_scan(t, g=g, tl=tl, vdiag=vdiag):
                t.wait_ge(sV, vdiag)
                t.wait_ge(sS, mile["vev"][(g, tl)])
                t.matmul(psSh[:, 0:512], w_bf[:, :], v_bf[:, 0:512], start=True, stop=False)
                t.matmul(psSh[:, 0:512], diagE[:, :], S_sb[:, 0:512], start=False, stop=True)
                t.matmul(psSh[:, 512:1024], w_bf[:, :], v_bf[:, 512:1024], start=True, stop=False)
                t.matmul(psSh[:, 512:1024], diagE[:, :], S_sb[:, 512:1024], start=False, stop=True)
                t.matmul(psM[0:8, 136:137], w_bf[:, :], ones_col[:, :], start=True, stop=False)
                i = t.matmul(psM[0:8, 136:137], diagE[:, :], Z_sb[:, :], start=False, stop=True)
                i.then_inc(sT, 1)
            P.tsec(t_scan)
            mile["scan"][(g, tl)] = P.inc("sT")

            def s_scan(s, g=g, tl=tl):
                s.wait_ge(sT, mile["scan"][(g, tl)])
                s.activation(S_sb[:, :], psSh[:, :], AF.Identity)
                s.activation(Z_sb[:, :], psM[0:8, 136:137], AF.Identity).then_inc(sS, 1)
            P.ssec(s_scan)
            mile["sscan"][(g, tl)] = sscan = P.inc("sS")

            def t_scwait(t, g=g, tl=tl, sscan=sscan):
                t.wait_ge(sS, sscan)
            P.tsec(t_scwait)

    def v_fin(v):
        v.wait_ge(sS, mile["sscan"][(NG - 1, 3)])
        v.tensor_copy(outbuf[:, 0:1024], S_sb[:, :])
        v.tensor_copy(outbuf[:, 1024:1025], Z_sb[:, :])
        v.tensor_copy(outbuf[:, 1025:1026], logD[:, :]).then_inc(sV, 1)
    P.vsec(v_fin)
    vfin = P.inc("sV")

    def g_fin(g_):
        g_.wait_ge(sV, vfin)
        g_.dma_start(sloc[:, :], outbuf[:, :]).then_inc(qW, 16)
        g_.wait_ge(sS, mile["batev"][NG - 1])
        g_.dma_start(abT_out[0:8, :], aT_s[:, :]).then_inc(qW, 16)
        g_.dma_start(abT_out[8:16, :], bT_s[:, :]).then_inc(qW, 16)
        g_.wait_ge(qW, P.cnt["qW"] + 48)
    P.gsec(g_fin)
    P.inc("qW", 48)


    P.pop_scope()
    P.emit_barrier()


def build_ssm_p1():
    key = "ssm1"
    if key in _PROG_CACHE:
        return _PROG_CACHE[key]
    P = Prog(key)
    nc = P.nc
    ext = {}
    ext["xin"] = nc.dram_tensor("xin", [NCH, 128, TPC], F32, kind="ExternalInput")
    ext["wqT"] = nc.dram_tensor("wqT", [DD, 512], F32, kind="ExternalInput")
    ext["wkT"] = nc.dram_tensor("wkT", [DD, 512], F32, kind="ExternalInput")
    ext["waT"] = nc.dram_tensor("waT", [DD, 512], F32, kind="ExternalInput")
    ext["wvT"] = nc.dram_tensor("wvT", [DD, DD], F32, kind="ExternalInput")
    ext["abt"] = nc.dram_tensor("abt", [128, 4], F32, kind="ExternalInput")
    ext["cut"] = nc.dram_tensor("cut", [128, 128], F32, kind="ExternalInput")
    ext["chs"] = nc.dram_tensor("chs", [128, 32], F32, kind="ExternalInput")
    ext["cnhs"] = nc.dram_tensor("cnhs", [128, 32], F32, kind="ExternalInput")
    ext["cid8"] = nc.dram_tensor("cid8", [8, 8], F32, kind="ExternalInput")
    ext["cid8b"] = nc.dram_tensor("cid8b", [8, 8], BF16, kind="ExternalInput")
    ext["sloc"] = nc.dram_tensor("sloc", [8, 1026], F32, kind="ExternalOutput")
    ext["abT_out"] = nc.dram_tensor("abT_out", [16, TPC], F32, kind="ExternalOutput")
    P.std_sems()
    P.open()
    emit_ssm_p1(P, ext)
    P.close()
    _PROG_CACHE[key] = P
    return P


def emit_ssm_p2(P, ext):
    nc = P.nc
    xin = ext["xin"]
    abT_in = ext["abT_in"]
    slocall = ext["slocall"]
    psel = ext["psel"]
    wvT = ext["wvT"]
    woT = ext["woT"]
    cut = ext["cut"]
    cutb = ext["cutb"]
    cdsel = ext["cdsel"]
    cid8 = ext["cid8"]
    cid128 = ext["cid128"]
    cid128b = ext["cid128b"]
    clneg = ext["clneg"]
    xout = ext["xout"]

    P.push_scope()
    swv = P.sb("swv", [128, NCH * 1024], BF16)
    swo = P.sb("swo", [128, NCH * 1024], BF16)
    s_sl = P.sb("s_sl", [8, 4 * 1026], F32)
    s_ps = P.sb("s_ps", [8, 3], F32)
    s_ut = P.sb("s_ut", [128, 128], F32)
    s_utb = P.sb("s_utb", [128, 128], BF16)
    s_ds = P.sb("s_ds", [8, 1024], F32)
    s_id8 = P.sb("s_id8", [8, 8], F32)
    s_id128 = P.sb("s_id128", [128, 128], F32)
    s_id128b = P.sb("s_id128b", [128, 128], BF16)
    s_lneg = P.sb("s_lneg", [128, 128], BF16)
    ones_bf = P.sb("ones_bf", [128, 128], BF16)
    ones_col = P.sb("ones_col", [128, 1], BF16)
    ones8 = P.sb("ones8", [8, 128], F32)
    epscol = P.sb("epscol", [128, 1], F32)
    aT_s = P.sb("aT_s", [8, TPC], F32)
    bT_s = P.sb("bT_s", [8, TPC], F32)
    xt = [P.sb(f"xt{i}", [128, TPC], F32) for i in range(2)]
    x_bf = P.sb("xbf", [128, TPC], BF16)
    x2_bf = P.sb("x2bf", [128, TPC], BF16)
    u_bf = [P.sb(f"ubf{i}", [128, TPC], BF16) for i in range(2)]
    v_bf = P.sb("v_bf", [128, 1024], BF16)
    yTg = [P.sb(f"yTg{i}", [128, NCH * G], BF16) for i in range(2)]
    Scr8 = P.sb("Scr8", [8, 1024], F32)
    aTok = P.sb("aTok", [128, 8], F32)
    Atok = P.sb("Atok", [128, 8], F32)
    eA = P.sb("eA", [128, 8], F32)
    ATr = P.sb("ATr", [8, 128], F32)
    eATr = P.sb("eATr", [8, 128], F32)
    bmA_T = P.sb("bmA_T", [8, 128], F32)
    bmA = P.sb("bmA", [128, 8], F32)
    Mh = [P.sb(f"Mh{i}", [128, 128], BF16) for i in range(8)]
    MS_bf = P.sb("MS_bf", [128, 128], BF16)
    tzB = P.sb("tzB", [128, 1], F32)
    ZtotB = P.sb("ZtotB", [128, 1], F32)
    rzAll = P.sb("rzAll", [128, 8], F32)
    rzT_sb = P.sb("rzT_sb", [8, 128], F32)
    R_sb = P.sb("R_sb", [8, 128], F32)
    Mlast = P.sb("Mlast", [128, 8], BF16)
    diagE2 = P.sb("diagE2", [8, 8], F32)
    diagZ = P.sb("diagZ", [8, 8], F32)
    S_in = P.sb("S_in", [8, 1024], F32)
    Zin_col = P.sb("Zin_col", [8, 1], F32)
    Zbc = P.sb("Zbc", [128, 8], F32)
    Dk = P.sb("Dk", [8, 1], F32)
    tO = [P.sb(f"tO{i}", [128, G], F32) for i in range(2)]
    Bmu = P.sb("Bmu", [128, G], F32)
    Bvar = P.sb("Bvar", [128, G], F32)
    Brstd = P.sb("Brstd", [128, G], F32)
    tA = P.sb("tA", [128, G], F32)
    tB = P.sb("tB", [128, G], F32)

    psB = P.ps("psB", [128, 2 * G])
    psY = P.ps("psY", [128, 1024])
    psM = P.ps("psM", [128, 141])
    psQZ = [P.ps(f"psQZ{i}", [128, 256]) for i in range(2)]

    qL = [P.handles["qL0"], P.handles["qL1"]]
    qS = [P.handles["qS0"], P.handles["qS1"]]
    qW = P.handles["qW"]
    sG = P.handles["sG"]
    sT = P.handles["sT"]; sV = P.handles["sV"]; sS = P.handles["sS"]

    def g_pro(g_):
        for a in range(NCH):
            g_.dma_start(swv[:, a * 1024:(a + 1) * 1024], wvT[a * 128:(a + 1) * 128, :]).then_inc(qW, 16)
            g_.dma_start(swo[:, a * 1024:(a + 1) * 1024], woT[a * 128:(a + 1) * 128, :]).then_inc(qW, 16)
        g_.dma_start(s_sl[:, :].rearrange("p (a n) -> p a n", a=4),
                     slocall[:, :, :].rearrange("a p n -> p a n")).then_inc(qW, 16)
        for nm, dst in [(psel, s_ps), (cut, s_ut), (cutb, s_utb), (cdsel, s_ds),
                        (cid8, s_id8), (cid128, s_id128),
                        (cid128b, s_id128b), (clneg, s_lneg)]:
            g_.dma_start(dst[:, :], nm[:, :]).then_inc(qW, 16)
        g_.dma_start(aT_s[:, :], abT_in[0:8, :]).then_inc(qW, 16)
        g_.dma_start(bT_s[:, :], abT_in[8:16, :]).then_inc(qW, 16)
        g_.memset(ones_bf[:, :], 1.0)
        g_.memset(ones_col[:, :], 1.0)
        g_.memset(ones8[:, :], 1.0)
        g_.memset(epscol[:, :], 1e-5).then_inc(sG, 1)
    P.gsec(g_pro)
    wready = P.inc("qW", 27 * 16)
    gready = P.inc("sG")

    def v_comb(v):
        v.wait_ge(qW, wready)
        v.wait_ge(sG, gready)
        v.tensor_scalar(S_in[:, :], s_sl[:, 0:1024], s_ps[:, 0:1], None, ALU.mult)
        v.tensor_scalar(Zin_col[:, :], s_sl[:, 1024:1025], s_ps[:, 0:1], None, ALU.mult)
        v.drain().then_inc(sV, 1)
    P.vsec(v_comb)
    prevv = P.inc("sV")

    for k in range(1, 3):
        def s_dk(s, k=k, prevv=prevv):
            s.wait_ge(sV, prevv)
            s.activation(Dk[:, :], s_sl[:, k * 1026 + 1025: k * 1026 + 1026], AF.Exp).then_inc(sS, 1)
        P.ssec(s_dk)
        sdk = P.inc("sS")

        def v_ck(v, k=k, sdk=sdk):
            v.wait_ge(sS, sdk)
            v.tensor_scalar(Dk[:, :], Dk[:, :], -1.0, s_ps[:, k:k + 1], ALU.add, ALU.mult)
            v.drain()
            v.tensor_scalar(Dk[:, :], Dk[:, :], 1.0, None, ALU.add)
            v.drain()
            v.tensor_scalar(S_in[:, :], S_in[:, :], Dk[:, 0:1], None, ALU.mult)
            v.tensor_scalar(Zin_col[:, :], Zin_col[:, :], Dk[:, 0:1], None, ALU.mult)
            v.drain()
            v.tensor_scalar(Scr8[:, :], s_sl[:, k * 1026: k * 1026 + 1024], s_ps[:, k:k + 1], None, ALU.mult)
            v.tensor_scalar(tzB[0:8, 0:1], s_sl[:, k * 1026 + 1024: k * 1026 + 1025], s_ps[:, k:k + 1], None, ALU.mult)
            v.drain()
            v.tensor_tensor(S_in[:, :], S_in[:, :], Scr8[:, :], ALU.add)
            v.tensor_tensor(Zin_col[:, :], Zin_col[:, :], tzB[0:8, 0:1], ALU.add)
            v.drain().then_inc(sV, 1)
        P.vsec(v_ck)
        prevv = P.inc("sV")

    def v_zb0(v, prevv=prevv):
        v.wait_ge(sV, prevv)
        v.tensor_scalar(diagZ[:, :], s_id8[:, :], Zin_col[:, 0:1], None, ALU.mult).then_inc(sV, 1)
    P.vsec(v_zb0)
    vzb0 = P.inc("sV")

    def t_zb0(t):
        t.wait_ge(sV, vzb0)
        t.matmul(psM[:, 0:8], ones8[:, :], diagZ[:, :], start=True, stop=True).then_inc(sT, 1)
    P.tsec(t_zb0)
    tzb0 = P.inc("sT")

    def s_zb0(s):
        s.wait_ge(sT, tzb0)
        s.activation(Zbc[:, :], psM[:, 0:8], AF.Identity).then_inc(sS, 1)
    P.ssec(s_zb0)
    zb0 = P.inc("sS")
    m = {"vmm": {}, "vev": {}, "yev": {}, "carry": {}, "cev": {}, "zbev": {-1: zb0},
         "mexp": {}, "msc": {}, "ztot": {}, "ymm": {}, "intery": {-1: P.cnt["sV"]},
         "cprep": {}, "rzt": {}, "wo": {}, "oev": {}, "xadd": {}}
    load_done = {}
    store_done = {}
    uapply = {}

    for g in range(NG):
        pb = g % 2

        def y_load(y, g=g, pb=pb):
            if g >= 2:
                y.wait_ge(*store_done[g - 2])
            y.dma_start(xt[pb][:, :].rearrange("p (c t) -> p c t", c=NCH),
                        xin[:, :, g * G:(g + 1) * G].rearrange("c p t -> p c t")
                        ).then_inc(qL[g % 2], 16)
        P.ysec(y_load)
        load_done[g] = (qL[g % 2], P.inc("qL%d" % (g % 2), 16))

        def v_wait(v, g=g):
            v.wait_ge(*load_done[g])
            if g == 0:
                v.wait_ge(sS, zb0)
            if g >= 2:
                v.wait_ge(sT, m["vmm"][(g - 2, 3, 1)])
        P.vsec(v_wait)

        bufs = dict(xt=xt[pb], x_bf=x_bf, x2_bf=x2_bf, u_bf=u_bf[pb],
                    Bmu=Bmu, Bvar=Bvar, Brstd=Brstd, tA=tA, tB=tB,
                    ones_bf=ones_bf, psB=psB, epscol=epscol)
        uapply[g] = emit_ln_group(P, bufs, G, 0)

        for tl in range(4):
            ck = g * 4 + tl
            sl = slice(g * G + tl * 128, g * G + tl * 128 + 128)

            for hh in range(2):
                def t_v(t, g=g, tl=tl, hh=hh, pb=pb, ck=ck):
                    if tl == 0 and hh == 0:
                        t.wait_ge(sV, uapply[g])
                        if g >= 1:
                            t.wait_ge(sV, m["xadd"][(g - 1, NCH - 1)])
                    if (ck, hh - 1) in m["vev"]:
                        t.wait_ge(sS, m["vev"][(ck, hh - 1)])
                    elif ck >= 1:
                        t.wait_ge(sS, m["vev"][(ck - 1, 1)])
                    for kc in range(NCH):
                        i = t.matmul(psB[:, 0:512],
                                     u_bf[pb][:, kc * G + tl * 128: kc * G + tl * 128 + 128],
                                     swv[:, kc * 1024 + hh * 512: kc * 1024 + hh * 512 + 512],
                                     start=(kc == 0), stop=(kc == NCH - 1))
                    i.then_inc(sT, 1)
                P.tsec(t_v)
                m["vmm"][(g, tl, hh)] = P.inc("sT")

                def s_vev(s, g=g, tl=tl, hh=hh, ck=ck):
                    s.wait_ge(sT, m["vmm"][(g, tl, hh)])
                    if ck >= 1 and hh == 0:
                        s.wait_ge(sT, m["carry"][ck - 1])
                    s.activation(v_bf[:, hh * 512:(hh + 1) * 512], psB[:, 0:512],
                                 AF.Identity).then_inc(sS, 1)
                P.ssec(s_vev)
                m["vev"][(ck, hh)] = P.inc("sS")

            def t_tr1(t, sl=sl, ck=ck):
                t.wait_ge(qW, wready)
                t.wait_ge(sS, m["zbev"][ck - 1])
                t.transpose(psM[:, 0:8], aT_s[:, sl], s_id8[:, :]).then_inc(sT, 1)
            P.tsec(t_tr1)
            ttr1 = P.inc("sT")

            def s_tr1(s, ttr1=ttr1):
                s.wait_ge(sT, ttr1)
                s.activation(aTok[:, :], psM[:, 0:8], AF.Identity).then_inc(sS, 1)
            P.ssec(s_tr1)
            str1 = P.inc("sS")

            def t_pre(t, str1=str1):
                t.wait_ge(sS, str1)
                t.matmul(psM[:, 0:8], s_ut[:, :], aTok[:, :], start=True, stop=True)
                i = t.matmul(psM[0:8, 8:136], aTok[:, :], s_ut[:, :], start=True, stop=True)
                i.then_inc(sT, 1)
            P.tsec(t_pre)
            tpre = P.inc("sT")

            def s_pre(s, tpre=tpre, ck=ck):
                s.wait_ge(sT, tpre)
                s.wait_ge(sV, m["cprep"].get(ck - 1, 0))
                s.activation(Atok[:, :], psM[:, 0:8], AF.Identity)
                s.activation(ATr[:, :], psM[0:8, 8:136], AF.Identity)
                s.drain()
                s.activation(eA[:, :], Atok[:, :], AF.Exp)
                s.activation(eATr[:, :], ATr[:, :], AF.Exp).then_inc(sS, 1)
            P.ssec(s_pre)
            spre = P.inc("sS")

            def v_bma(v, sl=sl, spre=spre):
                v.wait_ge(sS, spre)
                v.tensor_tensor(bmA_T[:, :], bT_s[:, sl], ATr[:, :], ALU.subtract).then_inc(sV, 1)
            P.vsec(v_bma)
            vbma = P.inc("sV")

            def t_tr2(t, vbma=vbma):
                t.wait_ge(sV, vbma)
                t.transpose(psM[:, 0:8], bmA_T[:, :], s_id8[:, :]).then_inc(sT, 1)
            P.tsec(t_tr2)
            ttr2 = P.inc("sT")

            def s_tr2(s, ttr2=ttr2):
                s.wait_ge(sT, ttr2)
                s.activation(bmA[:, :], psM[:, 0:8], AF.Identity).then_inc(sS, 1)
            P.ssec(s_tr2)
            str2 = P.inc("sS")

            for h in range(8):
                bq = h % 2

                def t_bq(t, h=h, bq=bq, ck=ck, str2=str2):
                    if h == 0:
                        t.wait_ge(sS, str2)
                        t.wait_ge(sV, m["msc"].get((ck - 1, 6 + bq), 0))
                    elif h == 1:
                        t.wait_ge(sV, m["msc"].get((ck - 1, 6 + bq), 0))
                    if (ck, h - 2) in m["mexp"]:
                        t.wait_ge(sS, m["mexp"][(ck, h - 2)])
                    t.matmul(psQZ[bq][:, 0:128], s_id128b[:, :], s_lneg[:, :],
                             start=True, stop=False)
                    t.matmul(psQZ[bq][:, 0:128], s_ds[:, h * 128:(h + 1) * 128],
                             ATr[:, :], start=False, stop=True).then_inc(sT, 1)
                P.tsec(t_bq)
                tbq = P.inc("sT")

                def s_mexp(s, h=h, bq=bq, tbq=tbq, ck=ck):
                    s.wait_ge(sT, tbq)
                    if h <= 1:
                        s.wait_ge(sV, m["cprep"].get(ck - 1, 0))
                    s.activation(Mh[h][:, :], psQZ[bq][:, 0:128],
                                 AF.Exp, bias=bmA[:, h:h + 1]).then_inc(sS, 1)
                P.ssec(s_mexp)
                m["mexp"][(ck, h)] = P.inc("sS")

                def t_zc(t, h=h, ck=ck):
                    t.wait_ge(sS, m["mexp"][(ck, h)])
                    if (ck, h - 1) in m["ztot"]:
                        t.wait_ge(sV, m["ztot"][(ck, h - 1)])
                    t.matmul(psM[:, 136:137], Mh[h][:, :], ones_col[:, :],
                             start=True, stop=True).then_inc(sT, 1)
                P.tsec(t_zc)
                tzc = P.inc("sT")

                def v_ztot(v, h=h, ck=ck, tzc=tzc):
                    v.wait_ge(sT, tzc)
                    v.tensor_tensor(tzB[:, :], eA[:, h:h + 1], Zbc[:, h:h + 1], ALU.mult)
                    v.drain()
                    v.tensor_tensor(ZtotB[:, :], psM[:, 136:137], tzB[:, :], ALU.add)
                    v.drain()
                    v.reciprocal(rzAll[:, h:h + 1], ZtotB[:, :]).then_inc(sV, 1)
                    v.drain()
                P.vsec(v_ztot)
                m["ztot"][(ck, h)] = P.inc("sV")

            def t_rz(t, ck=ck):
                t.wait_ge(sV, m["ztot"][(ck, 7)])
                t.transpose(psM[0:8, 8:136], rzAll[:, :], s_id128[:, :]).then_inc(sT, 1)
            P.tsec(t_rz)
            trz = P.inc("sT")

            def s_rz(s, trz=trz):
                s.wait_ge(sT, trz)
                s.activation(rzT_sb[:, :], psM[0:8, 8:136], AF.Identity).then_inc(sS, 1)
            P.ssec(s_rz)
            m["rzt"][ck] = srz = P.inc("sS")

            def v_cprep(v, ck=ck, srz=srz):
                v.wait_ge(sS, srz)
                v.tensor_tensor(R_sb[:, :], eATr[:, :], rzT_sb[:, :], ALU.mult)
                for h in range(8):
                    v.tensor_copy(Mlast[:, h:h + 1], Mh[h][:, 127:128])
                v.tensor_scalar(diagE2[:, :], s_id8[:, :], eATr[:, 127:128], None, ALU.mult)
                v.drain().then_inc(sV, 1)
            P.vsec(v_cprep)
            m["cprep"][ck] = P.inc("sV")

            for h in range(8):
                bz = h % 2

                def t_bz(t, h=h, bz=bz, ck=ck):
                    if h <= 1:
                        t.wait_ge(sS, m["rzt"][ck])
                    if (ck, h - 2) in m["msc"]:
                        t.wait_ge(sV, m["msc"][(ck, h - 2)])
                    t.matmul(psQZ[bz][:, 128:256], s_ds[:, h * 128:(h + 1) * 128],
                             rzT_sb[:, :], start=True, stop=True).then_inc(sT, 1)
                P.tsec(t_bz)
                tbz = P.inc("sT")

                def v_msc(v, h=h, bz=bz, tbz=tbz, ck=ck):
                    v.wait_ge(sT, tbz)
                    if (ck, h - 1) in m["ymm"]:
                        v.wait_ge(sT, m["ymm"][(ck, h - 1)])
                    v.tensor_tensor(MS_bf[:, :], Mh[h][:, :], psQZ[bz][:, 128:256], ALU.mult).then_inc(sV, 1)
                P.vsec(v_msc)
                m["msc"][(ck, h)] = P.inc("sV")

                def t_ym(t, h=h, ck=ck):
                    t.wait_ge(sV, m["msc"][(ck, h)])
                    if h == 0:
                        t.wait_ge(sS, m["cev"].get(ck - 1, 0))
                        t.wait_ge(sS, m["yev"].get(ck - 1, 0))
                    for dc in range(NCH):
                        i = t.matmul(psY[:, dc * 128:(dc + 1) * 128],
                                     v_bf[:, dc * 128:(dc + 1) * 128],
                                     MS_bf[:, :], start=(h == 0 and dc % 4 == 0),
                                     stop=False)
                    i.then_inc(sT, 1)
                P.tsec(t_ym)
                m["ymm"][(ck, h)] = P.inc("sT")

            def t_iy(t, ck=ck):
                t.wait_ge(sV, m["cprep"][ck])
                t.wait_ge(sS, m["cev"].get(ck - 1, 0))
                for dc in range(NCH):
                    i = t.matmul(psY[:, dc * 128:(dc + 1) * 128],
                                 S_in[:, dc * 128:(dc + 1) * 128],
                                 R_sb[:, :], start=False, stop=(dc % 4 == 3))
                i.then_inc(sT, 1)
            P.tsec(t_iy)
            m["intery"][ck] = tiy = P.inc("sT")

            def s_yev(s, g=g, tl=tl, ck=ck, pb=pb, tiy=tiy):
                s.wait_ge(sT, tiy)
                if g >= 2 and tl == 0:
                    s.wait_ge(sT, m["wo"][(g - 2, NCH - 1)])
                s.activation(
                    yTg[pb][:, :].rearrange("p (dc i) -> p dc i", dc=NCH)[:, :, tl * 128:(tl + 1) * 128],
                    psY[:, :].rearrange("p (dc i) -> p dc i", dc=NCH),
                    AF.Identity).then_inc(sS, 1)
            P.ssec(s_yev)
            m["yev"][ck] = P.inc("sS")

            def t_carry(t, ck=ck):
                t.wait_ge(sS, m["yev"][ck])
                t.wait_ge(sV, m["cprep"][ck])
                t.matmul(psY[0:8, 0:512], Mlast[:, :], v_bf[:, 0:512], start=True, stop=False)
                t.matmul(psY[0:8, 0:512], diagE2[:, :], S_in[:, 0:512], start=False, stop=True)
                t.matmul(psY[0:8, 512:1024], Mlast[:, :], v_bf[:, 512:1024], start=True, stop=False)
                t.matmul(psY[0:8, 512:1024], diagE2[:, :], S_in[:, 512:1024], start=False, stop=True)
                t.matmul(psM[0:8, 137:138], Mlast[:, :], ones_col[:, :], start=True, stop=False)
                i = t.matmul(psM[0:8, 137:138], diagE2[:, :], Zin_col[:, :], start=False, stop=True)
                i.then_inc(sT, 1)
            P.tsec(t_carry)
            m["carry"][ck] = P.inc("sT")

            def s_cev(s, ck=ck):
                s.wait_ge(sT, m["carry"][ck])
                s.activation(S_in[:, :], psY[0:8, 0:1024], AF.Identity)
                s.activation(Zin_col[:, :], psM[0:8, 137:138], AF.Identity).then_inc(sS, 1)
            P.ssec(s_cev)
            m["cev"][ck] = P.inc("sS")

            def v_zb(v, ck=ck):
                v.wait_ge(sS, m["cev"][ck])
                v.tensor_scalar(diagZ[:, :], s_id8[:, :], Zin_col[:, 0:1], None, ALU.mult).then_inc(sV, 1)
            P.vsec(v_zb)
            vzb = P.inc("sV")

            def t_zb(t, ck=ck, vzb=vzb):
                t.wait_ge(sV, vzb)
                t.matmul(psM[:, 0:8], ones8[:, :], diagZ[:, :], start=True, stop=True).then_inc(sT, 1)
            P.tsec(t_zb)
            tzb = P.inc("sT")

            def s_zb(s, ck=ck, tzb=tzb):
                s.wait_ge(sT, tzb)
                s.activation(Zbc[:, :], psM[:, 0:8], AF.Identity).then_inc(sS, 1)
            P.ssec(s_zb)
            m["zbev"][ck] = P.inc("sS")

        for oc in range(NCH):
            op = oc % 2

            def t_o(t, g=g, oc=oc, pb=pb):
                t.wait_ge(sS, m["yev"][g * 4 + 3])
                if oc == 0:
                    t.wait_ge(sS, m["vev"][(g * 4 + 3, 1)])
                else:
                    t.wait_ge(sS, m["oev"][(g, oc - 1)])
                for dc in range(NCH):
                    i = t.matmul(psB[:, 0:512],
                                 swo[:, dc * 1024 + oc * 128: dc * 1024 + oc * 128 + 128],
                                 yTg[pb][:, dc * G:(dc + 1) * G],
                                 start=(dc == 0), stop=(dc == NCH - 1))
                i.then_inc(sT, 1)
            P.tsec(t_o)
            m["wo"][(g, oc)] = P.inc("sT")

            def s_oev(s, g=g, oc=oc, op=op):
                s.wait_ge(sT, m["wo"][(g, oc)])
                if (g, oc - 2) in m["xadd"]:
                    s.wait_ge(sV, m["xadd"][(g, oc - 2)])
                elif g >= 1:
                    s.wait_ge(sV, m["xadd"][(g - 1, NCH - 2 + oc)])
                s.activation(tO[op][:, :], psB[:, 0:512], AF.Identity).then_inc(sS, 1)
            P.ssec(s_oev)
            m["oev"][(g, oc)] = P.inc("sS")

            def v_add(v, g=g, oc=oc, op=op, pb=pb):
                v.wait_ge(sS, m["oev"][(g, oc)])
                v.tensor_tensor(xt[pb][:, _chsl(oc)], xt[pb][:, _chsl(oc)],
                                tO[op][:, :], ALU.add).then_inc(sV, 1)
            P.vsec(v_add)
            m["xadd"][(g, oc)] = P.inc("sV")

        def g_store(g_, g=g, pb=pb):
            g_.wait_ge(sV, m["xadd"][(g, NCH - 1)])
            g_.dma_start(xout[:, :, g * G:(g + 1) * G].rearrange("c p t -> p c t"),
                         xt[pb][:, :].rearrange("p (c t) -> p c t", c=NCH)
                         ).then_inc(qS[g % 2], 16)
        P.gsec(g_store)
        store_done[g] = (qS[g % 2], P.inc("qS%d" % (g % 2), 16))

    def g_fin(g_):
        g_.wait_ge(qS[0], P.cnt["qS0"])
        g_.wait_ge(qS[1], P.cnt["qS1"])
    P.gsec(g_fin)


    P.pop_scope()
    P.emit_barrier()


def build_ssm_p2():
    key = "ssm2"
    if key in _PROG_CACHE:
        return _PROG_CACHE[key]
    P = Prog(key)
    nc = P.nc
    ext = {}
    ext["xin"] = nc.dram_tensor("xin", [NCH, 128, TPC], F32, kind="ExternalInput")
    ext["abT_in"] = nc.dram_tensor("abT_in", [16, TPC], F32, kind="ExternalInput")
    ext["slocall"] = nc.dram_tensor("slocall", [4, 8, 1026], F32, kind="ExternalInput")
    ext["psel"] = nc.dram_tensor("psel", [8, 3], F32, kind="ExternalInput")
    ext["wvT"] = nc.dram_tensor("wvT", [DD, DD], F32, kind="ExternalInput")
    ext["woT"] = nc.dram_tensor("woT", [DD, DD], F32, kind="ExternalInput")
    ext["cut"] = nc.dram_tensor("cut", [128, 128], F32, kind="ExternalInput")
    ext["cutb"] = nc.dram_tensor("cutb", [128, 128], BF16, kind="ExternalInput")
    ext["cdsel"] = nc.dram_tensor("cdsel", [8, 1024], F32, kind="ExternalInput")
    ext["cid8"] = nc.dram_tensor("cid8", [8, 8], F32, kind="ExternalInput")
    ext["cid128"] = nc.dram_tensor("cid128", [128, 128], F32, kind="ExternalInput")
    ext["cid128b"] = nc.dram_tensor("cid128b", [128, 128], BF16, kind="ExternalInput")
    ext["clneg"] = nc.dram_tensor("clneg", [128, 128], BF16, kind="ExternalInput")
    ext["xout"] = nc.dram_tensor("xout", [NCH, 128, TPC], F32, kind="ExternalOutput")
    P.std_sems()
    P.open()
    emit_ssm_p2(P, ext)
    P.close()
    _PROG_CACHE[key] = P
    return P


def run_ssm(xs, W_q, W_k, W_v, W_alpha_w, W_alpha_b, W_o, C):
    P1 = build_ssm_p1()
    com = {
        "wqT": np.ascontiguousarray(W_q.T), "wkT": np.ascontiguousarray(W_k.T),
        "waT": np.ascontiguousarray(W_alpha_w.T), "wvT": np.ascontiguousarray(W_v.T),
        "abt": np.ascontiguousarray(W_alpha_b.reshape(4, 128).T),
        "cut": C["ut_f32"], "chs": C["hsel"], "cnhs": C["nhsel"],
        "cid8": np.ascontiguousarray(C["ident_f32"][0:8, 0:8]),
        "cid8b": np.eye(8, dtype=ml_dtypes.bfloat16),
    }
    in_maps = [dict(xin=xs[c], **com) for c in range(NCORE)]
    res = run_bass_kernel_spmd(P1.nc, in_maps, core_ids=list(range(NCORE)))
    slocs = [np.asarray(res.results[c]["sloc"]) for c in range(NCORE)]
    abTs = [res.results[c]["abT_out"] for c in range(NCORE)]

    P2 = build_ssm_p2()
    sl_b = [np.stack(slocs[0:4]), np.stack(slocs[4:8])]
    com2 = {
        "wvT": com["wvT"], "woT": np.ascontiguousarray(W_o.T),
        "cut": C["ut_f32"], "cutb": C["ut_bf"], "cdsel": C["dsel"],
        "cid8": com["cid8"], "cid128": C["ident_f32"],
        "cid128b": C["ident_f32"].astype(ml_dtypes.bfloat16),
        "clneg": C["lneg_bf"],
    }
    in_maps = []
    for c in range(NCORE):
        mchunk = c % 4
        sel = np.zeros((8, 3), np.float32)
        for k in range(3):
            if k < mchunk:
                sel[:, k] = 1.0
        in_maps.append(dict(xin=xs[c], abT_in=abTs[c], slocall=sl_b[c // 4],
                            psel=sel, **com2))
    res = run_bass_kernel_spmd(P2.nc, in_maps, core_ids=list(range(NCORE)))
    return [res.results[c]["xout"] for c in range(NCORE)]


def l0_host_consts(C, inp_tabs=None):
    hc = {}
    hc["cm0"] = C["mask_b0"].astype(ml_dtypes.bfloat16)
    hc["cm4"] = C["mask_b4"].astype(ml_dtypes.bfloat16)
    hc["cid128b"] = C["ident_f32"].astype(ml_dtypes.bfloat16)
    hc["chs"] = C["hsel"]
    chsT = np.zeros((8, 512), np.float32)
    for oc in range(4):
        chsT[:, oc * 128:(oc + 1) * 128] = C["hsel"][:, oc * 8:(oc + 1) * 8].T
    hc["chsT"] = np.ascontiguousarray(chsT)
    hc["crot"] = C["rot_lhsT_bf"]
    return hc


def l0_smask(C, chunk0):
    sm = np.zeros((128, 16 * 128), np.float32)
    for m_ in range(4):
        for b in range(4):
            blk = sm[:, (m_ * 4 + b) * 128:(m_ * 4 + b + 1) * 128]
            if chunk0 and (m_ - 4 + b) < 0:
                blk[:] = NEG
            elif b == 0:
                blk[:] = C["mask_b0"]
    return sm.astype(ml_dtypes.bfloat16)


def l0_tables(C, c):
    t0 = (c % 4) * TPC
    out = {}
    for nm, key in [("tqc", "tab_qcos"), ("tqs", "tab_qsin"),
                    ("tkc", "tab_kcos"), ("tks", "tab_ksin")]:
        full = C[key]
        lo = t0 - 512
        if lo < 0:
            sl = np.concatenate([np.zeros((128, -lo), full.dtype), full[:, 0:t0 + TPC]], axis=1)
        else:
            sl = full[:, lo:t0 + TPC]
        out[nm] = np.ascontiguousarray(sl)
    return out


def _halo_slices(xs):
    halos = []
    for c in range(NCORE):
        if c % 4 == 0:
            halos.append(np.zeros((NCH, 128, 513), np.float32))
        else:
            halos.append(np.ascontiguousarray(np.asarray(xs[c - 1])[:, :, TPC - 513:TPC]))
    return halos


def _halo_col(xs):
    cols = []
    for c in range(NCORE):
        if c % 4 == 0:
            cols.append(np.zeros((NCH, 128, 1), np.float32))
        else:
            cols.append(np.ascontiguousarray(np.asarray(xs[c - 1])[:, :, TPC - 1:TPC]))
    return cols


def run_l0attn(l, xs, inp, C):
    P = build_l0attn(l)
    hc = l0_host_consts(C)
    wqkv = np.ascontiguousarray(inp["L0_ffn_wqkv"][l])
    wo = np.ascontiguousarray(inp["L0_ffn_wo"][l])
    qsc = np.ascontiguousarray(np.tile(inp["L0_ffn_q_scale"][l], 2)[:, None])
    ksc = np.ascontiguousarray(np.tile(inp["L0_ffn_k_scale"][l], 2)[:, None])
    halos = _halo_slices(xs)
    in_maps = []
    for c in range(NCORE):
        m_ = dict(xin=np.asarray(xs[c]), xhalo=halos[c], wqkv=wqkv, wo=wo,
                  qscol=qsc, kscol=ksc, smask=l0_smask(C, c % 4 == 0), **hc,
                  **l0_tables(C, c))
        in_maps.append(m_)
    res = run_bass_kernel_spmd(P.nc, in_maps, core_ids=list(range(NCORE)))
    return [res.results[c]["xout"] for c in range(NCORE)]


def _l0ffn_bias_tables(inp, l):
    b1 = inp["L0_ffn_b1"][l]
    b1a = np.zeros((128, 22), np.float32)
    b1g = np.zeros((128, 22), np.float32)
    for half in range(2):
        for c, (off, w) in enumerate(_hchunks(half)):
            b1a[0:w, half * 11 + c] = b1[off:off + w]
            b1g[0:w, half * 11 + c] = b1[HID + off: HID + off + w]
    lgt = np.ascontiguousarray(inp["L0_ffn_ln_g"][l].reshape(8, 128).T)
    lbt = np.ascontiguousarray(inp["L0_ffn_ln_b"][l].reshape(8, 128).T)
    b2t = np.ascontiguousarray(inp["L0_ffn_b2"][l].reshape(8, 128).T)
    return b1a, b1g, b2t, lgt, lbt


def run_l0ffn(l, final, xs, inp, C, xres=None):
    P = build_l0ffn(final)
    w1 = np.ascontiguousarray(inp["L0_ffn_w1"][l])
    w2 = np.ascontiguousarray(inp["L0_ffn_w2"][l])
    b1a, b1g, b2t, lgt, lbt = _l0ffn_bias_tables(inp, l)
    cols = _halo_col(xs)
    in_maps = []
    for c in range(NCORE):
        m_ = dict(xin=np.asarray(xs[c]), xh1=cols[c], w1=w1, w2=w2,
                  b1at=b1a, b1gt=b1g, b2t=b2t, lgt=lgt, lbt=lbt)
        if final:
            m_["xres"] = np.asarray(xres[c])
        in_maps.append(m_)
    res = run_bass_kernel_spmd(P.nc, in_maps, core_ids=list(range(NCORE)))
    return [res.results[c]["xout"] for c in range(NCORE)]


def run_l0call(xs, inp, C):
    x1_ = run_l0attn(0, xs, inp, C)
    x2_ = run_l0ffn(0, False, x1_, inp, C)
    x3_ = run_l0attn(1, x2_, inp, C)
    return run_l0ffn(1, True, x3_, inp, C, xres=xs)


def build_fused():
    key = "fused"
    if key in _PROG_CACHE:
        return _PROG_CACHE[key]
    P = Prog(key)
    nc = P.nc
    E = {}

    def ein(nm, shape, dt):
        E[nm] = nc.dram_tensor(nm, shape, dt, kind="ExternalInput")
        return E[nm]

    x0in = ein("x0in", [NCH, 128, TPC], F32)
    yout = nc.dram_tensor("yout", [NCH, 128, TPC], F32, kind="ExternalOutput")
    for l in range(2):
        ein(f"wqkv{l}", [DD, 1536], F32)
        ein(f"wo{l}", [512, DD], F32)
        ein(f"qscol{l}", [128, 1], F32)
        ein(f"kscol{l}", [128, 1], F32)
        ein(f"w1f{l}", [DD, 2 * HID], F32)
        ein(f"w2f{l}", [HID, DD], F32)
        ein(f"b1at{l}", [128, 22], F32)
        ein(f"b1gt{l}", [128, 22], F32)
        ein(f"b2tf{l}", [128, 8], F32)
        ein(f"lgt{l}", [128, 8], F32)
        ein(f"lbt{l}", [128, 8], F32)
    for nm in ("tqc", "tqs", "tkc", "tks"):
        ein(nm, [128, 4608], BF16)
    ein("smask", [128, 16 * 128], BF16)
    ein("cm0", [128, 128], BF16)
    ein("cm4", [128, 128], BF16)
    ein("cid128b", [128, 128], BF16)
    ein("chs", [128, 32], F32)
    ein("chsT", [8, 512], F32)
    ein("crot", [128, 128], BF16)
    ein("wqTs", [DD, 512], F32)
    ein("wkTs", [DD, 512], F32)
    ein("waTs", [DD, 512], F32)
    ein("wvTs", [DD, DD], F32)
    ein("abt", [128, 4], F32)
    ein("cut", [128, 128], F32)
    ein("cutb", [128, 128], BF16)
    ein("cnhs", [128, 32], F32)
    ein("cid8", [8, 8], F32)
    ein("cid8b", [8, 8], BF16)
    ein("cdsel", [8, 1024], F32)
    ein("cid128", [128, 128], F32)
    ein("clneg", [128, 128], BF16)
    ein("woTs", [DD, DD], F32)
    ein("psel", [8, 3], F32)
    ein("prevrank", [1, 1], mybir.dt.int32)
    ein("w1l1", [DD, 4096], F32)
    ein("w2l1", [4096, DD], F32)
    ein("b1tl1", [128, 32], F32)
    ein("b2tl1", [128, 8], F32)
    ein("wqTa", [DD, DD], F32)
    ein("wkTa", [DD, DD], F32)
    ein("wvTa", [DD, DD], F32)
    ein("woTa", [DD, DD], F32)

    xA = nc.dram_tensor("xA", [NCH, 128, TPC], F32)
    xB = nc.dram_tensor("xB", [NCH, 128, TPC], F32)
    xC = nc.dram_tensor("xC", [NCH, 128, TPC], F32)
    hsrc = nc.dram_tensor("hsrc", [1024, SW], F32)
    hall = nc.dram_tensor("hall", [5 * 1024, SW], F32)
    hcsrc = nc.dram_tensor("hcsrc", [1024, 1], F32)
    hcall = nc.dram_tensor("hcall", [5 * 1024, 1], F32)
    xhalo_d = nc.dram_tensor("xhalo_d", [NCH, 128, SW], F32)
    xh1_d = nc.dram_tensor("xh1_d", [NCH, 128, 1], F32)
    sloc_i = nc.dram_tensor("sloc_i", [8, 1026], F32)
    slall = nc.dram_tensor("slall", [4, 8, 1026], F32)
    abT_i = nc.dram_tensor("abT_i", [16, TPC], F32)
    qt_i = nc.dram_tensor("qt_i", [NCH, 128, TPC], BF16)
    kvp_i = nc.dram_tensor("kvp_i", [64, 1024], F32)
    kvr_i = nc.dram_tensor("kvr_i", [64, 1024], F32)

    P.std_sems()
    sC = P.sem("sC")
    GROUPS = [[0, 1, 2, 3], [4, 5, 6, 7]]
    P.open()
    qW = P.handles["qW"]
    sG = P.handles["sG"]

    P.push_scope()
    rk_t = P.sb("rk", [1, 1], mybir.dt.int32)
    zt = P.sb("zt", [128, NCH * SW], F32)
    rk = nc.gpsimd.alloc_register("rkreg")
    roff = nc.gpsimd.alloc_register("roffreg")

    def g_pre(g_):
        g_.dma_start(rk_t[:, :], E["prevrank"][:, :]).then_inc(qW, 16)
        g_.memset(zt[:, :], 0.0)
        g_.wait_ge(qW, P.cnt["qW"] + 16)
        g_.reg_load(rk, rk_t[0:1, 0:1])
        g_.dma_start(hall[4 * 1024:5 * 1024, :].rearrange("(c p) t -> p c t", p=128),
                     zt[:, :].rearrange("p (c t) -> p c t", c=NCH)).then_inc(qW, 16)
        g_.dma_start(hcall[4 * 1024:5 * 1024, :].rearrange("(c p) t -> p c t", p=128),
                     zt[:, 0:NCH].rearrange("p (c t) -> p c t", c=NCH),
                     allow_slow_non_contiguous=True).then_inc(qW, 16)
        g_.wait_ge(qW, P.cnt["qW"] + 48)
    P.gsec(g_pre)
    P.inc("qW", 48)
    P.pop_scope()
    P.emit_barrier()

    def halo_ex(src, width):

        def g_h(g_):
            g_.dma_start(hsrc[:, :].rearrange("(c p) t -> c p t", p=128),
                         src[:, :, TPC - SW:TPC]).then_inc(qW, 16)
            g_.wait_ge(qW, P.cnt["qW"] + 16)
            g_.collective_compute(
                "AllGather", mybir.AluOpType.bypass, GROUPS,
                ins=[hsrc.ap().opt()], outs=[hall[0:4 * 1024, :].opt()],
            ).then_inc(sC, 1)
            g_.wait_ge(sC, P.cnt["sC"] + 1)
            g_.reg_mov(roff, rk)
            g_.reg_mul(roff, roff, 1024 * SW)
            g_.dma_start(
                xhalo_d[:, :, :],
                bass.AP(hall, roff, [[128 * SW, NCH], [SW, 128], [1, SW]]),
            ).then_inc(qW, 16)
            g_.wait_ge(qW, P.cnt["qW"] + 32)
            if width == 1:
                g_.dma_start(xh1_d[:, :, :], xhalo_d[:, :, SW - 1:SW],
                             allow_slow_non_contiguous=True).then_inc(qW, 16)
                g_.wait_ge(qW, P.cnt["qW"] + 48)
        P.gsec(g_h)
        P.inc("qW", 32 + (16 if width == 1 else 0))
        P.inc("sC", 1)

        tgt = P.cnt["qW"]

        def y_h(y, tgt=tgt):
            y.wait_ge(qW, tgt)
        P.ysec(y_h)

    lw = {}
    for l in range(2):
        lw[l] = dict(wqkv=E[f"wqkv{l}"], wo=E[f"wo{l}"], qscol=E[f"qscol{l}"],
                     kscol=E[f"kscol{l}"],
                     tqc=E["tqc"], tqs=E["tqs"], tkc=E["tkc"], tks=E["tks"],
                     smask=E["smask"], cm0=E["cm0"], cm4=E["cm4"],
                     cid128b=E["cid128b"], chs=E["chs"], chsT=E["chsT"],
                     crot=E["crot"])
    fw = {}
    for l in range(2):
        fw[l] = dict(w1=E[f"w1f{l}"], w2=E[f"w2f{l}"], b1at=E[f"b1at{l}"],
                     b1gt=E[f"b1gt{l}"], b2t=E[f"b2tf{l}"], lgt=E[f"lgt{l}"],
                     lbt=E[f"lbt{l}"])

    def l0call(xres, t1, t2):
        halo_ex(xres, SW)
        emit_l0attn(P, dict(xin=xres, xhalo=xhalo_d, xout=t1, **lw[0]), 0)
        halo_ex(t1, 1)
        emit_l0ffn(P, dict(xin=t1, xh1=xh1_d, xout=t2, **fw[0]), False)
        halo_ex(t2, SW)
        emit_l0attn(P, dict(xin=t2, xhalo=xhalo_d, xout=t1, **lw[1]), 1)
        halo_ex(t1, 1)
        emit_l0ffn(P, dict(xin=t1, xh1=xh1_d, xres=xres, xout=t2, **fw[1]), True)
        return t2

    xa = l0call(x0in, xA, xB)
    emit_ssm_p1(P, dict(xin=xa, wqT=E["wqTs"], wkT=E["wkTs"], waT=E["waTs"],
                        wvT=E["wvTs"], abt=E["abt"], cut=E["cut"], chs=E["chs"],
                        cnhs=E["cnhs"], cid8=E["cid8"], cid8b=E["cid8b"],
                        sloc=sloc_i, abT_out=abT_i))

    def g_sl(g_):
        g_.collective_compute(
            "AllGather", mybir.AluOpType.bypass, GROUPS,
            ins=[sloc_i.ap().opt()],
            outs=[slall[:, :, :].rearrange("a p n -> (a p) n").opt()],
        ).then_inc(sC, 1)
        g_.wait_ge(sC, P.cnt["sC"] + 1)
    P.gsec(g_sl)
    P.inc("sC", 1)

    emit_ssm_p2(P, dict(xin=xa, abT_in=abT_i, slocall=slall, psel=E["psel"],
                        wvT=E["wvTs"], woT=E["woTs"],
                        cut=E["cut"], cutb=E["cutb"], cdsel=E["cdsel"],
                        cid8=E["cid8"], cid128=E["cid128"], cid128b=E["cid128b"],
                        clneg=E["clneg"], xout=xC))
    xc = l0call(xC, xA, xB)
    emit_l1ffn(P, dict(xin=xc, w1=E["w1l1"], w2=E["w2l1"], b1t=E["b1tl1"],
                       b2t=E["b2tl1"], xout=xA))
    emit_l1attn_p1(P, dict(xin=xA, wqT=E["wqTa"], wkT=E["wkTa"], wvT=E["wvTa"],
                           qt_out=qt_i, kvp=kvp_i))

    def g_kv(g_):
        g_.collective_compute(
            "AllReduce", mybir.AluOpType.add, GROUPS,
            ins=[kvp_i.ap().opt()], outs=[kvr_i.ap().opt()],
        ).then_inc(sC, 1)
        g_.wait_ge(sC, P.cnt["sC"] + 1)
    P.gsec(g_kv)
    P.inc("sC", 1)

    emit_l1attn_p2(P, dict(xin=xA, qt_in=qt_i, kvp4=kvr_i, woT=E["woTa"], xout=xC),
                   fused_kv=True)
    emit_l1ffn(P, dict(xin=xC, w1=E["w1l1"], w2=E["w2l1"], b1t=E["b1tl1"],
                       b2t=E["b2tl1"], xout=yout))
    P.close()
    _PROG_CACHE[key] = P
    return P


def build_fused_mini():
    key = "fusedmini"
    if key in _PROG_CACHE:
        return _PROG_CACHE[key]
    P = Prog(key)
    nc = P.nc
    E = {}

    def ein(nm, shape, dt):
        E[nm] = nc.dram_tensor(nm, shape, dt, kind="ExternalInput")
        return E[nm]

    x0in = ein("x0in", [NCH, 128, TPC], F32)
    yout = nc.dram_tensor("yout", [NCH, 128, TPC], F32, kind="ExternalOutput")
    ein("wqkv0", [DD, 1536], F32); ein("wo0", [512, DD], F32)
    ein("qscol0", [128, 1], F32); ein("kscol0", [128, 1], F32)
    for nm in ("tqc", "tqs", "tkc", "tks"):
        ein(nm, [128, 4608], BF16)
    ein("smask", [128, 16 * 128], BF16)
    ein("cm0", [128, 128], BF16); ein("cm4", [128, 128], BF16)
    ein("cid128b", [128, 128], BF16); ein("chs", [128, 32], F32)
    ein("chsT", [8, 512], F32); ein("crot", [128, 128], BF16)
    ein("prevrank", [1, 1], mybir.dt.int32)

    hsrc = nc.dram_tensor("hsrc", [1024, SW], F32)
    hall = nc.dram_tensor("hall", [5 * 1024, SW], F32)
    xhalo_d = nc.dram_tensor("xhalo_d", [NCH, 128, SW], F32)
    xhalo_dbg = nc.dram_tensor("xhalo_dbg", [NCH, 128, SW], F32, kind="ExternalOutput")

    P.std_sems()
    sC = P.sem("sC")
    GROUPS = [[0, 1, 2, 3], [4, 5, 6, 7]]
    P.open()
    qW = P.handles["qW"]

    P.push_scope()
    rk_t = P.sb("rk", [1, 1], mybir.dt.int32)
    zt = P.sb("zt", [128, NCH * SW], F32)
    rk = nc.gpsimd.alloc_register("rkreg")
    roff = nc.gpsimd.alloc_register("roffreg")

    def g_pre(g_):
        g_.dma_start(rk_t[:, :], E["prevrank"][:, :]).then_inc(qW, 16)
        g_.memset(zt[:, :], 0.0)
        g_.wait_ge(qW, P.cnt["qW"] + 16)
        g_.reg_load(rk, rk_t[0:1, 0:1])
        g_.dma_start(hall[4 * 1024:5 * 1024, :].rearrange("(c p) t -> p c t", p=128),
                     zt[:, :].rearrange("p (c t) -> p c t", c=NCH)).then_inc(qW, 16)
        g_.wait_ge(qW, P.cnt["qW"] + 32)
    P.gsec(g_pre)
    P.inc("qW", 32)
    P.pop_scope()
    P.emit_barrier()

    def g_h(g_):
        g_.dma_start(hsrc[:, :].rearrange("(c p) t -> c p t", p=128),
                     x0in[:, :, TPC - SW:TPC]).then_inc(qW, 16)
        g_.wait_ge(qW, P.cnt["qW"] + 16)
        g_.collective_compute(
            "AllGather", mybir.AluOpType.bypass, GROUPS,
            ins=[hsrc.ap().opt()], outs=[hall[0:4 * 1024, :].opt()],
        ).then_inc(sC, 1)
        g_.wait_ge(sC, P.cnt["sC"] + 1)
        g_.reg_mov(roff, rk)
        g_.reg_mul(roff, roff, 1024 * SW)
        g_.dma_start(xhalo_d[:, :, :],
                     bass.AP(hall, roff, [[128 * SW, NCH], [SW, 128], [1, SW]])
                     ).then_inc(qW, 16)
        g_.wait_ge(qW, P.cnt["qW"] + 32)
        g_.dma_start(xhalo_dbg[:, :, :], xhalo_d[:, :, :]).then_inc(qW, 16)
        g_.wait_ge(qW, P.cnt["qW"] + 48)
    P.gsec(g_h)
    P.inc("qW", 48)
    P.inc("sC", 1)

    lw0 = dict(wqkv=E["wqkv0"], wo=E["wo0"], qscol=E["qscol0"], kscol=E["kscol0"],
               tqc=E["tqc"], tqs=E["tqs"], tkc=E["tkc"], tks=E["tks"],
               smask=E["smask"], cm0=E["cm0"], cm4=E["cm4"],
               cid128b=E["cid128b"], chs=E["chs"], chsT=E["chsT"], crot=E["crot"])
    ein("w1f0", [DD, 2 * HID], F32)
    ein("w2f0", [HID, DD], F32)
    ein("b1at0", [128, 22], F32)
    ein("b1gt0", [128, 22], F32)
    ein("b2tf0", [128, 8], F32)
    ein("lgt0", [128, 8], F32)
    ein("lbt0", [128, 8], F32)
    xA = nc.dram_tensor("xA", [NCH, 128, TPC], F32)
    hcsrc = nc.dram_tensor("hcsrc", [1024, 1], F32)
    hcall = nc.dram_tensor("hcall", [5 * 1024, 1], F32)
    xh1_d = nc.dram_tensor("xh1_d", [NCH, 128, 1], F32)
    emit_l0attn(P, dict(xin=x0in, xhalo=xhalo_d, xout=xA, **lw0), 0)

    def g_z2(g_):
        g_.dma_start(hcall[4 * 1024:5 * 1024, :].rearrange("(c p) t -> p c t", p=128),
                     x0in[:, :, 0:1].rearrange("c p t -> p c t"),
                     allow_slow_non_contiguous=True).then_inc(qW, 16)
        g_.wait_ge(qW, P.cnt["qW"] + 16)
    P.gsec(g_z2)
    P.inc("qW", 16)

    def g_h1(g_):
        g_.dma_start(hcsrc[:, :].rearrange("(c p) t -> c p t", p=128),
                     xA[:, :, TPC - 1:TPC],
                     allow_slow_non_contiguous=True).then_inc(qW, 16)
        g_.wait_ge(qW, P.cnt["qW"] + 16)
        g_.collective_compute(
            "AllGather", mybir.AluOpType.bypass, GROUPS,
            ins=[hcsrc.ap().opt()], outs=[hcall[0:4 * 1024, :].opt()],
        ).then_inc(sC, 1)
        g_.wait_ge(sC, P.cnt["sC"] + 1)
        g_.reg_mov(roff, rk)
        g_.reg_mul(roff, roff, 1024)
        g_.dma_start(xh1_d[:, :, :],
                     bass.AP(hcall, roff, [[128, NCH], [1, 128], [1, 1]]),
                     allow_slow_non_contiguous=True).then_inc(qW, 16)
        g_.wait_ge(qW, P.cnt["qW"] + 32)
    P.gsec(g_h1)
    P.inc("qW", 32)
    P.inc("sC", 1)

    fw0 = dict(w1=E["w1f0"], w2=E["w2f0"], b1at=E["b1at0"], b1gt=E["b1gt0"],
               b2t=E["b2tf0"], lgt=E["lgt0"], lbt=E["lbt0"])
    emit_l0ffn(P, dict(xin=xA, xh1=xh1_d, xout=yout, **fw0), False)
    P.close()
    _PROG_CACHE[key] = P
    return P


def run_fused(inputs):
    C = host_constants()
    P = build_fused()
    xs = prep_x(np.ascontiguousarray(inputs["L0_attn_x"], dtype=np.float32))
    com = {}
    for l in range(2):
        com[f"wqkv{l}"] = np.ascontiguousarray(inputs["L0_ffn_wqkv"][l])
        com[f"wo{l}"] = np.ascontiguousarray(inputs["L0_ffn_wo"][l])
        com[f"qscol{l}"] = np.ascontiguousarray(np.tile(inputs["L0_ffn_q_scale"][l], 2)[:, None])
        com[f"kscol{l}"] = np.ascontiguousarray(np.tile(inputs["L0_ffn_k_scale"][l], 2)[:, None])
        com[f"w1f{l}"] = np.ascontiguousarray(inputs["L0_ffn_w1"][l])
        com[f"w2f{l}"] = np.ascontiguousarray(inputs["L0_ffn_w2"][l])
        b1a, b1g, b2t, lgt, lbt = _l0ffn_bias_tables(inputs, l)
        com[f"b1at{l}"] = b1a
        com[f"b1gt{l}"] = b1g
        com[f"b2tf{l}"] = b2t
        com[f"lgt{l}"] = lgt
        com[f"lbt{l}"] = lbt
    hc = l0_host_consts(C)
    com.update(cm0=hc["cm0"], cm4=hc["cm4"], cid128b=hc["cid128b"], chs=hc["chs"],
               chsT=hc["chsT"], crot=hc["crot"])
    com.update(wqTs=np.ascontiguousarray(inputs["L0_attn_W_q"].T),
               wkTs=np.ascontiguousarray(inputs["L0_attn_W_k"].T),
               waTs=np.ascontiguousarray(inputs["L0_attn_W_alpha_w"].T),
               wvTs=np.ascontiguousarray(inputs["L0_attn_W_v"].T),
               woTs=np.ascontiguousarray(inputs["L0_attn_W_o"].T),
               abt=np.ascontiguousarray(inputs["L0_attn_W_alpha_b"].reshape(4, 128).T),
               cut=C["ut_f32"], cutb=C["ut_bf"], cnhs=C["nhsel"],
               cid8=np.ascontiguousarray(C["ident_f32"][0:8, 0:8]),
               cid8b=np.eye(8, dtype=ml_dtypes.bfloat16),
               cdsel=C["dsel"], cid128=C["ident_f32"],
               clneg=C["lneg_bf"])
    com.update(w1l1=np.ascontiguousarray(inputs["L1_ffn_w1"]),
               w2l1=np.ascontiguousarray(inputs["L1_ffn_w2"]),
               b1tl1=np.ascontiguousarray(inputs["L1_ffn_b1"].reshape(32, 128).T),
               b2tl1=np.ascontiguousarray(inputs["L1_ffn_b2"].reshape(8, 128).T),
               wqTa=np.ascontiguousarray(inputs["L1_attn_W_q"].T),
               wkTa=np.ascontiguousarray(inputs["L1_attn_W_k"].T),
               wvTa=np.ascontiguousarray(inputs["L1_attn_W_v"].T),
               woTa=np.ascontiguousarray(inputs["L1_attn_W_o"].T))
    in_maps = []
    for c in range(NCORE):
        mchunk = c % 4
        sel = np.zeros((8, 3), np.float32)
        for k in range(3):
            if k < mchunk:
                sel[:, k] = 1.0
        pr = np.array([[mchunk - 1 if mchunk > 0 else 4]], np.int32)
        m_ = dict(com)
        m_.update(l0_tables(C, c))
        m_.update(x0in=xs[c], smask=l0_smask(C, mchunk == 0), psel=sel, prevrank=pr)
        in_maps.append(m_)
    res = run_bass_kernel_spmd(P.nc, in_maps, core_ids=list(range(NCORE)))
    return gather_x([np.asarray(res.results[c]["yout"]) for c in range(NCORE)])


def prep_x(L0_attn_x):
    xs = []
    for c in range(NCORE):
        b, t0 = c // 4, (c % 4) * TPC
        xs.append(to_fm(np.ascontiguousarray(L0_attn_x[b, t0:t0 + TPC, :])))
    return xs


def gather_x(xs):
    out = np.empty((BB, SS, DD), np.float32)
    for c in range(NCORE):
        b, t0 = c // 4, (c % 4) * TPC
        out[b, t0:t0 + TPC, :] = from_fm(xs[c])
    return out


FUSED = True

def kernel(**inputs):
    inputs = {k: np.asarray(v) for k, v in inputs.items()}
    if FUSED:
        return run_fused(inputs)
    C = host_constants()
    xs = prep_x(np.ascontiguousarray(inputs["L0_attn_x"], dtype=np.float32))
    xs = run_l0call(xs, inputs, C)
    xs = [np.ascontiguousarray(np.asarray(v)) for v in xs]
    xs = run_ssm(xs, inputs["L0_attn_W_q"], inputs["L0_attn_W_k"],
                 inputs["L0_attn_W_v"], inputs["L0_attn_W_alpha_w"],
                 inputs["L0_attn_W_alpha_b"], inputs["L0_attn_W_o"], C)
    xs = [np.ascontiguousarray(np.asarray(v)) for v in xs]
    xs = run_l0call(xs, inputs, C)
    xs = [np.ascontiguousarray(np.asarray(v)) for v in xs]
    xs = run_l1ffn(xs, inputs["L1_ffn_w1"], inputs["L1_ffn_w2"],
                   inputs["L1_ffn_b1"], inputs["L1_ffn_b2"])
    xs = [np.ascontiguousarray(np.asarray(v)) for v in xs]
    xs = run_l1attn(xs, inputs["L1_attn_W_q"], inputs["L1_attn_W_k"],
                    inputs["L1_attn_W_v"], inputs["L1_attn_W_o"])
    xs = [np.ascontiguousarray(np.asarray(v)) for v in xs]
    xs = run_l1ffn(xs, inputs["L1_ffn_w1"], inputs["L1_ffn_w2"],
                   inputs["L1_ffn_b1"], inputs["L1_ffn_b2"])
    xs = [np.asarray(v) for v in xs]
    return gather_x(xs)


# version 11
# speedup vs baseline: 1.3031x; 1.2343x over previous
import contextlib
import math
import numpy as np
import ml_dtypes
import concourse.bass as bass
import concourse.mybir as mybir
from concourse.bass_utils import run_bass_kernel_spmd

F32 = mybir.dt.float32
BF16 = mybir.dt.bfloat16
AF = mybir.ActivationFunctionType
ALU = mybir.AluOpType
AX = mybir.AxisListType

NCORE = 8
BB, SS, DD = 2, 16384, 1024
TPC = 4096
NCH = 8
G = 512
NG = TPC // G
NEG = -3.0e38
SW = 513

_PROG_CACHE = {}


class Prog:

    def __init__(self, name):
        self.nc = bass.Bass("TRN2", target_bir_lowering=False, name=name)
        self.es = contextlib.ExitStack()
        self.block = None
        self.cnt = {}
        self.handles = {}
        self._scopes = []
        self._sid = 0
        self._pend = {"t": [], "v": [], "s": [], "g": [], "y": []}

    def sem(self, name):
        h = self.es.enter_context(self.nc.semaphore(name))
        self.cnt[name] = 0
        self.handles[name] = h
        return h

    def std_sems(self):
        for nm in ("qL0", "qL1", "qS0", "qS1", "qW", "sG", "sT", "sV", "sS", "sB"):
            self.sem(nm)

    def inc(self, name, n=1):
        self.cnt[name] += n
        return self.cnt[name]

    def push_scope(self):
        self._sid += 1
        self._scopes.append(contextlib.ExitStack())

    def pop_scope(self):
        self.flush()
        self._scopes.pop().close()

    def flush(self):
        emitters = {"t": lambda b: self.block.tensor(b),
                    "v": lambda b: self.block.vector(b),
                    "s": lambda b: self.block.scalar(b),
                    "g": lambda b: self.block.gpsimd(b),
                    "y": lambda b: self.block.sync(b)}

        def run_all(eng, bs):
            live = self.cnt
            try:
                for body, snap in bs:
                    self.cnt = snap
                    body(eng)
            finally:
                self.cnt = live

        for k, emit in emitters.items():
            bodies = self._pend[k]
            if bodies:
                self._pend[k] = []
                emit(lambda eng, bs=bodies: run_all(eng, bs))

    def sb(self, name, shape, dt):
        st = self._scopes[-1] if self._scopes else self.es
        nm = f"g{self._sid}_{name}" if self._scopes else name
        return st.enter_context(self.nc.sbuf_tensor(nm, shape, dt))

    def ps(self, name, shape, dt=F32):
        st = self._scopes[-1] if self._scopes else self.es
        nm = f"g{self._sid}_{name}" if self._scopes else name
        return st.enter_context(self.nc.psum_tensor(nm, shape, dt))

    def open(self):
        self.block = self.es.enter_context(self.nc.Block())
        return self.block

    def close(self):
        self.flush()
        self.es.close()

    def emit_barrier(self):
        self.flush()
        sB = self.handles["sB"]
        dma_tgts = [(self.handles[nm], self.cnt[nm])
                    for nm in ("qL0", "qL1", "qS0", "qS1", "qW")]

        def g_b(g_):
            for h, v in dma_tgts:
                g_.wait_ge(h, v)
            g_.engine_nop().then_inc(sB, 1)
        self.gsec(g_b)

        def t_b(t):
            t.nop().then_inc(sB, 1)
        self.tsec(t_b)

        def v_b(v):
            v.drain()
            v.engine_nop().then_inc(sB, 1)
        self.vsec(v_b)

        def s_b(s):
            s.drain()
            s.nop().then_inc(sB, 1)
        self.ssec(s_b)

        def y_b(y):
            y.nop().then_inc(sB, 1)
        self.ysec(y_b)

        tgt = self.inc("sB", 5)

        def g_w(g_):
            g_.wait_ge(sB, tgt)
        self.gsec(g_w)

        def t_w(t):
            t.wait_ge(sB, tgt)
        self.tsec(t_w)

        def v_w(v):
            v.wait_ge(sB, tgt)
        self.vsec(v_w)

        def s_w(s):
            s.wait_ge(sB, tgt)
        self.ssec(s_w)

        def y_w(y):
            y.wait_ge(sB, tgt)
        self.ysec(y_w)

    def tsec(self, body):
        self._pend["t"].append((body, dict(self.cnt)))

    def vsec(self, body):
        self._pend["v"].append((body, dict(self.cnt)))

    def ssec(self, body):
        self._pend["s"].append((body, dict(self.cnt)))

    def gsec(self, body):
        self._pend["g"].append((body, dict(self.cnt)))

    def ysec(self, body):
        self._pend["y"].append((body, dict(self.cnt)))


def _chsl(ch, width=G, base=0):
    return slice(base + ch * width, base + (ch + 1) * width)


def host_constants():
    c = {}
    c["ident_f32"] = np.eye(128, dtype=np.float32)
    ut = np.triu(np.ones((128, 128), np.float32))
    c["ut_f32"] = ut
    c["ut_bf"] = ut.astype(ml_dtypes.bfloat16)
    p = np.arange(128)[:, None]
    r = np.arange(128)[None, :]
    c["mask_b0"] = np.where(r >= p, 0.0, NEG).astype(np.float32)
    c["lneg_bf"] = np.where(p > r, NEG, 0.0).astype(ml_dtypes.bfloat16)
    c["mask_b4"] = np.where(r <= p, 0.0, NEG).astype(np.float32)
    m64 = np.zeros((64, 64), np.float32)
    for d in range(32):
        m64[d, d + 32] = -1.0
        m64[d + 32, d] = 1.0
    m128 = np.zeros((128, 128), np.float32)
    m128[0:64, 0:64] = m64
    m128[64:128, 64:128] = m64
    c["rot_lhsT_bf"] = np.ascontiguousarray(m128.T).astype(ml_dtypes.bfloat16)
    hsel = np.zeros((128, 4 * 8), np.float32)
    for cc in range(4):
        hsel[0:64, cc * 8 + 2 * cc] = 1.0
        hsel[64:128, cc * 8 + 2 * cc + 1] = 1.0
    c["hsel"] = hsel
    c["nhsel"] = -hsel
    dsel = np.zeros((8, 8 * 128), np.float32)
    for h in range(8):
        dsel[h, h * 128:(h + 1) * 128] = 1.0
    c["dsel"] = dsel
    DH, WINDOW, n = 64, 512, 16384
    half = np.arange(0, DH, 2, dtype=np.float64)
    inv_freq = 1.0 / 10000.0 ** (half / DH)
    t = np.arange(n, dtype=np.float64)
    freqs = t[:, None] * inv_freq[None, :]
    freqs = np.concatenate([freqs, freqs], axis=-1)
    base = (half + 0.4 * DH) / (1.4 * DH)
    power = (t - n // 2) / float(WINDOW // 2)
    scale = base[None, :] ** power[:, None]
    scale = np.concatenate([scale, scale], axis=-1)
    cos, sin = np.cos(freqs), np.sin(freqs)
    qc = (cos * scale).T
    qs = (sin * scale).T
    kc = (cos / scale).T
    ks = (sin / scale).T
    for nm, tab in [("tab_qcos", qc), ("tab_qsin", qs), ("tab_kcos", kc), ("tab_ksin", ks)]:
        full = np.concatenate([tab, tab], axis=0).astype(np.float32)
        c[nm] = full.astype(ml_dtypes.bfloat16)
    return c


def to_fm(x_bt_d):
    t = np.ascontiguousarray(x_bt_d.T)
    return t.reshape(NCH, 128, -1)


def from_fm(fm):
    d, t = DD, fm.shape[2]
    return np.ascontiguousarray(fm.reshape(d, t).T)


def emit_ln_group(P, bufs, W, g, soff=None, sstride=None, doff=0, dstride=None,
                  want_uf=False, affine=False, skip_cast=False):
    xt, x_bf, x2_bf, u_bf = bufs["xt"], bufs["x_bf"], bufs["x2_bf"], bufs["u_bf"]
    Bmu, Bvar, Brstd, tA = bufs["Bmu"], bufs["Bvar"], bufs["Brstd"], bufs["tA"]
    tB = bufs["tB"]
    ones_bf, psB = bufs["ones_bf"], bufs["psB"]
    sV, sS, sT = P.handles["sV"], P.handles["sS"], P.handles["sT"]
    if sstride is None:
        sstride = W
    if dstride is None:
        dstride = W
    if soff is None:
        soff = 0
    offf = soff if callable(soff) else (lambda ch: soff)

    def ssl(ch):
        o = ch * sstride + offf(ch)
        return slice(o, o + W)

    if not skip_cast:
        def v_cast(v):
            v.tensor_copy(x_bf[:, 0:NCH * sstride], xt[:, 0:NCH * sstride]).then_inc(sV, 1)
        P.vsec(v_cast)
        vcast = P.inc("sV")

        def s_sq(s):
            s.wait_ge(sV, vcast)
            s.activation(x2_bf[:, 0:NCH * sstride], x_bf[:, 0:NCH * sstride],
                         AF.Square).then_inc(sS, 1)
        P.ssec(s_sq)
        ssq = P.inc("sS")
    else:
        ssq = bufs["cast_ready"]

    if bufs.get("seq_stats"):
        def t_st1(t):
            t.wait_ge(sS, ssq)
            for ch in range(NCH):
                i = t.matmul(psB[:, 0:W], ones_bf[:, :], x_bf[:, ssl(ch)],
                             start=(ch == 0), stop=(ch == NCH - 1))
            i.then_inc(sT, 1)
        P.tsec(t_st1)
        tst1 = P.inc("sT")

        def s_ev1(s):
            s.wait_ge(sT, tst1)
            s.activation(Bmu[:, 0:W], psB[:, 0:W], AF.Identity, scale=1.0 / DD).then_inc(sS, 1)
        P.ssec(s_ev1)
        sev1 = P.inc("sS")

        def t_st2(t):
            t.wait_ge(sS, sev1)
            for ch in range(NCH):
                i = t.matmul(psB[:, 0:W], ones_bf[:, :], x2_bf[:, ssl(ch)],
                             start=(ch == 0), stop=(ch == NCH - 1))
            i.then_inc(sT, 1)
        P.tsec(t_st2)
        tstat = P.inc("sT")

        def s_evac(s):
            s.wait_ge(sT, tstat)
            s.activation(Bvar[:, 0:W], psB[:, 0:W], AF.Identity, scale=1.0 / DD).then_inc(sS, 1)
        P.ssec(s_evac)
        sevac = P.inc("sS")
    else:
        def t_stats(t):
            t.wait_ge(sS, ssq)
            for ch in range(NCH):
                t.matmul(psB[:, 0:W], ones_bf[:, :], x_bf[:, ssl(ch)],
                         start=(ch == 0), stop=(ch == NCH - 1))
            for ch in range(NCH):
                i = t.matmul(psB[:, G:G + W], ones_bf[:, :], x2_bf[:, ssl(ch)],
                             start=(ch == 0), stop=(ch == NCH - 1))
            i.then_inc(sT, 1)
        P.tsec(t_stats)
        tstat = P.inc("sT")

        def s_evac(s):
            s.wait_ge(sT, tstat)
            s.activation(Bmu[:, 0:W], psB[:, 0:W], AF.Identity, scale=1.0 / DD)
            s.activation(Bvar[:, 0:W], psB[:, G:G + W], AF.Identity, scale=1.0 / DD).then_inc(sS, 1)
        P.ssec(s_evac)
        sevac = P.inc("sS")

    def v_var(v):
        v.wait_ge(sS, sevac)
        v.tensor_tensor(tA[:, 0:W], Bmu[:, 0:W], Bmu[:, 0:W], ALU.mult)
        v.drain()
        v.tensor_tensor(Bvar[:, 0:W], Bvar[:, 0:W], tA[:, 0:W], ALU.subtract).then_inc(sV, 1)
    P.vsec(v_var)
    vvar = P.inc("sV")

    def s_sd(s):
        s.wait_ge(sV, vvar)
        s.activation(Bvar[:, 0:W], Bvar[:, 0:W], AF.Sqrt, bias=bufs["epscol"][:, 0:1]).then_inc(sS, 1)
    P.ssec(s_sd)
    ssd = P.inc("sS")

    def v_apply(v):
        v.wait_ge(sS, ssd)
        v.reciprocal(Brstd[:, 0:W], Bvar[:, 0:W])
        v.drain()
        for ch in range(NCH):
            sc = tA if (ch % 2 == 0) else tB
            v.tensor_tensor(sc[:, 0:W], xt[:, ssl(ch)], Bmu[:, 0:W], ALU.subtract)
            v.drain()
            dsl = slice(ch * dstride + doff, ch * dstride + doff + W)
            if affine:
                v.tensor_tensor(sc[:, 0:W], sc[:, 0:W], Brstd[:, 0:W], ALU.mult)
                v.drain()
                i = v.tensor_scalar(u_bf[:, dsl], sc[:, 0:W],
                                    bufs["lg"][:, ch:ch + 1], bufs["lb"][:, ch:ch + 1],
                                    ALU.mult, ALU.add)
                if want_uf:
                    v.drain()
                    i = v.tensor_copy(bufs["uf"][:, ch * W:(ch + 1) * W], u_bf[:, dsl])
            else:
                if want_uf:
                    i = v.tensor_tensor(bufs["uf"][:, ch * W:(ch + 1) * W], sc[:, 0:W],
                                        Brstd[:, 0:W], ALU.mult)
                    v.drain()
                    i = v.tensor_copy(u_bf[:, dsl], bufs["uf"][:, ch * W:(ch + 1) * W])
                else:
                    i = v.tensor_tensor(u_bf[:, dsl], sc[:, 0:W], Brstd[:, 0:W], ALU.mult)
        i.then_inc(sV, 1)
    P.vsec(v_apply)
    return P.inc("sV")


def emit_l1ffn(P, ext):
    nc = P.nc
    xin = ext["xin"]
    w1 = ext["w1"]
    w2 = ext["w2"]
    b1t = ext["b1t"]
    b2t = ext["b2t"]
    xout = ext["xout"]

    HALF = 16

    P.push_scope()
    sw1 = P.sb("sw1", [128, NCH * HALF * 128], BF16)
    sw2 = P.sb("sw2", [128, HALF * 1024], BF16)
    sb1 = P.sb("sb1", [128, 32], F32)
    sb2h = P.sb("sb2h", [128, 8], F32)
    ones_bf = P.sb("ones_bf", [128, 128], BF16)
    xt = [P.sb(f"xt{i}", [128, TPC], F32) for i in range(2)]
    xtln = [P.sb(f"xtln{i}", [128, TPC], F32) for i in range(2)]
    x_bf = P.sb("xbf", [128, TPC], BF16)
    x2_bf = P.sb("x2bf", [128, TPC], BF16)
    u_bf = [P.sb(f"ubf{i}", [128, TPC], BF16) for i in range(2)]
    mid = P.sb("mid", [128, HALF * G], BF16)
    Bmu = P.sb("Bmu", [128, G], F32)
    Bvar = P.sb("Bvar", [128, G], F32)
    Brstd = P.sb("Brstd", [128, G], F32)
    tA = P.sb("tA", [128, G], F32)
    tB = P.sb("tB", [128, G], F32)
    tO = [P.sb(f"tO{i}", [128, G], F32) for i in range(2)]
    epscol = P.sb("epscol", [128, 1], F32)

    psB = P.ps("psB", [128, 2 * G])
    psA = [P.ps(f"psA{i}", [128, G]) for i in range(2)]
    psO = [P.ps(f"psO{i}", [128, G]) for i in range(2)]

    qL = [P.handles["qL0"], P.handles["qL1"]]
    qS = [P.handles["qS0"], P.handles["qS1"]]
    qW = P.handles["qW"]
    sG = P.handles["sG"]
    sT = P.handles["sT"]; sV = P.handles["sV"]; sS = P.handles["sS"]

    def g_pro(g_):
        g_.dma_start(sb1[:, :], b1t[:, :]).then_inc(qW, 16)
        g_.dma_start(sb2h[:, :], b2t[:, :]).then_inc(qW, 16)
        g_.memset(ones_bf[:, :], 1.0)
        g_.memset(epscol[:, :], 1e-5).then_inc(sG, 1)
    P.gsec(g_pro)
    bready = P.inc("qW", 32)
    gready = P.inc("sG")

    load_done = {}
    store_done = {}
    mid_evac = {}
    out_evac = {}
    amm_done = {}
    w2mm_done = {}
    xadd_done = {}
    uapply = {}
    wready = {}

    for half in range(2):
        def g_w(g_, half=half):
            if half == 1:
                g_.wait_ge(P.handles["sT"], w2mm_done[(0, NG - 1, NCH - 1)])
            for a in range(NCH):
                g_.dma_start(sw1[:, a * HALF * 128:(a + 1) * HALF * 128],
                             w1[a * 128:(a + 1) * 128,
                                half * HALF * 128:(half + 1) * HALF * 128]
                             ).then_inc(qW, 16)
            for a4 in range(4):
                g_.dma_start(
                    sw2[:, a4 * 4096:(a4 + 1) * 4096].rearrange("p (a n) -> p a n", a=4),
                    w2[half * 2048 + a4 * 512: half * 2048 + (a4 + 1) * 512, :]
                    .rearrange("(a p) n -> p a n", p=128)).then_inc(qW, 16)
        P.gsec(g_w)
        wready[half] = P.inc("qW", 12 * 16)

        if half == 0:
            def v_pro(v):
                v.wait_ge(qW, wready[0])
                v.wait_ge(sG, gready)
                v.tensor_scalar(sb2h[:, :], sb2h[:, :], 0.5, None, ALU.mult).then_inc(sV, 1)
            P.vsec(v_pro)
            b2ready = P.inc("sV")

        for g in range(NG):
            pb = g % 2
            def y_load(y, half=half, g=g, pb=pb):
                if (half, g - 2) in store_done:
                    y.wait_ge(*store_done[(half, g - 2)])
                elif half == 1:
                    y.wait_ge(*store_done[(0, NG - 2 + g)])
                if half == 0:
                    y.dma_start(xt[pb][:, :].rearrange("p (c t) -> p c t", c=NCH),
                                xin[:, :, g * G:(g + 1) * G].rearrange("c p t -> p c t")
                                ).then_inc(qL[g % 2], 16)
                else:
                    y.wait_ge(*store_done[(0, g)])
                    y.dma_start(xt[pb][:, :].rearrange("p (c t) -> p c t", c=NCH),
                                xout[:, :, g * G:(g + 1) * G].rearrange("c p t -> p c t")
                                ).then_inc(qL[g % 2], 16)
                    y.dma_start(xtln[pb][:, :].rearrange("p (c t) -> p c t", c=NCH),
                                xin[:, :, g * G:(g + 1) * G].rearrange("c p t -> p c t")
                                ).then_inc(qL[g % 2], 16)
            P.ysec(y_load)
            load_done[(half, g)] = (qL[g % 2], P.inc("qL%d" % (g % 2), 16 * (1 if half == 0 else 2)))

            def v_wait(v, half=half, g=g):
                v.wait_ge(*load_done[(half, g)])
                v.wait_ge(qW, wready[half])
                key2 = (half, g - 2) if g >= 2 else ((half - 1, NG - 2 + g) if half == 1 else None)
                if key2 is not None:
                    v.wait_ge(sT, amm_done[(key2[0], key2[1], HALF - 1)])
            P.vsec(v_wait)

            lnsrc = xt[pb] if half == 0 else xtln[pb]
            bufs = dict(xt=lnsrc, x_bf=x_bf, x2_bf=x2_bf, u_bf=u_bf[pb],
                        Bmu=Bmu, Bvar=Bvar, Brstd=Brstd, tA=tA, tB=tB,
                        ones_bf=ones_bf, psB=psB, epscol=epscol)
            uapply[(half, g)] = emit_ln_group(P, bufs, G, 0)

            for hc in range(HALF):
                hcg = half * HALF + hc
                ap = hc % 2

                def t_amm(t, half=half, g=g, hc=hc, ap=ap, pb=pb):
                    if hc == 0:
                        t.wait_ge(sV, uapply[(half, g)])
                    if hc >= 2:
                        key2 = (half, g, hc - 2)
                    elif (half, g - 1, HALF - 2 + hc) in mid_evac:
                        key2 = (half, g - 1, HALF - 2 + hc)
                    elif half == 1:
                        key2 = (0, NG - 1, HALF - 2 + hc)
                    else:
                        key2 = None
                    if key2 is not None:
                        t.wait_ge(sS, mid_evac[key2])
                    for kc in range(NCH):
                        i = t.matmul(psA[ap][:, :],
                                     sw1[:, kc * HALF * 128 + hc * 128:
                                         kc * HALF * 128 + hc * 128 + 128],
                                     u_bf[pb][:, _chsl(kc)],
                                     start=(kc == 0), stop=(kc == NCH - 1))
                    i.then_inc(sT, 1)
                P.tsec(t_amm)
                amm_done[(half, g, hc)] = P.inc("sT")

                def s_mid(s, half=half, g=g, hc=hc, hcg=hcg, ap=ap):
                    s.wait_ge(sT, amm_done[(half, g, hc)])
                    if hc == 0 and (half, g - 1, NCH - 1) in w2mm_done:
                        s.wait_ge(sT, w2mm_done[(half, g - 1, NCH - 1)])
                    elif hc == 0 and half == 1:
                        s.wait_ge(sT, w2mm_done[(0, NG - 1, NCH - 1)])
                    s.activation(mid[:, hc * G:(hc + 1) * G], psA[ap][:, :],
                                 AF.Relu, bias=sb1[:, hcg:hcg + 1]).then_inc(sS, 1)
                P.ssec(s_mid)
                mid_evac[(half, g, hc)] = P.inc("sS")

            for oc in range(NCH):
                op = oc % 2

                def t_w2(t, half=half, g=g, oc=oc, op=op):
                    t.wait_ge(sS, mid_evac[(half, g, HALF - 1)])
                    if oc >= 2:
                        key2 = (half, g, oc - 2)
                    elif (half, g - 1, NCH - 2 + oc) in out_evac:
                        key2 = (half, g - 1, NCH - 2 + oc)
                    elif half == 1:
                        key2 = (0, NG - 1, NCH - 2 + oc)
                    else:
                        key2 = None
                    if key2 is not None:
                        t.wait_ge(sS, out_evac[key2])
                    for hc in range(HALF):
                        i = t.matmul(psO[op][:, :],
                                     sw2[:, hc * 1024 + oc * 128: hc * 1024 + oc * 128 + 128],
                                     mid[:, hc * G:(hc + 1) * G],
                                     start=(hc == 0), stop=(hc == HALF - 1))
                    i.then_inc(sT, 1)
                P.tsec(t_w2)
                w2mm_done[(half, g, oc)] = P.inc("sT")

                def s_out(s, half=half, g=g, oc=oc, op=op):
                    s.wait_ge(sT, w2mm_done[(half, g, oc)])
                    if oc >= 2:
                        key2 = (half, g, oc - 2)
                    elif (half, g - 1, NCH - 2 + oc) in xadd_done:
                        key2 = (half, g - 1, NCH - 2 + oc)
                    elif half == 1:
                        key2 = (0, NG - 1, NCH - 2 + oc)
                    else:
                        key2 = None
                    if key2 is not None:
                        s.wait_ge(sV, xadd_done[key2])
                    else:
                        s.wait_ge(sV, b2ready)
                    if half == 0:
                        s.activation(tO[op][:, :], psO[op][:, :], AF.Identity,
                                     scale=0.5, bias=sb2h[:, oc:oc + 1]).then_inc(sS, 1)
                    else:
                        s.activation(tO[op][:, :], psO[op][:, :], AF.Identity,
                                     scale=0.5).then_inc(sS, 1)
                P.ssec(s_out)
                out_evac[(half, g, oc)] = P.inc("sS")

                def v_add(v, half=half, g=g, oc=oc, op=op, pb=pb):
                    v.wait_ge(sS, out_evac[(half, g, oc)])
                    v.tensor_tensor(xt[pb][:, _chsl(oc)], xt[pb][:, _chsl(oc)],
                                    tO[op][:, :], ALU.add).then_inc(sV, 1)
                P.vsec(v_add)
                xadd_done[(half, g, oc)] = P.inc("sV")

            def g_store(g_, half=half, g=g, pb=pb):
                g_.wait_ge(sV, xadd_done[(half, g, NCH - 1)])
                g_.dma_start(xout[:, :, g * G:(g + 1) * G].rearrange("c p t -> p c t"),
                             xt[pb][:, :].rearrange("p (c t) -> p c t", c=NCH)
                             ).then_inc(qS[g % 2], 16)
            P.gsec(g_store)
            store_done[(half, g)] = (qS[g % 2], P.inc("qS%d" % (g % 2), 16))

    def g_fin(g_):
        g_.wait_ge(qS[0], P.cnt["qS0"])
        g_.wait_ge(qS[1], P.cnt["qS1"])
    P.gsec(g_fin)


    P.pop_scope()
    P.emit_barrier()


def build_l1ffn():
    key = "l1ffn"
    if key in _PROG_CACHE:
        return _PROG_CACHE[key]
    P = Prog(key)
    nc = P.nc
    ext = {}
    ext["xin"] = nc.dram_tensor("xin", [NCH, 128, TPC], F32, kind="ExternalInput")
    ext["w1"] = nc.dram_tensor("w1", [DD, 4096], F32, kind="ExternalInput")
    ext["w2"] = nc.dram_tensor("w2", [4096, DD], F32, kind="ExternalInput")
    ext["b1t"] = nc.dram_tensor("b1t", [128, 32], F32, kind="ExternalInput")
    ext["b2t"] = nc.dram_tensor("b2t", [128, 8], F32, kind="ExternalInput")
    ext["xout"] = nc.dram_tensor("xout", [NCH, 128, TPC], F32, kind="ExternalOutput")
    P.std_sems()
    P.open()
    emit_l1ffn(P, ext)
    P.close()
    _PROG_CACHE[key] = P
    return P


def run_l1ffn(xs, w1, w2, b1, b2):
    P = build_l1ffn()
    b1t = np.ascontiguousarray(b1.reshape(32, 128).T)
    b2t = np.ascontiguousarray(b2.reshape(8, 128).T)
    in_maps = [{"xin": xs[c], "w1": w1, "w2": w2, "b1t": b1t, "b2t": b2t}
               for c in range(NCORE)]
    res = run_bass_kernel_spmd(P.nc, in_maps, core_ids=list(range(NCORE)))
    return [res.results[c]["xout"] for c in range(NCORE)]


def emit_l1attn_p1(P, ext):
    nc = P.nc
    xin = ext["xin"]
    wqT = ext["wqT"]
    wkT = ext["wkT"]
    wvT = ext["wvT"]
    qt_out = ext["qt_out"]
    kvp = ext["kvp"]

    P.push_scope()
    swq = P.sb("swq", [128, NCH * 1024], BF16)
    swk = P.sb("swk", [128, NCH * 1024], BF16)
    swv = P.sb("swv", [128, NCH * 1024], BF16)
    ones_bf = P.sb("ones_bf", [128, 128], BF16)
    epscol = P.sb("epscol", [128, 1], F32)
    xt = [P.sb(f"xt{i}", [128, TPC], F32) for i in range(2)]
    x_bf = P.sb("xbf", [128, TPC], BF16)
    x2_bf = P.sb("x2bf", [128, TPC], BF16)
    u_bf = [P.sb(f"ubf{i}", [128, TPC], BF16) for i in range(2)]
    qt = [P.sb(f"qt{i}", [128, TPC], BF16) for i in range(2)]
    kt_bf = P.sb("kt_bf", [128, 2 * 512], BF16)
    vt_bf = P.sb("vt_bf", [128, 2 * 512], BF16)
    kv_acc = P.sb("kv_acc", [64, 1024], F32)
    kv_out_s = P.sb("kv_out_s", [64, 1024], F32)
    Bmu = P.sb("Bmu", [128, G], F32)
    Bvar = P.sb("Bvar", [128, G], F32)
    Brstd = P.sb("Brstd", [128, G], F32)
    tA = P.sb("tA", [128, G], F32)
    tB = P.sb("tB", [128, G], F32)

    psB = P.ps("psB", [128, 2 * G])
    psQ = [P.ps(f"psQ{i}", [128, G]) for i in range(2)]
    psK = P.ps("psK", [128, 512])
    psV = P.ps("psV", [128, 512])
    psKV = P.ps("psKV", [64, 1024])

    qL = [P.handles["qL0"], P.handles["qL1"]]
    qS = [P.handles["qS0"], P.handles["qS1"]]
    qW = P.handles["qW"]
    sG = P.handles["sG"]
    sT = P.handles["sT"]; sV = P.handles["sV"]; sS = P.handles["sS"]

    def g_pro(g_):
        for a in range(NCH):
            g_.dma_start(swq[:, a * 1024:(a + 1) * 1024], wqT[a * 128:(a + 1) * 128, :]).then_inc(qW, 16)
            g_.dma_start(swk[:, a * 1024:(a + 1) * 1024], wkT[a * 128:(a + 1) * 128, :]).then_inc(qW, 16)
            g_.dma_start(swv[:, a * 1024:(a + 1) * 1024], wvT[a * 128:(a + 1) * 128, :]).then_inc(qW, 16)
        g_.memset(ones_bf[:, :], 1.0)
        g_.memset(epscol[:, :], 1e-5)
        g_.memset(kv_acc[:, :], 0.0).then_inc(sG, 1)
    P.gsec(g_pro)
    wready = P.inc("qW", 24 * 16)
    gready = P.inc("sG")

    load_done = {}
    qstore_done = {}
    uapply = {}
    qev = {}
    kvacc_done = {}
    kvmm_done = {}
    kvev_done = {}

    for g in range(NG):
        pb = g % 2

        def y_load(y, g=g, pb=pb):
            if g >= 2:
                y.wait_ge(*qstore_done[g - 2])
            y.dma_start(xt[pb][:, :].rearrange("p (c t) -> p c t", c=NCH),
                        xin[:, :, g * G:(g + 1) * G].rearrange("c p t -> p c t")
                        ).then_inc(qL[g % 2], 16)
        P.ysec(y_load)
        load_done[g] = (qL[g % 2], P.inc("qL%d" % (g % 2), 16))

        def v_wait(v, g=g):
            v.wait_ge(*load_done[g])
            v.wait_ge(qW, wready)
            v.wait_ge(sG, gready)
            if g >= 2:
                v.wait_ge(sT, kvmm_done[((g - 2), 3, 1, "proj")])
        P.vsec(v_wait)

        bufs = dict(xt=xt[pb], x_bf=x_bf, x2_bf=x2_bf, u_bf=u_bf[pb],
                    Bmu=Bmu, Bvar=Bvar, Brstd=Brstd, tA=tA, tB=tB,
                    ones_bf=ones_bf, psB=psB, epscol=epscol)
        uapply[g] = emit_ln_group(P, bufs, G, 0)

        for oc in range(NCH):
            op = oc % 2

            def t_q(t, g=g, oc=oc, op=op, pb=pb):
                if oc == 0:
                    t.wait_ge(sV, uapply[g])
                if oc >= 2:
                    t.wait_ge(sS, qev[(g, oc - 2)])
                elif g >= 1:
                    t.wait_ge(sS, qev[(g - 1, NCH - 2 + oc)])
                for kc in range(NCH):
                    i = t.matmul(psQ[op][:, :],
                                 swq[:, kc * 1024 + oc * 128: kc * 1024 + oc * 128 + 128],
                                 u_bf[pb][:, _chsl(kc)],
                                 start=(kc == 0), stop=(kc == NCH - 1))
                i.then_inc(sT, 1)
            P.tsec(t_q)
            tq = P.inc("sT")

            def s_qev(s, g=g, oc=oc, op=op, pb=pb, tq=tq):
                s.wait_ge(sT, tq)
                if g >= 2 and oc == 0:
                    s.wait_ge(*qstore_done[g - 2])
                s.activation(qt[pb][:, _chsl(oc)], psQ[op][:, :], AF.Identity).then_inc(sS, 1)
            P.ssec(s_qev)
            qev[(g, oc)] = P.inc("sS")

        for tl in range(4):
            for hh in range(2):
                tp = (2 * tl + hh) % 2

                def t_kv(t, g=g, tl=tl, hh=hh, tp=tp, pb=pb):
                    if tl == 0 and hh == 0:
                        t.wait_ge(sV, uapply[g])
                    idx = g * 8 + tl * 2 + hh
                    if idx >= 1:
                        t.wait_ge(sS, kvev_done[idx - 1])
                    for kc in range(NCH):
                        t.matmul(psK[:, :],
                                 u_bf[pb][:, kc * G + tl * 128: kc * G + tl * 128 + 128],
                                 swk[:, kc * 1024 + hh * 512: kc * 1024 + hh * 512 + 512],
                                 start=(kc == 0), stop=(kc == NCH - 1))
                    for kc in range(NCH):
                        i = t.matmul(psV[:, :],
                                     u_bf[pb][:, kc * G + tl * 128: kc * G + tl * 128 + 128],
                                     swv[:, kc * 1024 + hh * 512: kc * 1024 + hh * 512 + 512],
                                     start=(kc == 0), stop=(kc == NCH - 1))
                    i.then_inc(sT, 1)
                P.tsec(t_kv)
                tkv = P.inc("sT")
                kvmm_done[(g, tl, hh, "proj")] = tkv

                def s_kvev(s, tp=tp, tkv=tkv, g=g, tl=tl, hh=hh):
                    s.wait_ge(sT, tkv)
                    idx = g * 8 + tl * 2 + hh
                    if idx >= 2:
                        s.wait_ge(sT, kvmm_done[idx - 2])
                    s.activation(kt_bf[:, tp * 512:(tp + 1) * 512], psK[:, :], AF.Identity)
                    s.activation(vt_bf[:, tp * 512:(tp + 1) * 512], psV[:, :], AF.Identity).then_inc(sS, 1)
                P.ssec(s_kvev)
                idx = g * 8 + tl * 2 + hh
                kvev_done[idx] = P.inc("sS")

                def t_kvmm(t, g=g, tl=tl, hh=hh, tp=tp, idx=idx):
                    t.wait_ge(sS, kvev_done[idx])
                    if idx >= 1:
                        t.wait_ge(sV, kvacc_done[idx - 1])
                    for h8 in range(8):
                        h = hh * 8 + h8
                        i = t.matmul(psKV[:, h * 64:(h + 1) * 64],
                                     kt_bf[:, tp * 512 + h8 * 64: tp * 512 + (h8 + 1) * 64],
                                     vt_bf[:, tp * 512 + h8 * 64: tp * 512 + (h8 + 1) * 64],
                                     start=True, stop=True)
                    i.then_inc(sT, 1)
                P.tsec(t_kvmm)
                kvmm_done[idx] = P.inc("sT")

                def v_kvacc(v, hh=hh, idx=idx):
                    v.wait_ge(sT, kvmm_done[idx])
                    v.tensor_tensor(kv_acc[:, hh * 512:(hh + 1) * 512],
                                    kv_acc[:, hh * 512:(hh + 1) * 512],
                                    psKV[:, hh * 512:(hh + 1) * 512], ALU.add).then_inc(sV, 1)
                P.vsec(v_kvacc)
                kvacc_done[idx] = P.inc("sV")

        def g_qstore(g_, g=g, pb=pb):
            g_.wait_ge(sS, qev[(g, NCH - 1)])
            g_.dma_start(qt_out[:, :, g * G:(g + 1) * G].rearrange("c p t -> p c t"),
                         qt[pb][:, :].rearrange("p (c t) -> p c t", c=NCH)
                         ).then_inc(qS[g % 2], 16)
        P.gsec(g_qstore)
        qstore_done[g] = (qS[g % 2], P.inc("qS%d" % (g % 2), 16))

    def v_fin(v):
        v.wait_ge(sV, kvacc_done[NG * 8 - 1])
        v.tensor_copy(kv_out_s[:, :], kv_acc[:, :]).then_inc(sV, 1)
    P.vsec(v_fin)
    vfin = P.inc("sV")

    def g_fin(g_):
        g_.wait_ge(sV, vfin)
        g_.dma_start(kvp[:, :], kv_out_s[:, :]).then_inc(qW, 16)
        g_.wait_ge(qW, P.cnt["qW"] + 16)
        g_.wait_ge(qS[0], P.cnt["qS0"])
        g_.wait_ge(qS[1], P.cnt["qS1"])
    P.gsec(g_fin)
    P.inc("qW", 16)


    P.pop_scope()
    P.emit_barrier()


def build_l1attn_p1():
    key = "l1attn1"
    if key in _PROG_CACHE:
        return _PROG_CACHE[key]
    P = Prog(key)
    nc = P.nc
    ext = {}
    ext["xin"] = nc.dram_tensor("xin", [NCH, 128, TPC], F32, kind="ExternalInput")
    ext["wqT"] = nc.dram_tensor("wqT", [DD, DD], F32, kind="ExternalInput")
    ext["wkT"] = nc.dram_tensor("wkT", [DD, DD], F32, kind="ExternalInput")
    ext["wvT"] = nc.dram_tensor("wvT", [DD, DD], F32, kind="ExternalInput")
    ext["qt_out"] = nc.dram_tensor("qt_out", [NCH, 128, TPC], BF16, kind="ExternalOutput")
    ext["kvp"] = nc.dram_tensor("kvp", [64, 1024], F32, kind="ExternalOutput")
    P.std_sems()
    P.open()
    emit_l1attn_p1(P, ext)
    P.close()
    _PROG_CACHE[key] = P
    return P


def emit_l1attn_p2(P, ext, fused_kv=False):
    nc = P.nc
    xin = ext["xin"]
    qt_in = ext["qt_in"]
    kvp4 = ext["kvp4"]
    woT = ext["woT"]
    xout = ext["xout"]

    P.push_scope()
    swo = P.sb("swo", [128, NCH * 1024], BF16)
    kv4 = P.sb("kv4", [64, 4 * 1024], F32)
    kv_bf = P.sb("kv_bf", [64, 1024], BF16)
    kv2 = P.sb("kv2", [128, 512], BF16)
    xt = [P.sb(f"xt{i}", [128, TPC], F32) for i in range(2)]
    qt = [P.sb(f"qt{i}", [128, TPC], BF16) for i in range(2)]
    at_bf = P.sb("at_bf", [128, TPC], BF16)
    tO = [P.sb(f"tO{i}", [128, G], F32) for i in range(2)]

    psA = [P.ps(f"psA{i}", [128, G]) for i in range(2)]
    psO = [P.ps(f"psO{i}", [128, G]) for i in range(2)]

    qL = [P.handles["qL0"], P.handles["qL1"]]
    qS = [P.handles["qS0"], P.handles["qS1"]]
    qW = P.handles["qW"]
    sT = P.handles["sT"]; sV = P.handles["sV"]; sS = P.handles["sS"]

    def g_pro(g_):
        for a in range(NCH):
            g_.dma_start(swo[:, a * 1024:(a + 1) * 1024], woT[a * 128:(a + 1) * 128, :]).then_inc(qW, 16)
        if fused_kv:
            g_.dma_start(kv4[:, 0:1024], kvp4[:, :]).then_inc(qW, 16)
        else:
            g_.dma_start(kv4[:, :].rearrange("p (a n) -> p a n", a=4),
                         kvp4[:, :, :].rearrange("a p n -> p a n")).then_inc(qW, 16)
    P.gsec(g_pro)
    wready = P.inc("qW", 9 * 16)

    def v_kv(v):
        v.wait_ge(qW, wready)
        if not fused_kv:
            v.tensor_tensor(kv4[:, 0:1024], kv4[:, 0:1024], kv4[:, 1024:2048], ALU.add)
            v.tensor_tensor(kv4[:, 2048:3072], kv4[:, 2048:3072], kv4[:, 3072:4096], ALU.add)
            v.drain()
            v.tensor_tensor(kv4[:, 0:1024], kv4[:, 0:1024], kv4[:, 2048:3072], ALU.add)
            v.drain()
        v.tensor_copy(kv_bf[:, :], kv4[:, 0:1024]).then_inc(sV, 1)
    P.vsec(v_kv)
    kvready0 = P.inc("sV")

    def g_kv2(g_):
        g_.wait_ge(sV, kvready0)
        for par in range(2):
            g_.dma_start(
                kv2[par * 64:(par + 1) * 64, :].rearrange("d (hp e) -> d hp e", e=64),
                kv_bf[:, :].rearrange("d (hp par e) -> d par hp e", par=2, e=64)[:, par, :, :]
            ).then_inc(qW, 16)
    P.gsec(g_kv2)
    kvready = P.inc("qW", 32)

    load_done = {}
    store_done = {}
    atev = {}
    atmm = {}
    omm = {}
    oev = {}
    xadd = {}

    for g in range(NG):
        pb = g % 2

        def y_load(y, g=g, pb=pb):
            if g >= 2:
                y.wait_ge(*store_done[g - 2])
            y.dma_start(xt[pb][:, :].rearrange("p (c t) -> p c t", c=NCH),
                        xin[:, :, g * G:(g + 1) * G].rearrange("c p t -> p c t")
                        ).then_inc(qL[g % 2], 16)
            y.dma_start(qt[pb][:, :].rearrange("p (c t) -> p c t", c=NCH),
                        qt_in[:, :, g * G:(g + 1) * G].rearrange("c p t -> p c t")
                        ).then_inc(qL[g % 2], 16)
        P.ysec(y_load)
        load_done[g] = (qL[g % 2], P.inc("qL%d" % (g % 2), 32))

        for hp in range(8):
            ap = hp % 2

            def t_at(t, g=g, hp=hp, ap=ap, pb=pb):
                if g == 0 and hp == 0:
                    t.wait_ge(qW, kvready)
                t.wait_ge(*load_done[g])
                if hp >= 2:
                    t.wait_ge(sS, atev[(g, hp - 2)])
                elif g >= 1:
                    t.wait_ge(sS, atev[(g - 1, 6 + hp)])
                if hp == 0 and g >= 1:
                    t.wait_ge(sT, omm[(g - 1, NCH - 1)])
                for sub in range(2):
                    h = hp * 2 + sub
                    oc = h // 2
                    pbase = 64 * (h % 2)
                    i = t.matmul(psA[ap][pbase:pbase + 64, :],
                                 kv2[pbase:pbase + 64, oc * 64:(oc + 1) * 64],
                                 qt[pb][pbase:pbase + 64, _chsl(oc)],
                                 start=True, stop=True)
                i.then_inc(sT, 1)
            P.tsec(t_at)
            atmm[(g, hp)] = P.inc("sT")

            def s_atev(s, g=g, hp=hp, ap=ap, pb=pb):
                s.wait_ge(sT, atmm[(g, hp)])
                s.activation(at_bf[:, hp * G:(hp + 1) * G], psA[ap][:, :],
                             AF.Identity, scale=0.125).then_inc(sS, 1)
            P.ssec(s_atev)
            atev[(g, hp)] = P.inc("sS")

        for oc in range(NCH):
            op = oc % 2

            def t_o(t, g=g, oc=oc, op=op):
                t.wait_ge(sS, atev[(g, 7)])
                if oc >= 2:
                    t.wait_ge(sS, oev[(g, oc - 2)])
                elif g >= 1:
                    t.wait_ge(sS, oev[(g - 1, NCH - 2 + oc)])
                for kc in range(NCH):
                    i = t.matmul(psO[op][:, :],
                                 swo[:, kc * 1024 + oc * 128: kc * 1024 + oc * 128 + 128],
                                 at_bf[:, _chsl(kc)],
                                 start=(kc == 0), stop=(kc == NCH - 1))
                i.then_inc(sT, 1)
            P.tsec(t_o)
            omm[(g, oc)] = P.inc("sT")

            def s_oev(s, g=g, oc=oc, op=op):
                s.wait_ge(sT, omm[(g, oc)])
                if oc >= 2:
                    key2 = (g, oc - 2)
                elif g >= 1:
                    key2 = (g - 1, NCH - 2 + oc)
                else:
                    key2 = None
                if key2 is not None:
                    s.wait_ge(sV, xadd[key2])
                s.activation(tO[op][:, :], psO[op][:, :], AF.Identity).then_inc(sS, 1)
            P.ssec(s_oev)
            oev[(g, oc)] = P.inc("sS")

            def v_add(v, g=g, oc=oc, op=op, pb=pb):
                v.wait_ge(sS, oev[(g, oc)])
                v.tensor_tensor(xt[pb][:, _chsl(oc)], xt[pb][:, _chsl(oc)],
                                tO[op][:, :], ALU.add).then_inc(sV, 1)
            P.vsec(v_add)
            xadd[(g, oc)] = P.inc("sV")

        def g_store(g_, g=g, pb=pb):
            g_.wait_ge(sV, xadd[(g, NCH - 1)])
            g_.dma_start(xout[:, :, g * G:(g + 1) * G].rearrange("c p t -> p c t"),
                         xt[pb][:, :].rearrange("p (c t) -> p c t", c=NCH)
                         ).then_inc(qS[g % 2], 16)
        P.gsec(g_store)
        store_done[g] = (qS[g % 2], P.inc("qS%d" % (g % 2), 16))

    def g_fin(g_):
        g_.wait_ge(qS[0], P.cnt["qS0"])
        g_.wait_ge(qS[1], P.cnt["qS1"])
    P.gsec(g_fin)


    P.pop_scope()
    P.emit_barrier()


def build_l1attn_p2():
    key = "l1attn2"
    if key in _PROG_CACHE:
        return _PROG_CACHE[key]
    P = Prog(key)
    nc = P.nc
    ext = {}
    ext["xin"] = nc.dram_tensor("xin", [NCH, 128, TPC], F32, kind="ExternalInput")
    ext["qt_in"] = nc.dram_tensor("qt_in", [NCH, 128, TPC], BF16, kind="ExternalInput")
    ext["kvp4"] = nc.dram_tensor("kvp4", [4, 64, 1024], F32, kind="ExternalInput")
    ext["woT"] = nc.dram_tensor("woT", [DD, DD], F32, kind="ExternalInput")
    ext["xout"] = nc.dram_tensor("xout", [NCH, 128, TPC], F32, kind="ExternalOutput")
    P.std_sems()
    P.open()
    emit_l1attn_p2(P, ext)
    P.close()
    _PROG_CACHE[key] = P
    return P


def run_l1attn(xs, W_q, W_k, W_v, W_o):
    P1 = build_l1attn_p1()
    wqT = np.ascontiguousarray(W_q.T)
    wkT = np.ascontiguousarray(W_k.T)
    wvT = np.ascontiguousarray(W_v.T)
    woT = np.ascontiguousarray(W_o.T)
    in_maps = [{"xin": xs[c], "wqT": wqT, "wkT": wkT, "wvT": wvT} for c in range(NCORE)]
    res = run_bass_kernel_spmd(P1.nc, in_maps, core_ids=list(range(NCORE)))
    qts = [res.results[c]["qt_out"] for c in range(NCORE)]
    kvps = [res.results[c]["kvp"] for c in range(NCORE)]
    kv_b0 = np.stack(kvps[0:4]).astype(np.float32)
    kv_b1 = np.stack(kvps[4:8]).astype(np.float32)
    P2 = build_l1attn_p2()
    in_maps = [{"xin": xs[c], "qt_in": qts[c],
                "kvp4": (kv_b0 if c < 4 else kv_b1), "woT": woT}
               for c in range(NCORE)]
    res = run_bass_kernel_spmd(P2.nc, in_maps, core_ids=list(range(NCORE)))
    return [res.results[c]["xout"] for c in range(NCORE)]


def emit_l0attn(P, ext, l):
    nc = P.nc
    SW = 513
    xin = ext["xin"]
    xhalo = ext["xhalo"]
    wqkv = ext["wqkv"]
    wo = ext["wo"]
    qscol = ext["qscol"]
    kscol = ext["kscol"]
    tqc = ext["tqc"]
    tqs = ext["tqs"]
    tkc = ext["tkc"]
    tks = ext["tks"]
    smask = ext["smask"]
    cm0 = ext["cm0"]
    cm4 = ext["cm4"]
    cid128b = ext["cid128b"]
    chs = ext["chs"]
    chsT = ext["chsT"]
    crot = ext["crot"]
    xout = ext["xout"]

    P.push_scope()
    s_wqkv = P.sb("s_wqkv", [128, NCH * 1536], BF16)
    s_wo = P.sb("s_wo", [128, 4 * 1024], BF16)
    s_qs = P.sb("s_qs", [128, 1], F32)
    s_ks = P.sb("s_ks", [128, 1], F32)
    s_sm = P.sb("s_sm", [128, 16 * 128], BF16)
    s_m0 = P.sb("s_m0", [128, 128], BF16)
    s_m4 = P.sb("s_m4", [128, 128], BF16)
    s_idb = P.sb("s_idb", [128, 128], BF16)
    s_hs = P.sb("s_hs", [128, 32], F32)
    s_hsT = P.sb("s_hsT", [8, 512], F32)
    s_rot = P.sb("s_rot", [128, 128], BF16)
    ones_bf = P.sb("ones_bf", [128, 128], BF16)
    epscol = P.sb("epscol", [128, 1], F32)
    tb = {nm: [P.sb(f"tb_{nm}{i}", [128, 512], BF16) for i in range(2)]
          for nm in ("qc", "qs", "kc", "ks")}
    xt = [P.sb(f"xt{i}", [128, NCH * SW], F32) for i in range(2)]
    xh = xt[1]
    u_bf = [P.sb(f"ubf{i}", [128, NCH * SW], BF16) for i in range(2)]
    uh_bf = u_bf[1]
    uf = [P.sb(f"uf{i}", [128, TPC], F32) for i in range(2)] if l == 0 else None
    x_bf = P.sb("xbf", [128, NCH * SW], BF16)
    x2_bf = P.sb("x2bf", [128, NCH * SW], BF16)
    qf = P.sb("qf", [128, G], F32)
    sqv = P.sb("sqv", [128, G], F32)
    nr = P.sb("nr", [8, G], F32)
    rr = P.sb("rr", [8, G], F32)
    qsc = P.sb("qsc", [128, G], BF16)
    t1b = P.sb("t1b", [128, G], BF16)
    qT_f = [P.sb(f"qTf{i}", [128, 2048], BF16) for i in range(2)]
    kT_c = P.sb("kT_c", [128, 4096], BF16)
    v_c = P.sb("v_c", [128, 4096], BF16)
    attn_bf = [P.sb(f"attn_bf{i}", [128, 640], BF16) for i in range(2)]
    dgr = [P.sb(f"dgr{i}", [128, 128], BF16) for i in range(2)]
    attnT_bf = P.sb("attnT_bf", [128, 640], BF16)
    negmax = P.sb("negmax", [128, 1], F32)
    rsum = P.sb("rsum", [128, 1], F32)
    rrec = P.sb("rrec", [128, 1], F32)
    oT_g = P.sb("oT_g", [128, 2048], BF16)
    tO = [P.sb(f"tO{i}", [128, G], F32) for i in range(2)]
    Bmu = P.sb("Bmu", [128, G], F32)
    Bvar = P.sb("Bvar", [128, G], F32)
    Brstd = P.sb("Brstd", [128, G], F32)
    tA = P.sb("tA", [128, G], F32)
    tB = P.sb("tB", [128, G], F32)

    psQN = P.ps("psQN", [128, G])
    psB = psQN
    psSim = [P.ps(f"psSim{i}", [128, 640]) for i in range(2)]
    psAV = P.ps("psAV", [128, 1024])

    qL = [P.handles["qL0"], P.handles["qL1"]]
    qS = [P.handles["qS0"], P.handles["qS1"]]
    qW = P.handles["qW"]
    sG = P.handles["sG"]
    sT = P.handles["sT"]; sV = P.handles["sV"]; sS = P.handles["sS"]

    def g_pro(g_):
        for a in range(NCH):
            g_.dma_start(s_wqkv[:, a * 1536:(a + 1) * 1536], wqkv[a * 128:(a + 1) * 128, :]).then_inc(qW, 16)
        for a in range(4):
            g_.dma_start(s_wo[:, a * 1024:(a + 1) * 1024], wo[a * 128:(a + 1) * 128, :]).then_inc(qW, 16)
        g_.dma_start(s_sm[:, :], smask[:, :]).then_inc(qW, 16)
        for nm, dst in [(qscol, s_qs), (kscol, s_ks), (cm0, s_m0), (cm4, s_m4),
                        (cid128b, s_idb), (chs, s_hs), (chsT, s_hsT), (crot, s_rot)]:
            g_.dma_start(dst[:, :], nm[:, :]).then_inc(qW, 16)
        g_.dma_start(xh[:, :].rearrange("p (c t) -> p c t", c=NCH),
                     xhalo[:, :, :].rearrange("c p t -> p c t")).then_inc(qW, 16)
        g_.memset(ones_bf[:, :], 1.0)
        g_.memset(epscol[:, :], 1e-5).then_inc(sG, 1)
    P.gsec(g_pro)
    wready = P.inc("qW", 22 * 16)
    gready = P.inc("sG")

    def v_qs8(v):
        v.wait_ge(qW, wready)
        v.tensor_scalar(s_qs[:, :], s_qs[:, :], 8.0, None, ALU.mult).then_inc(sV, 1)
        v.drain()
    P.vsec(v_qs8)
    P.inc("sV")

    load_done = {}
    store_done = {}
    mm = {"qev": {}, "rope": {}, "vev": {}, "vmm": {}, "qkmm": {}, "nb": {},
          "sim": {}, "exp": {}, "scl": {}, "trp": {}, "atev": {}, "avmm": {},
          "oev": {}, "wo": {}, "woev": {}, "xadd": {}, "uready": {}}

    def shoff(kc, base):
        return base + (1 if kc < 4 else 0)

    for gi in range(-1, NG):
        pb = gi % 2
        halo = gi < 0
        ubuf = uh_bf if halo else u_bf[pb]
        srcx = xh if halo else xt[pb]

        def y_load(y, gi=gi, pb=pb, halo=halo):
            if not halo:
                if gi >= 2:
                    y.wait_ge(*store_done[gi - 2])
                elif gi == 1:
                    y.wait_ge(sV, mm["uready"][-1])
                y.dma_start(xt[pb][:, 0:TPC].rearrange("p (c t) -> p c t", c=NCH),
                            xin[:, :, gi * G:(gi + 1) * G].rearrange("c p t -> p c t")
                            ).then_inc(qL[pb], 16)
            if (gi - 2, "k", 3) in mm["rope"]:
                y.wait_ge(sT, mm["rope"][(gi - 2, "k", 3)])
            tcol = 512 + gi * 512
            for nm, t_ in [("qc", tqc), ("qs", tqs), ("kc", tkc), ("ks", tks)]:
                y.dma_start(tb[nm][pb][:, :], t_[:, tcol:tcol + 512]).then_inc(qL[pb], 16)
        P.ysec(y_load)
        load_done[gi] = (qL[pb], P.inc("qL%d" % pb, 16 * (4 if halo else 5)))

        def v_w(v, gi=gi):
            v.wait_ge(*load_done[gi])
            v.wait_ge(qW, wready)
            v.wait_ge(sG, gready)
            if (gi - 2, 3) in mm["vmm"]:
                v.wait_ge(sT, mm["vmm"][(gi - 2, 3)])
        P.vsec(v_w)

        def t_bguard(t, gi=gi):
            if (gi - 1, NCH - 1) in mm["woev"]:
                t.wait_ge(sS, mm["woev"][(gi - 1, NCH - 1)])
            elif (gi - 1, 3) in mm["vev"]:
                t.wait_ge(sS, mm["vev"][(gi - 1, 3)])
        P.tsec(t_bguard)

        if l == 0:
            bufs = dict(xt=srcx, x_bf=x_bf, x2_bf=x2_bf, u_bf=ubuf,
                        Bmu=Bmu, Bvar=Bvar, Brstd=Brstd, tA=tA, tB=tB,
                        ones_bf=ones_bf, psB=psB, epscol=epscol, seq_stats=True)
            if halo:
                emit_ln_group(P, bufs, 512, 0, soff=0, sstride=SW, doff=0, dstride=SW)
                ua = emit_ln_group(P, bufs, 1, 0, soff=512, sstride=SW, doff=512, dstride=SW)
            else:
                bufs["uf"] = uf[pb]
                ua = emit_ln_group(P, bufs, 512, 0, soff=0, sstride=512,
                                   doff=1, dstride=SW, want_uf=True)
            mm["uready"][gi] = ua
        else:
            def v_cast(v, gi=gi, halo=halo, srcx=srcx, ubuf=ubuf):
                W2 = SW if halo else 512
                do = 0 if halo else 1
                ss = SW if halo else 512
                for ch in range(NCH):
                    i = v.tensor_copy(ubuf[:, ch * SW + do: ch * SW + do + W2],
                                      srcx[:, ch * ss: ch * ss + W2])
                i.then_inc(sV, 1)
            P.vsec(v_cast)
            mm["uready"][gi] = P.inc("sV")

        if not halo:
            def v_c0(v, gi=gi, pb=pb, ubuf=ubuf):
                prev = uh_bf if gi == 0 else u_bf[1 - pb]
                v.drain()
                for ch in range(NCH):
                    i = v.tensor_copy(ubuf[:, ch * SW: ch * SW + 1],
                                      prev[:, ch * SW + 512: ch * SW + 513])
                i.then_inc(sV, 1)
            P.vsec(v_c0)
            mm["uready"][gi] = P.inc("sV")
        uready = mm["uready"][gi]

        seq = [("q", s_qs, "qc", "qs", oc) for oc in range(4)] + \
              [("k", s_ks, "kc", "ks", oc) for oc in range(4)]
        previx = (gi - 1, "k", 3)
        for six, (which, scol, tcn, tsn, oc) in enumerate(seq):
            ix = (gi, which, oc)
            pix = (gi, seq[six - 1][0], seq[six - 1][4]) if six > 0 else \
                  (previx if previx in mm["qev"] else None)
            wcol = oc * 128 + (0 if which == "q" else 512)

            def t_qk(t, ix=ix, pix=pix, wcol=wcol, ubuf=ubuf, uready=uready):
                if ix[1] == "q" and ix[2] == 0:
                    t.wait_ge(sV, uready)
                if pix is not None:
                    t.wait_ge(sS, mm["qev"][pix])
                for kc in range(NCH):
                    i = t.matmul(psQN[:, :],
                                 s_wqkv[:, kc * 1536 + wcol: kc * 1536 + wcol + 128],
                                 ubuf[:, ix[0] * 0 + kc * SW + shoff(kc, 0): kc * SW + shoff(kc, 0) + 512],
                                 start=(kc == 0), stop=(kc == NCH - 1))
                i.then_inc(sT, 1)
            P.tsec(t_qk)
            mm["qkmm"][ix] = tqk = P.inc("sT")

            def s_qev(s, tqk=tqk, pix=pix):
                s.wait_ge(sT, tqk)
                if pix is not None and pix in mm["rope"]:
                    s.wait_ge(sT, mm["rope"][pix])
                s.activation(qf[:, :], psQN[:, :], AF.Identity).then_inc(sS, 1)
            P.ssec(s_qev)
            mm["qev"][ix] = qev = P.inc("sS")

            def v_sq(v, qev=qev, pix=pix):
                v.wait_ge(sS, qev)
                if pix is not None and pix in mm["nb"]:
                    v.wait_ge(sT, mm["nb"][pix])
                v.tensor_tensor(sqv[:, :], qf[:, :], qf[:, :], ALU.mult).then_inc(sV, 1)
            P.vsec(v_sq)
            vsq = P.inc("sV")

            def t_nr(t, oc=oc, vsq=vsq):
                t.wait_ge(sV, vsq)
                t.matmul(psQN[0:8, :], s_hs[:, oc * 8:(oc + 1) * 8], sqv[:, :],
                         start=True, stop=True).then_inc(sT, 1)
            P.tsec(t_nr)
            tnr = P.inc("sT")

            def v_nr(v, tnr=tnr):
                v.wait_ge(sT, tnr)
                v.tensor_scalar(nr[:, :], psQN[0:8, :], 1e-24, None, ALU.max).then_inc(sV, 1)
            P.vsec(v_nr)
            vnr = P.inc("sV")

            def s_nr(s, vnr=vnr, pix=pix):
                s.wait_ge(sV, vnr)
                if pix is not None and pix in mm["nb"]:
                    s.wait_ge(sT, mm["nb"][pix])
                s.activation(rr[:, :], nr[:, :], AF.Sqrt).then_inc(sS, 1)
            P.ssec(s_nr)
            snr = P.inc("sS")

            def v_rr(v, snr=snr):
                v.wait_ge(sS, snr)
                v.reciprocal(rr[:, :], rr[:, :])
                v.drain().then_inc(sV, 1)
            P.vsec(v_rr)
            vrr = P.inc("sV")

            def t_nb(t, oc=oc, vrr=vrr):
                t.wait_ge(sV, vrr)
                t.matmul(psQN[:, :], s_hsT[:, oc * 128:(oc + 1) * 128], rr[:, :],
                         start=True, stop=True).then_inc(sT, 1)
            P.tsec(t_nb)
            mm["nb"][ix] = tnb = P.inc("sT")

            def v_stt(v, scol=scol, tnb=tnb):
                v.wait_ge(sT, tnb)
                v.scalar_tensor_tensor(qsc[:, :], qf[:, :], scol[:, 0:1], psQN[:, :],
                                       ALU.mult, ALU.mult).then_inc(sV, 1)
                v.drain()
            P.vsec(v_stt)
            vstt = P.inc("sV")

            def t_rot(t, vstt=vstt):
                t.wait_ge(sV, vstt)
                t.matmul(psQN[:, :], s_rot[:, :], qsc[:, :], start=True, stop=True).then_inc(sT, 1)
            P.tsec(t_rot)
            trot = P.inc("sT")

            def v_rope(v, gi=gi, pb=pb, which=which, oc=oc, tcn=tcn, tsn=tsn, trot=trot):
                v.wait_ge(sT, trot)
                if which == "q" and (gi - 2, 3, 7) in mm["sim"]:
                    v.wait_ge(sT, mm["sim"][(gi - 2, 3, 7)])
                if which == "k" and (gi - 1, 3, 7) in mm["sim"]:
                    v.wait_ge(sT, mm["sim"][(gi - 1, 3, 7)])
                v.tensor_tensor(t1b[:, :], qsc[:, :], tb[tcn][pb][:, :], ALU.mult)
                v.drain()
                v.tensor_tensor(qsc[:, :], psQN[:, :], tb[tsn][pb][:, :], ALU.mult)
                v.drain()
                if which == "q":
                    dst = qT_f[pb][:, oc * 512:(oc + 1) * 512]
                else:
                    dst = kT_c[:, oc * 1024 + pb * 512: oc * 1024 + pb * 512 + 512]
                v.tensor_tensor(dst, t1b[:, :], qsc[:, :], ALU.add).then_inc(sV, 1)
                v.drain()
            P.vsec(v_rope)
            rope_v = P.inc("sV")

            def t_rd(t, rope_v=rope_v):
                t.wait_ge(sV, rope_v)
                t.nop().then_inc(sT, 1)
            P.tsec(t_rd)
            mm["rope"][ix] = P.inc("sT")

        for tl in range(4):
            slot = pb * 4 + tl

            def t_v(t, gi=gi, tl=tl, ubuf=ubuf, uready=uready):
                if tl == 0:
                    t.wait_ge(sV, uready)
                    t.wait_ge(sS, mm["woev"].get(gi - 1, 0))
                if (gi, tl - 1) in mm["vev"]:
                    t.wait_ge(sS, mm["vev"][(gi, tl - 1)])
                elif (gi - 1, 3) in mm["vev"]:
                    t.wait_ge(sS, mm["vev"][(gi - 1, 3)])
                for kc in range(NCH):
                    i = t.matmul(psB[:, 0:512],
                                 ubuf[:, kc * SW + shoff(kc, tl * 128): kc * SW + shoff(kc, tl * 128) + 128],
                                 s_wqkv[:, kc * 1536 + 1024: kc * 1536 + 1536],
                                 start=(kc == 0), stop=(kc == NCH - 1))
                i.then_inc(sT, 1)
            P.tsec(t_v)
            mm["vmm"][(gi, tl)] = P.inc("sT")

            def s_vev(s, gi=gi, tl=tl, slot=slot):
                s.wait_ge(sT, mm["vmm"][(gi, tl)])
                if (gi - 1, 3, 7) in mm["avmm"]:
                    s.wait_ge(sT, mm["avmm"][(gi - 1, 3, 7)])
                s.activation(v_c[:, slot * 512:(slot + 1) * 512], psB[:, 0:512],
                             AF.Identity).then_inc(sS, 1)
            P.ssec(s_vev)
            mm["vev"][(gi, tl)] = P.inc("sS")

        if halo:
            continue

        def emit_back(key):
            gi_, tl_, h_ = key
            sp_ = h_ % 2
            p64_ = 64 * (h_ % 2)
            mglob_ = gi_ * 4 + tl_
            if h_ > 0:
                pk = (gi_, tl_, h_ - 1)
            elif tl_ > 0:
                pk = (gi_, tl_ - 1, 7)
            elif (gi_ - 1, 3, 7) in mm["exp"]:
                pk = (gi_ - 1, 3, 7)
            else:
                pk = None

            def t_trp(t, key=key, sp_=sp_):
                t.wait_ge(sV, mm["scl"][key])
                for b in range(5):
                    i = t.matmul(psSim[sp_][:, b * 128:(b + 1) * 128],
                                 attn_bf[sp_][:, b * 128:(b + 1) * 128], dgr[sp_][:, :],
                                 start=True, stop=True)
                i.then_inc(sT, 1)
            P.tsec(t_trp)
            mm["trp"][key] = P.inc("sT")

            def v_atev(v, key=key, sp_=sp_, pk=pk):
                v.wait_ge(sT, mm["trp"][key])
                if pk is not None:
                    v.wait_ge(sT, mm["avmm"][pk])
                v.tensor_copy(attnT_bf[:, :], psSim[sp_][:, :]).then_inc(sV, 1)
            P.vsec(v_atev)
            mm["atev"][key] = P.inc("sV")

            def t_av(t, key=key, p64_=p64_, mglob_=mglob_):
                gi_, tl_, h_ = key
                t.wait_ge(sV, mm["atev"][key])
                if h_ == 0:
                    if (gi_, tl_ - 1) in mm["oev"]:
                        t.wait_ge(sS, mm["oev"][(gi_, tl_ - 1)])
                    elif (gi_ - 1, 3) in mm["oev"]:
                        t.wait_ge(sS, mm["oev"][(gi_ - 1, 3)])
                cbase = 512 * (h_ % 2) + (h_ // 2) * 128
                for b in range(5):
                    slot = (mglob_ - 4 + b) % 8
                    i = t.matmul(psAV[p64_:p64_ + 64, cbase: cbase + 128],
                                 v_c[:, slot * 512 + h_ * 64: slot * 512 + h_ * 64 + 64],
                                 attnT_bf[:, b * 128:(b + 1) * 128],
                                 start=(h_ <= 1 and b == 0), stop=(h_ >= 6 and b == 4))
                i.then_inc(sT, 1)
            P.tsec(t_av)
            mm["avmm"][key] = P.inc("sT")

            if h_ == 7:
                def s_oev(s, gi_=gi_, tl_=tl_):
                    s.wait_ge(sT, mm["avmm"][(gi_, tl_, 7)])
                    if tl_ == 0 and (gi_ - 1, NCH - 1) in mm["wo"]:
                        s.wait_ge(sT, mm["wo"][(gi_ - 1, NCH - 1)])
                    s.activation(
                        oT_g[0:64, :].rearrange("p (pr i) -> p pr i", pr=4)[:, :, tl_ * 128:(tl_ + 1) * 128],
                        psAV[0:64, 0:512].rearrange("p (pr i) -> p pr i", i=128),
                        AF.Identity)
                    s.activation(
                        oT_g[64:128, :].rearrange("p (pr i) -> p pr i", pr=4)[:, :, tl_ * 128:(tl_ + 1) * 128],
                        psAV[64:128, 512:1024].rearrange("p (pr i) -> p pr i", i=128),
                        AF.Identity).then_inc(sS, 1)
                P.ssec(s_oev)
                mm["oev"][(gi_, tl_)] = P.inc("sS")

        pending = None
        for tl in range(4):
            mglob = gi * 4 + tl
            for h in range(8):
                oc = h // 2
                p64 = 64 * (h % 2)
                sp = h % 2
                if h > 0:
                    prevk = (gi, tl, h - 1)
                elif tl > 0:
                    prevk = (gi, tl - 1, 7)
                elif (gi - 1, 3, 7) in mm["exp"]:
                    prevk = (gi - 1, 3, 7)
                else:
                    prevk = None
                prevk2 = None
                if prevk is not None:
                    g2, t2, h2 = prevk
                    if h2 > 0:
                        prevk2 = (g2, t2, h2 - 1)
                    elif t2 > 0:
                        prevk2 = (g2, t2 - 1, 7)
                    elif (g2 - 1, 3, 7) in mm["exp"]:
                        prevk2 = (g2 - 1, 3, 7)

                def t_sim(t, gi=gi, tl=tl, h=h, oc=oc, p64=p64, pb=pb, sp=sp,
                          mglob=mglob, prevk2=prevk2):
                    if h == 0 and tl == 0:
                        t.wait_ge(sS, mm["vev"][(gi, 3)])
                    if prevk2 is not None:
                        t.wait_ge(sV, mm["atev"][prevk2])
                    i = None
                    for b in range(5):
                        kglob = mglob - 4 + b
                        mask = None
                        if mglob < 4 and b < 4:
                            mask = s_sm[:, (tl * 4 + b) * 128:(tl * 4 + b + 1) * 128]
                        elif b == 0:
                            mask = s_m0[:, :]
                        elif b == 4:
                            mask = s_m4[:, :]
                        slot = kglob % 8
                        rhs = kT_c[p64:p64 + 64,
                                   oc * 1024 + (slot // 4) * 512 + (slot % 4) * 128:
                                   oc * 1024 + (slot // 4) * 512 + (slot % 4) * 128 + 128]
                        lq = qT_f[pb][p64:p64 + 64, oc * 512 + tl * 128: oc * 512 + tl * 128 + 128]
                        if mask is not None:
                            t.matmul(psSim[sp][:, b * 128:(b + 1) * 128], s_idb[:, :], mask,
                                     start=True, stop=False)
                            i = t.matmul(psSim[sp][:, b * 128:(b + 1) * 128], lq, rhs,
                                         start=False, stop=True)
                        else:
                            i = t.matmul(psSim[sp][:, b * 128:(b + 1) * 128], lq, rhs,
                                         start=True, stop=True)
                    i.then_inc(sT, 1)
                P.tsec(t_sim)
                mm["sim"][(gi, tl, h)] = P.inc("sT")

                def v_max(v, gi=gi, tl=tl, h=h, sp=sp):
                    v.wait_ge(sT, mm["sim"][(gi, tl, h)])
                    v.tensor_reduce(negmax[:, :], psSim[sp][:, :], AX.X, ALU.max,
                                    negate=True).then_inc(sV, 1)
                P.vsec(v_max)
                vmax = P.inc("sV")

                def s_exp(s, vmax=vmax, gi=gi, tl=tl, h=h, sp=sp, prevk=prevk,
                          prevk2=prevk2):
                    s.wait_ge(sV, vmax)
                    if prevk2 is not None:
                        s.wait_ge(sT, mm["trp"][prevk2])
                    if prevk is not None:
                        s.wait_ge(sV, mm["scl"][prevk])
                    s.activation(attn_bf[sp][:, :], psSim[sp][:, :], AF.Exp, bias=negmax[:, 0:1],
                                 accum_out=rsum[:, 0:1]).then_inc(sS, 1)
                P.ssec(s_exp)
                mm["exp"][(gi, tl, h)] = P.inc("sS")

                def v_scl(v, gi=gi, tl=tl, h=h, sp=sp, prevk2=prevk2):
                    v.wait_ge(sS, mm["exp"][(gi, tl, h)])
                    if prevk2 is not None:
                        v.wait_ge(sT, mm["trp"][prevk2])
                    v.reciprocal(rrec[:, :], rsum[:, :])
                    v.drain()
                    v.tensor_scalar(dgr[sp][:, :], s_idb[:, :], rrec[:, 0:1], None,
                                    ALU.mult).then_inc(sV, 1)
                    v.drain()
                P.vsec(v_scl)
                mm["scl"][(gi, tl, h)] = P.inc("sV")

                if pending is not None:
                    emit_back(pending)
                pending = (gi, tl, h)
        emit_back(pending)
        pending = None

        for oc in range(NCH):
            op = oc % 2

            def t_wo(t, gi=gi, oc=oc):
                t.wait_ge(sS, mm["oev"][(gi, 3)])
                if oc == 0:
                    t.wait_ge(sS, mm["vev"][(gi, 3)])
                else:
                    t.wait_ge(sS, mm["woev"][(gi, oc - 1)])
                for pr in range(4):
                    i = t.matmul(psB[:, 0:512],
                                 s_wo[:, pr * 1024 + oc * 128: pr * 1024 + oc * 128 + 128],
                                 oT_g[:, pr * 512:(pr + 1) * 512],
                                 start=(pr == 0), stop=(pr == 3))
                i.then_inc(sT, 1)
            P.tsec(t_wo)
            mm["wo"][(gi, oc)] = P.inc("sT")

            def s_woev(s, gi=gi, oc=oc, op=op):
                s.wait_ge(sT, mm["wo"][(gi, oc)])
                if (gi, oc - 2) in mm["xadd"]:
                    s.wait_ge(sV, mm["xadd"][(gi, oc - 2)])
                elif (gi - 1, NCH - 2 + oc) in mm["xadd"]:
                    s.wait_ge(sV, mm["xadd"][(gi - 1, NCH - 2 + oc)])
                s.activation(tO[op][:, :], psB[:, 0:512], AF.Identity).then_inc(sS, 1)
            P.ssec(s_woev)
            mm["woev"][(gi, oc)] = P.inc("sS")
            if oc == NCH - 1:
                mm["woev"][gi] = mm["woev"][(gi, oc)]

            def v_add(v, gi=gi, oc=oc, op=op, pb=pb):
                v.wait_ge(sS, mm["woev"][(gi, oc)])
                if l == 0:
                    v.tensor_tensor(xt[pb][:, _chsl(oc)], uf[pb][:, _chsl(oc)],
                                    tO[op][:, :], ALU.add).then_inc(sV, 1)
                else:
                    v.tensor_tensor(xt[pb][:, _chsl(oc)], xt[pb][:, _chsl(oc)],
                                    tO[op][:, :], ALU.add).then_inc(sV, 1)
            P.vsec(v_add)
            mm["xadd"][(gi, oc)] = P.inc("sV")

        def g_store(g_, gi=gi, pb=pb):
            g_.wait_ge(sV, mm["xadd"][(gi, NCH - 1)])
            g_.dma_start(xout[:, :, gi * G:(gi + 1) * G].rearrange("c p t -> p c t"),
                         xt[pb][:, 0:TPC].rearrange("p (c t) -> p c t", c=NCH)
                         ).then_inc(qS[pb], 16)
        P.gsec(g_store)
        store_done[gi] = (qS[pb], P.inc("qS%d" % pb, 16))

    def g_fin(g_):
        g_.wait_ge(qS[0], P.cnt["qS0"])
        g_.wait_ge(qS[1], P.cnt["qS1"])
    P.gsec(g_fin)


    P.pop_scope()
    P.emit_barrier()


def build_l0attn(l):
    key = f"l0attn{l}"
    if key in _PROG_CACHE:
        return _PROG_CACHE[key]
    P = Prog(key)
    nc = P.nc
    ext = {}
    ext["xin"] = nc.dram_tensor("xin", [NCH, 128, TPC], F32, kind="ExternalInput")
    ext["xhalo"] = nc.dram_tensor("xhalo", [NCH, 128, SW], F32, kind="ExternalInput")
    ext["wqkv"] = nc.dram_tensor("wqkv", [DD, 1536], F32, kind="ExternalInput")
    ext["wo"] = nc.dram_tensor("wo", [512, DD], F32, kind="ExternalInput")
    ext["qscol"] = nc.dram_tensor("qscol", [128, 1], F32, kind="ExternalInput")
    ext["kscol"] = nc.dram_tensor("kscol", [128, 1], F32, kind="ExternalInput")
    ext["tqc"] = nc.dram_tensor("tqc", [128, 4608], BF16, kind="ExternalInput")
    ext["tqs"] = nc.dram_tensor("tqs", [128, 4608], BF16, kind="ExternalInput")
    ext["tkc"] = nc.dram_tensor("tkc", [128, 4608], BF16, kind="ExternalInput")
    ext["tks"] = nc.dram_tensor("tks", [128, 4608], BF16, kind="ExternalInput")
    ext["smask"] = nc.dram_tensor("smask", [128, 16 * 128], BF16, kind="ExternalInput")
    ext["cm0"] = nc.dram_tensor("cm0", [128, 128], BF16, kind="ExternalInput")
    ext["cm4"] = nc.dram_tensor("cm4", [128, 128], BF16, kind="ExternalInput")
    ext["cid128b"] = nc.dram_tensor("cid128b", [128, 128], BF16, kind="ExternalInput")
    ext["chs"] = nc.dram_tensor("chs", [128, 32], F32, kind="ExternalInput")
    ext["chsT"] = nc.dram_tensor("chsT", [8, 512], F32, kind="ExternalInput")
    ext["crot"] = nc.dram_tensor("crot", [128, 128], BF16, kind="ExternalInput")
    ext["xout"] = nc.dram_tensor("xout", [NCH, 128, TPC], F32, kind="ExternalOutput")
    P.std_sems()
    P.open()
    emit_l0attn(P, ext, l)
    P.close()
    _PROG_CACHE[key] = P
    return P


HID = 2730
HHALF = 1408

def _hchunks(half):
    if half == 0:
        return [(c * 128, 128) for c in range(11)]
    out = [(HHALF + c * 128, 128) for c in range(10)]
    out.append((HHALF + 1280, 42))
    return out

def emit_l0ffn(P, ext, final):
    nc = P.nc
    SW = 513
    xin = ext["xin"]
    xh1 = ext["xh1"]
    w1 = ext["w1"]
    w2 = ext["w2"]
    b1at = ext["b1at"]
    b1gt = ext["b1gt"]
    b2t = ext["b2t"]
    lgt = ext["lgt"]
    lbt = ext["lbt"]
    if final:
        xres = ext["xres"]
    xout = ext["xout"]

    P.push_scope()
    s_w1 = P.sb("s_w1", [128, NCH * 2816], BF16)
    s_w2 = P.sb("s_w2", [128, 11 * 1024], BF16)
    s_b1a = P.sb("s_b1a", [128, 22], F32)
    s_b1g = P.sb("s_b1g", [128, 22], F32)
    s_b2 = P.sb("s_b2", [128, 8], F32)
    s_lg = P.sb("s_lg", [128, 8], F32)
    s_lb = P.sb("s_lb", [128, 8], F32)
    ones_bf = P.sb("ones_bf", [128, 128], BF16)
    epscol = P.sb("epscol", [128, 1], F32)
    xth_s = P.sb("xth_s", [128, NCH * SW], F32)
    xth = [xth_s, xth_s]
    tOx = [P.sb(f"tOx{i}", [128, TPC], F32) for i in range(2)]
    xtr_s = P.sb("xtr_s", [128, TPC], F32) if final else None
    xtr = [xtr_s, xtr_s] if final else None
    x_bf = P.sb("xbf", [128, NCH * SW], BF16)
    x2_bf = P.sb("x2bf", [128, NCH * SW], BF16)
    h_bf = [P.sb(f"hbf{i}", [128, TPC], BF16) for i in range(2)]
    mid = P.sb("mid", [128, 11 * G], BF16)
    gscr = [P.sb(f"gscr{i}", [128, G], F32) for i in range(2)]
    tO = [P.sb(f"tO{i}", [128, G], F32) for i in range(2)]
    Bmu = P.sb("Bmu", [128, G], F32)
    Bvar = P.sb("Bvar", [128, G], F32)
    Brstd = P.sb("Brstd", [128, G], F32)
    tA = P.sb("tA", [128, G], F32)
    tB = P.sb("tB", [128, G], F32)

    psB = P.ps("psB", [128, 2 * G])
    psA = [P.ps(f"psA{i}", [128, G]) for i in range(2)]
    psG = [P.ps(f"psG{i}", [128, G]) for i in range(2)]
    psO = [P.ps(f"psO{i}", [128, G]) for i in range(2)]

    qL = [P.handles["qL0"], P.handles["qL1"]]
    qS = [P.handles["qS0"], P.handles["qS1"]]
    qW = P.handles["qW"]
    sG = P.handles["sG"]
    sT = P.handles["sT"]; sV = P.handles["sV"]; sS = P.handles["sS"]

    def g_pro(g_):
        for nm, dst in [(b1at, s_b1a), (b1gt, s_b1g), (b2t, s_b2), (lgt, s_lg), (lbt, s_lb)]:
            g_.dma_start(dst[:, :], nm[:, :]).then_inc(qW, 16)
        g_.memset(ones_bf[:, :], 1.0)
        g_.memset(epscol[:, :], 1e-5).then_inc(sG, 1)
    P.gsec(g_pro)
    bready = P.inc("qW", 5 * 16)
    gready = P.inc("sG")

    load_done = {}
    store_done = {}
    uapply = {}
    wready = {}
    mm = {"amm": {}, "aev": {}, "gmm": {}, "gev": {}, "midv": {}, "w2": {},
          "oev": {}, "xadd": {}, "fin": {}}

    def shoff(kc, base):
        return base + (1 if kc < 4 else 0)

    for half in range(2):
        chunks = _hchunks(half)

        hw = 1408 if half == 0 else 1322

        def g_w(g_, half=half, hw=hw, chunks=chunks):
            if half == 1:
                g_.wait_ge(P.handles["sT"], mm["w2"][(0, NG - 1, NCH - 1)])
            for a in range(NCH):
                g_.dma_start(s_w1[:, a * 2816: a * 2816 + hw],
                             w1[a * 128:(a + 1) * 128, half * HHALF: half * HHALF + hw]
                             ).then_inc(qW, 16)
                g_.dma_start(s_w1[:, a * 2816 + 1408: a * 2816 + 1408 + hw],
                             w1[a * 128:(a + 1) * 128, HID + half * HHALF: HID + half * HHALF + hw]
                             ).then_inc(qW, 16)
            for c, (off, w) in enumerate(chunks):
                g_.dma_start(s_w2[0:w, c * 1024:(c + 1) * 1024],
                             w2[off: off + w, :]).then_inc(qW, 16)
        P.gsec(g_w)
        wready[half] = P.inc("qW", (2 * NCH + len(chunks)) * 16)

        for g in range(NG):
            pb = g % 2

            def y_load(y, half=half, g=g, pb=pb):
                if (half, g - 2) in store_done:
                    y.wait_ge(*store_done[(half, g - 2)])
                elif half == 1:
                    y.wait_ge(*store_done[(0, NG - 2 + g)])
                if (half, g - 1, NCH - 1) in mm["xadd"]:
                    y.wait_ge(sV, mm["xadd"][(half, g - 1, NCH - 1)])
                elif half == 1 and (0, NG - 1, NCH - 1) in mm["xadd"]:
                    y.wait_ge(sV, mm["xadd"][(0, NG - 1, NCH - 1)])
                if final and half == 1 and (1, g - 1) in store_done:
                    y.wait_ge(*store_done[(1, g - 1)])
                src = xin if half == 0 else xout
                if g == 0:
                    y.dma_start(xth[pb][:, 0:NCH * SW].rearrange("p (c t) -> p c t", t=SW)[:, :, 0:1],
                                xh1[:, :, :].rearrange("c p t -> p c t"),
                                allow_slow_non_contiguous=True).then_inc(qL[pb], 16)
                    y.dma_start(xth[pb][:, :].rearrange("p (c t) -> p c t", t=SW)[:, :, 1:SW],
                                xin[:, :, 0:G].rearrange("c p t -> p c t")).then_inc(qL[pb], 16)
                else:
                    y.dma_start(xth[pb][:, :].rearrange("p (c t) -> p c t", t=SW),
                                xin[:, :, g * G - 1:(g + 1) * G].rearrange("c p t -> p c t")
                                ).then_inc(qL[pb], 16)
                if half == 1:
                    y.wait_ge(*store_done[(0, g)])
                    y.dma_start(tOx[pb][:, :].rearrange("p (c t) -> p c t", c=NCH),
                                xout[:, :, g * G:(g + 1) * G].rearrange("c p t -> p c t")
                                ).then_inc(qL[pb], 16)
                if final and half == 1:
                    y.dma_start(xtr[pb][:, :].rearrange("p (c t) -> p c t", c=NCH),
                                xres[:, :, g * G:(g + 1) * G].rearrange("c p t -> p c t")
                                ).then_inc(qL[pb], 16)
            P.ysec(y_load)
            nload = (2 if g == 0 else 1) + (1 if half == 1 else 0) + (1 if (final and half == 1) else 0)
            load_done[(half, g)] = (qL[pb], P.inc("qL%d" % pb, 16 * nload))

            def v_wait(v, half=half, g=g):
                v.wait_ge(*load_done[(half, g)])
                v.wait_ge(qW, wready[half])
                v.wait_ge(sG, gready)
                key2 = (half, g - 2) if g >= 2 else ((0, NG - 2 + g) if half == 1 else None)
                if key2 is not None:
                    v.wait_ge(sT, mm["amm"][(key2[0], key2[1], 10)])
            P.vsec(v_wait)

            bufs = dict(xt=xth[pb], x_bf=x_bf, x2_bf=x2_bf, u_bf=h_bf[pb],
                        Bmu=Bmu, Bvar=Bvar, Brstd=Brstd, tA=tA, tB=tB,
                        ones_bf=ones_bf, psB=psB, epscol=epscol,
                        lg=s_lg, lb=s_lb)
            uapply[(half, g)] = emit_ln_group(P, bufs, 512, 0,
                                             soff=lambda ch: (1 if ch < 4 else 0),
                                             sstride=SW, doff=0, dstride=512,
                                             affine=True)

            for c, (off, w) in enumerate(chunks):
                woff = off - half * HHALF
                cp = c % 2

                def prev_ag(cb):
                    if (half, g, c - cb) in mm["gev"]:
                        return (half, g, c - cb)
                    if (half, g - 1, 11 - cb + c) in mm["gev"]:
                        return (half, g - 1, 11 - cb + c)
                    if half == 1 and (0, NG - 1, 11 - cb + c) in mm["gev"]:
                        return (0, NG - 1, 11 - cb + c)
                    return None
                pk2 = prev_ag(2)

                def t_ag(t, half=half, g=g, c=c, w=w, woff=woff, pb=pb, cp=cp, pk2=pk2):
                    if c == 0:
                        t.wait_ge(sV, uapply[(half, g)])
                    if pk2 is not None:
                        t.wait_ge(sS, mm["gev"][pk2])
                        t.wait_ge(sV, mm["midv"][pk2])
                    for kc in range(NCH):
                        t.matmul(psA[cp][0:w, :],
                                 s_w1[:, kc * 2816 + woff: kc * 2816 + woff + w],
                                 h_bf[pb][:, _chsl(kc)],
                                 start=(kc == 0), stop=(kc == NCH - 1))
                    for kc in range(NCH):
                        i = t.matmul(psG[cp][0:w, :],
                                     s_w1[:, kc * 2816 + 1408 + woff: kc * 2816 + 1408 + woff + w],
                                     h_bf[pb][:, _chsl(kc)],
                                     start=(kc == 0), stop=(kc == NCH - 1))
                    i.then_inc(sT, 1)
                P.tsec(t_ag)
                mm["amm"][(half, g, c)] = tag = P.inc("sT")
                pk1 = prev_ag(1)

                def s_ag(s, half=half, g=g, c=c, w=w, tag=tag, cp=cp, pk2=pk2):
                    s.wait_ge(sT, tag)
                    if pk2 is not None:
                        s.wait_ge(sV, mm["midv"][pk2])
                    bcol = half * 11 + c
                    s.activation(gscr[cp][0:w, :], psG[cp][0:w, :], AF.Gelu, bias=s_b1g[0:w, bcol:bcol + 1]).then_inc(sS, 1)
                P.ssec(s_ag)
                mm["aev"][(half, g, c)] = mm["gev"][(half, g, c)] = P.inc("sS")

                def v_mid(v, half=half, g=g, c=c, w=w, cp=cp):
                    v.wait_ge(sS, mm["aev"][(half, g, c)])
                    if (half, g - 1, c) in mm["w2"]:
                        v.wait_ge(sT, mm["w2"][(half, g - 1, NCH - 1)])
                    elif half == 1 and c == 0 and g == 0:
                        v.wait_ge(sT, mm["w2"][(0, NG - 1, NCH - 1)])
                    bcol = half * 11 + c
                    v.scalar_tensor_tensor(mid[0:w, c * G:(c + 1) * G], psA[cp][0:w, :],
                                           s_b1a[0:w, bcol:bcol + 1], gscr[cp][0:w, :],
                                           ALU.add, ALU.mult).then_inc(sV, 1)
                P.vsec(v_mid)
                mm["midv"][(half, g, c)] = P.inc("sV")

            for oc in range(NCH):
                op = oc % 2

                def t_w2(t, half=half, g=g, oc=oc, op=op, chunks=chunks):
                    t.wait_ge(sV, mm["midv"][(half, g, 10)])
                    if oc >= 2:
                        t.wait_ge(sV, mm["xadd"][(half, g, oc - 2)])
                    elif (half, g - 1, NCH - 2 + oc) in mm["xadd"]:
                        t.wait_ge(sV, mm["xadd"][(half, g - 1, NCH - 2 + oc)])
                    elif half == 1 and (0, NG - 1, NCH - 2 + oc) in mm["xadd"]:
                        t.wait_ge(sV, mm["xadd"][(0, NG - 1, NCH - 2 + oc)])
                    for c, (off, w) in enumerate(chunks):
                        i = t.matmul(psO[op][:, :],
                                     s_w2[0:w, c * 1024 + oc * 128: c * 1024 + oc * 128 + 128],
                                     mid[0:w, c * G:(c + 1) * G],
                                     start=(c == 0), stop=(c == len(chunks) - 1))
                    i.then_inc(sT, 1)
                P.tsec(t_w2)
                mm["w2"][(half, g, oc)] = P.inc("sT")

                def v_add(v, half=half, g=g, oc=oc, op=op, pb=pb):
                    v.wait_ge(sT, mm["w2"][(half, g, oc)])
                    if (half, g - 2) in store_done:
                        v.wait_ge(*store_done[(half, g - 2)])
                    elif half == 1 and (0, NG - 2 + g) in store_done:
                        v.wait_ge(*store_done[(0, NG - 2 + g)])
                    if half == 0:
                        v.scalar_tensor_tensor(
                            tOx[pb][:, _chsl(oc)], psO[op][:, :], s_b2[:, oc:oc + 1],
                            xth[pb][:, :].rearrange("p (c t) -> p c t", t=SW)[:, oc, 1:SW].rearrange("p t -> p t"),
                            ALU.add, ALU.add).then_inc(sV, 1)
                    else:
                        i = v.tensor_tensor(tOx[pb][:, _chsl(oc)], tOx[pb][:, _chsl(oc)],
                                            psO[op][:, :], ALU.add)
                        if final:
                            v.drain()
                            i = v.scalar_tensor_tensor(xtr[pb][:, _chsl(oc)],
                                                       tOx[pb][:, _chsl(oc)], 0.5,
                                                       xtr[pb][:, _chsl(oc)],
                                                       ALU.mult, ALU.add)
                        i.then_inc(sV, 1)
                        if final:
                            v.drain()
                P.vsec(v_add)
                mm["xadd"][(half, g, oc)] = P.inc("sV")

            def g_store(g_, half=half, g=g, pb=pb):
                g_.wait_ge(sV, mm["xadd"][(half, g, NCH - 1)])
                src = xtr[pb] if (final and half == 1) else tOx[pb]
                g_.dma_start(xout[:, :, g * G:(g + 1) * G].rearrange("c p t -> p c t"),
                             src[:, :].rearrange("p (c t) -> p c t", c=NCH)
                             ).then_inc(qS[pb], 16)
            P.gsec(g_store)
            store_done[(half, g)] = (qS[pb], P.inc("qS%d" % pb, 16))

    def g_fin(g_):
        g_.wait_ge(qS[0], P.cnt["qS0"])
        g_.wait_ge(qS[1], P.cnt["qS1"])
    P.gsec(g_fin)


    P.pop_scope()
    P.emit_barrier()


def build_l0ffn(final):
    key = f"l0ffn{int(final)}"
    if key in _PROG_CACHE:
        return _PROG_CACHE[key]
    P = Prog(key)
    nc = P.nc
    ext = {}
    ext["xin"] = nc.dram_tensor("xin", [NCH, 128, TPC], F32, kind="ExternalInput")
    ext["xh1"] = nc.dram_tensor("xh1", [NCH, 128, 1], F32, kind="ExternalInput")
    ext["w1"] = nc.dram_tensor("w1", [DD, 2 * HID], F32, kind="ExternalInput")
    ext["w2"] = nc.dram_tensor("w2", [HID, DD], F32, kind="ExternalInput")
    ext["b1at"] = nc.dram_tensor("b1at", [128, 22], F32, kind="ExternalInput")
    ext["b1gt"] = nc.dram_tensor("b1gt", [128, 22], F32, kind="ExternalInput")
    ext["b2t"] = nc.dram_tensor("b2t", [128, 8], F32, kind="ExternalInput")
    ext["lgt"] = nc.dram_tensor("lgt", [128, 8], F32, kind="ExternalInput")
    ext["lbt"] = nc.dram_tensor("lbt", [128, 8], F32, kind="ExternalInput")
    if final:
        ext["xres"] = nc.dram_tensor("xres", [NCH, 128, TPC], F32, kind="ExternalInput")
    ext["xout"] = nc.dram_tensor("xout", [NCH, 128, TPC], F32, kind="ExternalOutput")
    P.std_sems()
    P.open()
    emit_l0ffn(P, ext, final)
    P.close()
    _PROG_CACHE[key] = P
    return P


def emit_ssm_p1(P, ext):
    nc = P.nc
    xin = ext["xin"]
    wqT = ext["wqT"]
    wkT = ext["wkT"]
    waT = ext["waT"]
    wvT = ext["wvT"]
    abt = ext["abt"]
    cut = ext["cut"]
    chs = ext["chs"]
    cnhs = ext["cnhs"]
    cid8 = ext["cid8"]
    cid8b = ext["cid8b"]
    sloc = ext["sloc"]
    abT_out = ext["abT_out"]

    P.push_scope()
    swq = P.sb("swq", [128, NCH * 512], BF16)
    swk = P.sb("swk", [128, NCH * 512], BF16)
    swa = P.sb("swa", [128, NCH * 512], BF16)
    swv = P.sb("swv", [128, NCH * 1024], BF16)
    s_abt = P.sb("s_abt", [128, 4], F32)
    s_ut = P.sb("s_ut", [128, 128], F32)
    s_hs = P.sb("s_hs", [128, 32], F32)
    s_nhs = P.sb("s_nhs", [128, 32], F32)
    s_id8 = P.sb("s_id8", [8, 8], F32)
    s_id8b = P.sb("s_id8b", [8, 8], BF16)
    ones_bf = P.sb("ones_bf", [128, 128], BF16)
    ones_col = P.sb("ones_col", [128, 1], BF16)
    onecol_f = P.sb("onecol_f", [128, 1], F32)
    epscol = P.sb("epscol", [128, 1], F32)
    xt = [P.sb(f"xt{i}", [128, TPC], F32) for i in range(2)]
    x_bf = P.sb("xbf", [128, TPC], BF16)
    x2_bf = P.sb("x2bf", [128, TPC], BF16)
    u_bf = [P.sb(f"ubf{i}", [128, TPC], BF16) for i in range(2)]
    qf = P.sb("qf", [128, G], F32)
    tqk = P.sb("tqk", [128, G], F32)
    r_sp = P.sb("r_sp", [128, G], F32)
    aab = P.sb("aab", [128, G], F32)
    e_sp = P.sb("e_sp", [128, G], F32)
    l_sp = P.sb("l_sp", [128, G], F32)
    sp = P.sb("sp", [128, G], F32)
    v_bf = P.sb("v_bf", [128, 1024], BF16)
    aT_s = P.sb("aT_s", [8, TPC], F32)
    bT_s = P.sb("bT_s", [8, TPC], F32)
    aTok = P.sb("aTok", [128, 8], F32)
    ATr = P.sb("ATr", [8, 128], F32)
    bmA_T = P.sb("bmA_T", [8, 128], F32)
    wT_bf = P.sb("wT_bf", [8, 128], BF16)
    w_bf = P.sb("w_bf", [128, 8], BF16)
    eal = P.sb("eal", [8, 1], F32)
    diagE = P.sb("diagE", [8, 8], F32)
    S_sb = P.sb("S_sb", [8, 1024], F32)
    Z_sb = P.sb("Z_sb", [8, 1], F32)
    logD = P.sb("logD", [8, 1], F32)
    outbuf = P.sb("outbuf", [8, 1026], F32)
    Bmu = P.sb("Bmu", [128, G], F32)
    Bvar = P.sb("Bvar", [128, G], F32)
    Brstd = P.sb("Brstd", [128, G], F32)
    tA = P.sb("tA", [128, G], F32)
    tB = P.sb("tB", [128, G], F32)

    psB = P.ps("psB", [128, 2 * G])
    psQ = P.ps("psQ", [128, G])
    psK = P.ps("psK", [128, G])
    psSh = P.ps("psSh", [8, 1024])
    psM = P.ps("psM", [128, 138])
    psW = P.ps("psW", [128, 8], BF16)

    qL = [P.handles["qL0"], P.handles["qL1"]]
    qW = P.handles["qW"]
    sG = P.handles["sG"]
    sT = P.handles["sT"]; sV = P.handles["sV"]; sS = P.handles["sS"]

    def g_pro(g_):
        for a in range(NCH):
            g_.dma_start(swq[:, a * 512:(a + 1) * 512], wqT[a * 128:(a + 1) * 128, :]).then_inc(qW, 16)
            g_.dma_start(swk[:, a * 512:(a + 1) * 512], wkT[a * 128:(a + 1) * 128, :]).then_inc(qW, 16)
            g_.dma_start(swa[:, a * 512:(a + 1) * 512], waT[a * 128:(a + 1) * 128, :]).then_inc(qW, 16)
            g_.dma_start(swv[:, a * 1024:(a + 1) * 1024], wvT[a * 128:(a + 1) * 128, :]).then_inc(qW, 16)
        for nm, dst in [(abt, s_abt), (cut, s_ut), (chs, s_hs), (cnhs, s_nhs),
                        (cid8, s_id8), (cid8b, s_id8b)]:
            g_.dma_start(dst[:, :], nm[:, :]).then_inc(qW, 16)
        g_.memset(ones_bf[:, :], 1.0)
        g_.memset(ones_col[:, :], 1.0)
        g_.memset(onecol_f[:, :], 1.0)
        g_.memset(epscol[:, :], 1e-5)
        g_.memset(S_sb[:, :], 0.0)
        g_.memset(Z_sb[:, :], 0.0)
        g_.memset(logD[:, :], 0.0).then_inc(sG, 1)
    P.gsec(g_pro)
    wready = P.inc("qW", 38 * 16)
    gready = P.inc("sG")

    load_done = {}
    uapply = {}
    mile = {"qkev": {}, "tqk": {}, "bmm": {}, "amm": {}, "spv": {}, "spev": {},
            "vev": {}, "scan": {}, "batev": {}, "vmm": {}, "sscan": {}}

    for g in range(NG):
        pb = g % 2

        def y_load(y, g=g, pb=pb):
            if g >= 2:
                y.wait_ge(sV, uapply[g - 2])
            y.dma_start(xt[pb][:, :].rearrange("p (c t) -> p c t", c=NCH),
                        xin[:, :, g * G:(g + 1) * G].rearrange("c p t -> p c t")
                        ).then_inc(qL[g % 2], 16)
        P.ysec(y_load)
        load_done[g] = (qL[g % 2], P.inc("qL%d" % (g % 2), 16))

        def v_wait(v, g=g):
            v.wait_ge(*load_done[g])
            v.wait_ge(qW, wready)
            v.wait_ge(sG, gready)
            if g >= 2:
                v.wait_ge(sT, mile["vmm"][(g - 2, 3)])
        P.vsec(v_wait)

        bufs = dict(xt=xt[pb], x_bf=x_bf, x2_bf=x2_bf, u_bf=u_bf[pb],
                    Bmu=Bmu, Bvar=Bvar, Brstd=Brstd, tA=tA, tB=tB,
                    ones_bf=ones_bf, psB=psB, epscol=epscol)
        uapply[g] = emit_ln_group(P, bufs, G, 0)

        for oc in range(4):
            def t_qk(t, g=g, oc=oc, pb=pb):
                if oc == 0:
                    t.wait_ge(sV, uapply[g])
                    if g >= 1:
                        t.wait_ge(sS, mile["sscan"][(g - 1, 3)])
                else:
                    t.wait_ge(sS, mile["qkev"][(g, oc - 1)])
                    t.wait_ge(sV, mile["tqk"][(g, oc - 1)])
                for kc in range(NCH):
                    t.matmul(psQ[:, :], swq[:, kc * 512 + oc * 128: kc * 512 + oc * 128 + 128],
                             u_bf[pb][:, _chsl(kc)], start=(kc == 0), stop=(kc == NCH - 1))
                for kc in range(NCH):
                    i = t.matmul(psK[:, :], swk[:, kc * 512 + oc * 128: kc * 512 + oc * 128 + 128],
                                 u_bf[pb][:, _chsl(kc)], start=(kc == 0), stop=(kc == NCH - 1))
                i.then_inc(sT, 1)
            P.tsec(t_qk)
            tqkmm = P.inc("sT")

            def s_qev(s, tqkmm=tqkmm):
                s.wait_ge(sT, tqkmm)
                s.activation(qf[:, :], psQ[:, :], AF.Identity).then_inc(sS, 1)
            P.ssec(s_qev)
            mile["qkev"][(g, oc)] = P.inc("sS")

            def v_tqk(v, g=g, oc=oc):
                v.wait_ge(sS, mile["qkev"][(g, oc)])
                if (g, oc - 1) in mile["bmm"]:
                    v.wait_ge(sT, mile["bmm"][(g, oc - 1)])
                v.tensor_tensor(tqk[:, :], qf[:, :], psK[:, :], ALU.mult).then_inc(sV, 1)
            P.vsec(v_tqk)
            mile["tqk"][(g, oc)] = P.inc("sV")

            def t_bmm(t, g=g, oc=oc):
                t.wait_ge(sV, mile["tqk"][(g, oc)])
                t.matmul(psSh[:, 0:512], s_hs[:, oc * 8:(oc + 1) * 8], tqk[:, :],
                         start=(oc == 0), stop=(oc == 3)).then_inc(sT, 1)
            P.tsec(t_bmm)
            mile["bmm"][(g, oc)] = P.inc("sT")

            def t_amm(t, g=g, oc=oc, pb=pb):
                t.wait_ge(sS, mile["qkev"][(g, oc)])
                if (g, oc - 1) in mile["spev"]:
                    t.wait_ge(sS, mile["spev"][(g, oc - 1)])
                for kc in range(NCH):
                    i = t.matmul(psQ[:, :], swa[:, kc * 512 + oc * 128: kc * 512 + oc * 128 + 128],
                                 u_bf[pb][:, _chsl(kc)], start=(kc == 0), stop=(kc == NCH - 1))
                i.then_inc(sT, 1)
            P.tsec(t_amm)
            tamm = P.inc("sT")

            def s_sp(s, g=g, oc=oc, tamm=tamm):
                s.wait_ge(sT, tamm)
                if (g, oc - 1) in mile["spv"]:
                    s.wait_ge(sV, mile["spv"][(g, oc - 1)])
                s.activation(r_sp[:, :], psQ[:, :], AF.Relu, bias=s_abt[:, oc:oc + 1])
                s.activation(aab[:, :], psQ[:, :], AF.Abs, bias=s_abt[:, oc:oc + 1])
                s.drain()
                s.activation(e_sp[:, :], aab[:, :], AF.Exp, scale=-1.0)
                s.drain()
                s.activation(l_sp[:, :], e_sp[:, :], AF.Ln, bias=onecol_f[:, 0:1]).then_inc(sS, 1)
            P.ssec(s_sp)
            mile["spev"][(g, oc)] = P.inc("sS")

            def v_sp(v, g=g, oc=oc):
                v.wait_ge(sS, mile["spev"][(g, oc)])
                if (g, oc - 1) in mile["amm"]:
                    v.wait_ge(sT, mile["amm"][(g, oc - 1)])
                v.tensor_tensor(sp[:, :], r_sp[:, :], l_sp[:, :], ALU.add).then_inc(sV, 1)
            P.vsec(v_sp)
            mile["spv"][(g, oc)] = P.inc("sV")

            def t_arow(t, g=g, oc=oc):
                t.wait_ge(sV, mile["spv"][(g, oc)])
                t.matmul(psSh[:, 512:1024], s_nhs[:, oc * 8:(oc + 1) * 8], sp[:, :],
                         start=(oc == 0), stop=(oc == 3)).then_inc(sT, 1)
            P.tsec(t_arow)
            mile["amm"][(g, oc)] = P.inc("sT")

        def s_bat(s, g=g):
            s.wait_ge(sT, mile["amm"][(g, 3)])
            s.activation(bT_s[:, g * G:(g + 1) * G], psSh[:, 0:512], AF.Identity, scale=0.125)
            s.activation(aT_s[:, g * G:(g + 1) * G], psSh[:, 512:1024], AF.Identity).then_inc(sS, 1)
        P.ssec(s_bat)
        mile["batev"][g] = P.inc("sS")

        for tl in range(4):
            def t_v(t, g=g, tl=tl, pb=pb):
                if tl == 0:
                    t.wait_ge(sV, mile["tqk"][(g, 3)])
                    t.wait_ge(sS, mile["spev"][(g, 3)])
                else:
                    t.wait_ge(sS, mile["vev"][(g, tl - 1)])
                for hh in range(2):
                    for kc in range(NCH):
                        i = t.matmul(psK[:, :] if hh == 0 else psQ[:, :],
                                     u_bf[pb][:, kc * G + tl * 128: kc * G + tl * 128 + 128],
                                     swv[:, kc * 1024 + hh * 512: kc * 1024 + hh * 512 + 512],
                                     start=(kc == 0), stop=(kc == NCH - 1))
                i.then_inc(sT, 1)
            P.tsec(t_v)
            mile["vmm"][(g, tl)] = P.inc("sT")

            def s_vev(s, g=g, tl=tl):
                s.wait_ge(sT, mile["vmm"][(g, tl)])
                if (g, tl - 1) in mile["scan"]:
                    s.wait_ge(sT, mile["scan"][(g, tl - 1)])
                elif g >= 1 and tl == 0:
                    s.wait_ge(sT, mile["scan"][(g - 1, 3)])
                s.activation(v_bf[:, 0:512], psK[:, :], AF.Identity)
                s.activation(v_bf[:, 512:1024], psQ[:, :], AF.Identity).then_inc(sS, 1)
            P.ssec(s_vev)
            mile["vev"][(g, tl)] = P.inc("sS")

            sl = slice(g * G + tl * 128, g * G + tl * 128 + 128)

            def t_tr1(t, g=g, tl=tl, sl=sl):
                t.wait_ge(sS, mile["batev"][g])
                key2 = (g, tl - 1) if tl >= 1 else ((g - 1, 3) if g >= 1 else None)
                if key2 is not None:
                    t.wait_ge(sS, mile["sscan"][key2])
                t.transpose(psM[:, 0:8], aT_s[:, sl], s_id8[:, :]).then_inc(sT, 1)
            P.tsec(t_tr1)
            ttr1 = P.inc("sT")

            def s_tr1(s, ttr1=ttr1):
                s.wait_ge(sT, ttr1)
                s.activation(aTok[:, :], psM[:, 0:8], AF.Identity).then_inc(sS, 1)
            P.ssec(s_tr1)
            str1 = P.inc("sS")

            def t_pre(t, str1=str1):
                t.wait_ge(sS, str1)
                t.matmul(psM[0:8, 8:136], aTok[:, :], s_ut[:, :], start=True, stop=True).then_inc(sT, 1)
            P.tsec(t_pre)
            tpre = P.inc("sT")

            def s_pre(s, tpre=tpre):
                s.wait_ge(sT, tpre)
                s.activation(ATr[:, :], psM[0:8, 8:136], AF.Identity).then_inc(sS, 1)
            P.ssec(s_pre)
            spre = P.inc("sS")

            def v_bma(v, g=g, sl=sl, spre=spre):
                v.wait_ge(sS, spre)
                v.tensor_tensor(bmA_T[:, :], bT_s[:, sl], ATr[:, :], ALU.subtract).then_inc(sV, 1)
            P.vsec(v_bma)
            vbma = P.inc("sV")

            def s_w(s, vbma=vbma):
                s.wait_ge(sV, vbma)
                s.activation(wT_bf[:, :], bmA_T[:, :], AF.Exp, bias=ATr[:, 127:128])
                s.drain()
                s.activation(eal[:, :], ATr[:, 127:128], AF.Exp).then_inc(sS, 1)
            P.ssec(s_w)
            sw_ = P.inc("sS")

            def t_trw(t, sw_=sw_):
                t.wait_ge(sS, sw_)
                t.transpose(psW[:, :], wT_bf[:, :], s_id8b[:, :]).then_inc(sT, 1)
            P.tsec(t_trw)
            ttrw = P.inc("sT")

            def s_trw(s, ttrw=ttrw):
                s.wait_ge(sT, ttrw)
                s.activation(w_bf[:, :], psW[:, :], AF.Identity).then_inc(sS, 1)
            P.ssec(s_trw)
            strw = P.inc("sS")

            def v_diag(v, g=g, tl=tl, strw=strw):
                v.wait_ge(sS, strw)
                key2 = (g, tl - 1) if tl >= 1 else ((g - 1, 3) if g >= 1 else None)
                if key2 is not None:
                    v.wait_ge(sT, mile["scan"][key2])
                v.tensor_scalar(diagE[:, :], s_id8[:, :], eal[:, 0:1], None, ALU.mult)
                v.drain()
                v.tensor_tensor(logD[:, :], logD[:, :], ATr[:, 127:128], ALU.add).then_inc(sV, 1)
                v.drain()
            P.vsec(v_diag)
            vdiag = P.inc("sV")

            def t_scan(t, g=g, tl=tl, vdiag=vdiag):
                t.wait_ge(sV, vdiag)
                t.wait_ge(sS, mile["vev"][(g, tl)])
                t.matmul(psSh[:, 0:512], w_bf[:, :], v_bf[:, 0:512], start=True, stop=False)
                t.matmul(psSh[:, 0:512], diagE[:, :], S_sb[:, 0:512], start=False, stop=True)
                t.matmul(psSh[:, 512:1024], w_bf[:, :], v_bf[:, 512:1024], start=True, stop=False)
                t.matmul(psSh[:, 512:1024], diagE[:, :], S_sb[:, 512:1024], start=False, stop=True)
                t.matmul(psM[0:8, 136:137], w_bf[:, :], ones_col[:, :], start=True, stop=False)
                i = t.matmul(psM[0:8, 136:137], diagE[:, :], Z_sb[:, :], start=False, stop=True)
                i.then_inc(sT, 1)
            P.tsec(t_scan)
            mile["scan"][(g, tl)] = P.inc("sT")

            def s_scan(s, g=g, tl=tl):
                s.wait_ge(sT, mile["scan"][(g, tl)])
                s.activation(S_sb[:, :], psSh[:, :], AF.Identity)
                s.activation(Z_sb[:, :], psM[0:8, 136:137], AF.Identity).then_inc(sS, 1)
            P.ssec(s_scan)
            mile["sscan"][(g, tl)] = sscan = P.inc("sS")

            def t_scwait(t, g=g, tl=tl, sscan=sscan):
                t.wait_ge(sS, sscan)
            P.tsec(t_scwait)

    def v_fin(v):
        v.wait_ge(sS, mile["sscan"][(NG - 1, 3)])
        v.tensor_copy(outbuf[:, 0:1024], S_sb[:, :])
        v.tensor_copy(outbuf[:, 1024:1025], Z_sb[:, :])
        v.tensor_copy(outbuf[:, 1025:1026], logD[:, :]).then_inc(sV, 1)
    P.vsec(v_fin)
    vfin = P.inc("sV")

    def g_fin(g_):
        g_.wait_ge(sV, vfin)
        g_.dma_start(sloc[:, :], outbuf[:, :]).then_inc(qW, 16)
        g_.wait_ge(sS, mile["batev"][NG - 1])
        g_.dma_start(abT_out[0:8, :], aT_s[:, :]).then_inc(qW, 16)
        g_.dma_start(abT_out[8:16, :], bT_s[:, :]).then_inc(qW, 16)
        g_.wait_ge(qW, P.cnt["qW"] + 48)
    P.gsec(g_fin)
    P.inc("qW", 48)


    P.pop_scope()
    P.emit_barrier()


def build_ssm_p1():
    key = "ssm1"
    if key in _PROG_CACHE:
        return _PROG_CACHE[key]
    P = Prog(key)
    nc = P.nc
    ext = {}
    ext["xin"] = nc.dram_tensor("xin", [NCH, 128, TPC], F32, kind="ExternalInput")
    ext["wqT"] = nc.dram_tensor("wqT", [DD, 512], F32, kind="ExternalInput")
    ext["wkT"] = nc.dram_tensor("wkT", [DD, 512], F32, kind="ExternalInput")
    ext["waT"] = nc.dram_tensor("waT", [DD, 512], F32, kind="ExternalInput")
    ext["wvT"] = nc.dram_tensor("wvT", [DD, DD], F32, kind="ExternalInput")
    ext["abt"] = nc.dram_tensor("abt", [128, 4], F32, kind="ExternalInput")
    ext["cut"] = nc.dram_tensor("cut", [128, 128], F32, kind="ExternalInput")
    ext["chs"] = nc.dram_tensor("chs", [128, 32], F32, kind="ExternalInput")
    ext["cnhs"] = nc.dram_tensor("cnhs", [128, 32], F32, kind="ExternalInput")
    ext["cid8"] = nc.dram_tensor("cid8", [8, 8], F32, kind="ExternalInput")
    ext["cid8b"] = nc.dram_tensor("cid8b", [8, 8], BF16, kind="ExternalInput")
    ext["sloc"] = nc.dram_tensor("sloc", [8, 1026], F32, kind="ExternalOutput")
    ext["abT_out"] = nc.dram_tensor("abT_out", [16, TPC], F32, kind="ExternalOutput")
    P.std_sems()
    P.open()
    emit_ssm_p1(P, ext)
    P.close()
    _PROG_CACHE[key] = P
    return P


def emit_ssm_p2(P, ext):
    nc = P.nc
    xin = ext["xin"]
    abT_in = ext["abT_in"]
    slocall = ext["slocall"]
    psel = ext["psel"]
    wvT = ext["wvT"]
    woT = ext["woT"]
    cut = ext["cut"]
    cutb = ext["cutb"]
    cdsel = ext["cdsel"]
    cid8 = ext["cid8"]
    cid128 = ext["cid128"]
    cid128b = ext["cid128b"]
    clneg = ext["clneg"]
    xout = ext["xout"]

    P.push_scope()
    swv = P.sb("swv", [128, NCH * 1024], BF16)
    swo = P.sb("swo", [128, NCH * 1024], BF16)
    s_sl = P.sb("s_sl", [8, 4 * 1026], F32)
    s_ps = P.sb("s_ps", [8, 3], F32)
    s_ut = P.sb("s_ut", [128, 128], F32)
    s_utb = P.sb("s_utb", [128, 128], BF16)
    s_ds = P.sb("s_ds", [8, 1024], F32)
    s_id8 = P.sb("s_id8", [8, 8], F32)
    s_id128 = P.sb("s_id128", [128, 128], F32)
    s_id128b = P.sb("s_id128b", [128, 128], BF16)
    s_lneg = P.sb("s_lneg", [128, 128], BF16)
    ones_bf = P.sb("ones_bf", [128, 128], BF16)
    ones_col = P.sb("ones_col", [128, 1], BF16)
    ones8 = P.sb("ones8", [8, 128], F32)
    epscol = P.sb("epscol", [128, 1], F32)
    aT_s = P.sb("aT_s", [8, TPC], F32)
    bT_s = P.sb("bT_s", [8, TPC], F32)
    xt = [P.sb(f"xt{i}", [128, TPC], F32) for i in range(2)]
    x_bf = P.sb("xbf", [128, TPC], BF16)
    x2_bf = P.sb("x2bf", [128, TPC], BF16)
    u_bf = [P.sb(f"ubf{i}", [128, TPC], BF16) for i in range(2)]
    v_bf = P.sb("v_bf", [128, 1024], BF16)
    yTg = [P.sb(f"yTg{i}", [128, NCH * G], BF16) for i in range(2)]
    Scr8 = P.sb("Scr8", [8, 1024], F32)
    aTok = P.sb("aTok", [128, 8], F32)
    Atok = P.sb("Atok", [128, 8], F32)
    eA = P.sb("eA", [128, 8], F32)
    ATr = P.sb("ATr", [8, 128], F32)
    eATr = P.sb("eATr", [8, 128], F32)
    bmA_T = P.sb("bmA_T", [8, 128], F32)
    bmA = P.sb("bmA", [128, 8], F32)
    Mh = [P.sb(f"Mh{i}", [128, 128], BF16) for i in range(8)]
    MS_bf = P.sb("MS_bf", [128, 128], BF16)
    tzB = P.sb("tzB", [128, 1], F32)
    ZtotB = P.sb("ZtotB", [128, 1], F32)
    rzAll = P.sb("rzAll", [128, 8], F32)
    rzT_sb = P.sb("rzT_sb", [8, 128], F32)
    R_sb = P.sb("R_sb", [8, 128], F32)
    Mlast = P.sb("Mlast", [128, 8], BF16)
    diagE2 = P.sb("diagE2", [8, 8], F32)
    diagZ = P.sb("diagZ", [8, 8], F32)
    S_in = P.sb("S_in", [8, 1024], F32)
    Zin_col = P.sb("Zin_col", [8, 1], F32)
    Zbc = P.sb("Zbc", [128, 8], F32)
    Dk = P.sb("Dk", [8, 1], F32)
    tO = [P.sb(f"tO{i}", [128, G], F32) for i in range(2)]
    Bmu = P.sb("Bmu", [128, G], F32)
    Bvar = P.sb("Bvar", [128, G], F32)
    Brstd = P.sb("Brstd", [128, G], F32)
    tA = P.sb("tA", [128, G], F32)
    tB = P.sb("tB", [128, G], F32)

    psB = P.ps("psB", [128, 2 * G])
    psY = P.ps("psY", [128, 1024])
    psM = P.ps("psM", [128, 141])
    psQZ = [P.ps(f"psQZ{i}", [128, 256]) for i in range(2)]

    qL = [P.handles["qL0"], P.handles["qL1"]]
    qS = [P.handles["qS0"], P.handles["qS1"]]
    qW = P.handles["qW"]
    sG = P.handles["sG"]
    sT = P.handles["sT"]; sV = P.handles["sV"]; sS = P.handles["sS"]

    def g_pro(g_):
        for a in range(NCH):
            g_.dma_start(swv[:, a * 1024:(a + 1) * 1024], wvT[a * 128:(a + 1) * 128, :]).then_inc(qW, 16)
            g_.dma_start(swo[:, a * 1024:(a + 1) * 1024], woT[a * 128:(a + 1) * 128, :]).then_inc(qW, 16)
        g_.dma_start(s_sl[:, :].rearrange("p (a n) -> p a n", a=4),
                     slocall[:, :, :].rearrange("a p n -> p a n")).then_inc(qW, 16)
        for nm, dst in [(psel, s_ps), (cut, s_ut), (cutb, s_utb), (cdsel, s_ds),
                        (cid8, s_id8), (cid128, s_id128),
                        (cid128b, s_id128b), (clneg, s_lneg)]:
            g_.dma_start(dst[:, :], nm[:, :]).then_inc(qW, 16)
        g_.dma_start(aT_s[:, :], abT_in[0:8, :]).then_inc(qW, 16)
        g_.dma_start(bT_s[:, :], abT_in[8:16, :]).then_inc(qW, 16)
        g_.memset(ones_bf[:, :], 1.0)
        g_.memset(ones_col[:, :], 1.0)
        g_.memset(ones8[:, :], 1.0)
        g_.memset(epscol[:, :], 1e-5).then_inc(sG, 1)
    P.gsec(g_pro)
    wready = P.inc("qW", 27 * 16)
    gready = P.inc("sG")

    def v_comb(v):
        v.wait_ge(qW, wready)
        v.wait_ge(sG, gready)
        v.tensor_scalar(S_in[:, :], s_sl[:, 0:1024], s_ps[:, 0:1], None, ALU.mult)
        v.tensor_scalar(Zin_col[:, :], s_sl[:, 1024:1025], s_ps[:, 0:1], None, ALU.mult)
        v.drain().then_inc(sV, 1)
    P.vsec(v_comb)
    prevv = P.inc("sV")

    for k in range(1, 3):
        def s_dk(s, k=k, prevv=prevv):
            s.wait_ge(sV, prevv)
            s.activation(Dk[:, :], s_sl[:, k * 1026 + 1025: k * 1026 + 1026], AF.Exp).then_inc(sS, 1)
        P.ssec(s_dk)
        sdk = P.inc("sS")

        def v_ck(v, k=k, sdk=sdk):
            v.wait_ge(sS, sdk)
            v.tensor_scalar(Dk[:, :], Dk[:, :], -1.0, s_ps[:, k:k + 1], ALU.add, ALU.mult)
            v.drain()
            v.tensor_scalar(Dk[:, :], Dk[:, :], 1.0, None, ALU.add)
            v.drain()
            v.tensor_scalar(S_in[:, :], S_in[:, :], Dk[:, 0:1], None, ALU.mult)
            v.tensor_scalar(Zin_col[:, :], Zin_col[:, :], Dk[:, 0:1], None, ALU.mult)
            v.drain()
            v.tensor_scalar(Scr8[:, :], s_sl[:, k * 1026: k * 1026 + 1024], s_ps[:, k:k + 1], None, ALU.mult)
            v.tensor_scalar(tzB[0:8, 0:1], s_sl[:, k * 1026 + 1024: k * 1026 + 1025], s_ps[:, k:k + 1], None, ALU.mult)
            v.drain()
            v.tensor_tensor(S_in[:, :], S_in[:, :], Scr8[:, :], ALU.add)
            v.tensor_tensor(Zin_col[:, :], Zin_col[:, :], tzB[0:8, 0:1], ALU.add)
            v.drain().then_inc(sV, 1)
        P.vsec(v_ck)
        prevv = P.inc("sV")

    def v_zb0(v, prevv=prevv):
        v.wait_ge(sV, prevv)
        v.tensor_scalar(diagZ[:, :], s_id8[:, :], Zin_col[:, 0:1], None, ALU.mult).then_inc(sV, 1)
    P.vsec(v_zb0)
    vzb0 = P.inc("sV")

    def t_zb0(t):
        t.wait_ge(sV, vzb0)
        t.matmul(psM[:, 0:8], ones8[:, :], diagZ[:, :], start=True, stop=True).then_inc(sT, 1)
    P.tsec(t_zb0)
    tzb0 = P.inc("sT")

    def s_zb0(s):
        s.wait_ge(sT, tzb0)
        s.activation(Zbc[:, :], psM[:, 0:8], AF.Identity).then_inc(sS, 1)
    P.ssec(s_zb0)
    zb0 = P.inc("sS")
    m = {"vmm": {}, "vev": {}, "yev": {}, "carry": {}, "cev": {}, "zbev": {-1: zb0},
         "mexp": {}, "msc": {}, "ztot": {}, "ymm": {}, "intery": {-1: P.cnt["sV"]},
         "cprep": {}, "rzt": {}, "wo": {}, "oev": {}, "xadd": {}}
    load_done = {}
    store_done = {}
    uapply = {}

    for g in range(NG):
        pb = g % 2

        def y_load(y, g=g, pb=pb):
            if g >= 2:
                y.wait_ge(*store_done[g - 2])
            y.dma_start(xt[pb][:, :].rearrange("p (c t) -> p c t", c=NCH),
                        xin[:, :, g * G:(g + 1) * G].rearrange("c p t -> p c t")
                        ).then_inc(qL[g % 2], 16)
        P.ysec(y_load)
        load_done[g] = (qL[g % 2], P.inc("qL%d" % (g % 2), 16))

        def v_wait(v, g=g):
            v.wait_ge(*load_done[g])
            if g == 0:
                v.wait_ge(sS, zb0)
            if g >= 2:
                v.wait_ge(sT, m["vmm"][(g - 2, 3, 1)])
        P.vsec(v_wait)

        bufs = dict(xt=xt[pb], x_bf=x_bf, x2_bf=x2_bf, u_bf=u_bf[pb],
                    Bmu=Bmu, Bvar=Bvar, Brstd=Brstd, tA=tA, tB=tB,
                    ones_bf=ones_bf, psB=psB, epscol=epscol)
        uapply[g] = emit_ln_group(P, bufs, G, 0)

        for tl in range(4):
            ck = g * 4 + tl
            sl = slice(g * G + tl * 128, g * G + tl * 128 + 128)

            for hh in range(2):
                def t_v(t, g=g, tl=tl, hh=hh, pb=pb, ck=ck):
                    if tl == 0 and hh == 0:
                        t.wait_ge(sV, uapply[g])
                        if g >= 1:
                            t.wait_ge(sV, m["xadd"][(g - 1, NCH - 1)])
                    if (ck, hh - 1) in m["vev"]:
                        t.wait_ge(sS, m["vev"][(ck, hh - 1)])
                    elif ck >= 1:
                        t.wait_ge(sS, m["vev"][(ck - 1, 1)])
                    for kc in range(NCH):
                        i = t.matmul(psB[:, 0:512],
                                     u_bf[pb][:, kc * G + tl * 128: kc * G + tl * 128 + 128],
                                     swv[:, kc * 1024 + hh * 512: kc * 1024 + hh * 512 + 512],
                                     start=(kc == 0), stop=(kc == NCH - 1))
                    i.then_inc(sT, 1)
                P.tsec(t_v)
                m["vmm"][(g, tl, hh)] = P.inc("sT")

                def s_vev(s, g=g, tl=tl, hh=hh, ck=ck):
                    s.wait_ge(sT, m["vmm"][(g, tl, hh)])
                    if ck >= 1 and hh == 0:
                        s.wait_ge(sT, m["carry"][ck - 1])
                    s.activation(v_bf[:, hh * 512:(hh + 1) * 512], psB[:, 0:512],
                                 AF.Identity).then_inc(sS, 1)
                P.ssec(s_vev)
                m["vev"][(ck, hh)] = P.inc("sS")

            def t_tr1(t, sl=sl, ck=ck):
                t.wait_ge(qW, wready)
                t.wait_ge(sS, m["zbev"][ck - 1])
                t.transpose(psM[:, 0:8], aT_s[:, sl], s_id8[:, :]).then_inc(sT, 1)
            P.tsec(t_tr1)
            ttr1 = P.inc("sT")

            def s_tr1(s, ttr1=ttr1):
                s.wait_ge(sT, ttr1)
                s.activation(aTok[:, :], psM[:, 0:8], AF.Identity).then_inc(sS, 1)
            P.ssec(s_tr1)
            str1 = P.inc("sS")

            def t_pre(t, str1=str1):
                t.wait_ge(sS, str1)
                t.matmul(psM[:, 0:8], s_ut[:, :], aTok[:, :], start=True, stop=True)
                i = t.matmul(psM[0:8, 8:136], aTok[:, :], s_ut[:, :], start=True, stop=True)
                i.then_inc(sT, 1)
            P.tsec(t_pre)
            tpre = P.inc("sT")

            def s_pre(s, tpre=tpre, ck=ck):
                s.wait_ge(sT, tpre)
                s.wait_ge(sV, m["cprep"].get(ck - 1, 0))
                s.activation(Atok[:, :], psM[:, 0:8], AF.Identity)
                s.activation(ATr[:, :], psM[0:8, 8:136], AF.Identity)
                s.drain()
                s.activation(eA[:, :], Atok[:, :], AF.Exp)
                s.activation(eATr[:, :], ATr[:, :], AF.Exp).then_inc(sS, 1)
            P.ssec(s_pre)
            spre = P.inc("sS")

            def v_bma(v, sl=sl, spre=spre):
                v.wait_ge(sS, spre)
                v.tensor_tensor(bmA_T[:, :], bT_s[:, sl], ATr[:, :], ALU.subtract).then_inc(sV, 1)
            P.vsec(v_bma)
            vbma = P.inc("sV")

            def t_tr2(t, vbma=vbma):
                t.wait_ge(sV, vbma)
                t.transpose(psM[:, 0:8], bmA_T[:, :], s_id8[:, :]).then_inc(sT, 1)
            P.tsec(t_tr2)
            ttr2 = P.inc("sT")

            def s_tr2(s, ttr2=ttr2):
                s.wait_ge(sT, ttr2)
                s.activation(bmA[:, :], psM[:, 0:8], AF.Identity).then_inc(sS, 1)
            P.ssec(s_tr2)
            str2 = P.inc("sS")

            for h in range(8):
                bq = h % 2

                def t_bq(t, h=h, bq=bq, ck=ck, str2=str2):
                    if h == 0:
                        t.wait_ge(sS, str2)
                        t.wait_ge(sV, m["msc"].get((ck - 1, 6 + bq), 0))
                    elif h == 1:
                        t.wait_ge(sV, m["msc"].get((ck - 1, 6 + bq), 0))
                    if (ck, h - 2) in m["mexp"]:
                        t.wait_ge(sS, m["mexp"][(ck, h - 2)])
                    t.matmul(psQZ[bq][:, 0:128], s_id128b[:, :], s_lneg[:, :],
                             start=True, stop=False)
                    t.matmul(psQZ[bq][:, 0:128], s_ds[:, h * 128:(h + 1) * 128],
                             ATr[:, :], start=False, stop=True).then_inc(sT, 1)
                P.tsec(t_bq)
                tbq = P.inc("sT")

                def s_mexp(s, h=h, bq=bq, tbq=tbq, ck=ck):
                    s.wait_ge(sT, tbq)
                    if h <= 1:
                        s.wait_ge(sV, m["cprep"].get(ck - 1, 0))
                    s.activation(Mh[h][:, :], psQZ[bq][:, 0:128],
                                 AF.Exp, bias=bmA[:, h:h + 1]).then_inc(sS, 1)
                P.ssec(s_mexp)
                m["mexp"][(ck, h)] = P.inc("sS")

                def t_zc(t, h=h, ck=ck):
                    t.wait_ge(sS, m["mexp"][(ck, h)])
                    if (ck, h - 1) in m["ztot"]:
                        t.wait_ge(sV, m["ztot"][(ck, h - 1)])
                    t.matmul(psM[:, 136:137], Mh[h][:, :], ones_col[:, :],
                             start=True, stop=True).then_inc(sT, 1)
                P.tsec(t_zc)
                tzc = P.inc("sT")

                def v_ztot(v, h=h, ck=ck, tzc=tzc):
                    v.wait_ge(sT, tzc)
                    v.tensor_tensor(tzB[:, :], eA[:, h:h + 1], Zbc[:, h:h + 1], ALU.mult)
                    v.drain()
                    v.tensor_tensor(ZtotB[:, :], psM[:, 136:137], tzB[:, :], ALU.add)
                    v.drain()
                    v.reciprocal(rzAll[:, h:h + 1], ZtotB[:, :]).then_inc(sV, 1)
                    v.drain()
                P.vsec(v_ztot)
                m["ztot"][(ck, h)] = P.inc("sV")

            def t_rz(t, ck=ck):
                t.wait_ge(sV, m["ztot"][(ck, 7)])
                t.transpose(psM[0:8, 8:136], rzAll[:, :], s_id128[:, :]).then_inc(sT, 1)
            P.tsec(t_rz)
            trz = P.inc("sT")

            def s_rz(s, trz=trz):
                s.wait_ge(sT, trz)
                s.activation(rzT_sb[:, :], psM[0:8, 8:136], AF.Identity).then_inc(sS, 1)
            P.ssec(s_rz)
            m["rzt"][ck] = srz = P.inc("sS")

            def v_cprep(v, ck=ck, srz=srz):
                v.wait_ge(sS, srz)
                v.tensor_tensor(R_sb[:, :], eATr[:, :], rzT_sb[:, :], ALU.mult)
                for h in range(8):
                    v.tensor_copy(Mlast[:, h:h + 1], Mh[h][:, 127:128])
                v.tensor_scalar(diagE2[:, :], s_id8[:, :], eATr[:, 127:128], None, ALU.mult)
                v.drain().then_inc(sV, 1)
            P.vsec(v_cprep)
            m["cprep"][ck] = P.inc("sV")

            for h in range(8):
                bz = h % 2

                def t_bz(t, h=h, bz=bz, ck=ck):
                    if h <= 1:
                        t.wait_ge(sS, m["rzt"][ck])
                    if (ck, h - 2) in m["msc"]:
                        t.wait_ge(sV, m["msc"][(ck, h - 2)])
                    t.matmul(psQZ[bz][:, 128:256], s_ds[:, h * 128:(h + 1) * 128],
                             rzT_sb[:, :], start=True, stop=True).then_inc(sT, 1)
                P.tsec(t_bz)
                tbz = P.inc("sT")

                def v_msc(v, h=h, bz=bz, tbz=tbz, ck=ck):
                    v.wait_ge(sT, tbz)
                    if (ck, h - 1) in m["ymm"]:
                        v.wait_ge(sT, m["ymm"][(ck, h - 1)])
                    v.tensor_tensor(MS_bf[:, :], Mh[h][:, :], psQZ[bz][:, 128:256], ALU.mult).then_inc(sV, 1)
                P.vsec(v_msc)
                m["msc"][(ck, h)] = P.inc("sV")

                def t_ym(t, h=h, ck=ck):
                    t.wait_ge(sV, m["msc"][(ck, h)])
                    if h == 0:
                        t.wait_ge(sS, m["cev"].get(ck - 1, 0))
                        t.wait_ge(sS, m["yev"].get(ck - 1, 0))
                    for dc in range(NCH):
                        i = t.matmul(psY[:, dc * 128:(dc + 1) * 128],
                                     v_bf[:, dc * 128:(dc + 1) * 128],
                                     MS_bf[:, :], start=(h == 0 and dc % 4 == 0),
                                     stop=False)
                    i.then_inc(sT, 1)
                P.tsec(t_ym)
                m["ymm"][(ck, h)] = P.inc("sT")

            def t_iy(t, ck=ck):
                t.wait_ge(sV, m["cprep"][ck])
                t.wait_ge(sS, m["cev"].get(ck - 1, 0))
                for dc in range(NCH):
                    i = t.matmul(psY[:, dc * 128:(dc + 1) * 128],
                                 S_in[:, dc * 128:(dc + 1) * 128],
                                 R_sb[:, :], start=False, stop=(dc % 4 == 3))
                i.then_inc(sT, 1)
            P.tsec(t_iy)
            m["intery"][ck] = tiy = P.inc("sT")

            def s_yev(s, g=g, tl=tl, ck=ck, pb=pb, tiy=tiy):
                s.wait_ge(sT, tiy)
                if g >= 2 and tl == 0:
                    s.wait_ge(sT, m["wo"][(g - 2, NCH - 1)])
                s.activation(
                    yTg[pb][:, :].rearrange("p (dc i) -> p dc i", dc=NCH)[:, :, tl * 128:(tl + 1) * 128],
                    psY[:, :].rearrange("p (dc i) -> p dc i", dc=NCH),
                    AF.Identity).then_inc(sS, 1)
            P.ssec(s_yev)
            m["yev"][ck] = P.inc("sS")

            def t_carry(t, ck=ck):
                t.wait_ge(sS, m["yev"][ck])
                t.wait_ge(sV, m["cprep"][ck])
                t.matmul(psY[0:8, 0:512], Mlast[:, :], v_bf[:, 0:512], start=True, stop=False)
                t.matmul(psY[0:8, 0:512], diagE2[:, :], S_in[:, 0:512], start=False, stop=True)
                t.matmul(psY[0:8, 512:1024], Mlast[:, :], v_bf[:, 512:1024], start=True, stop=False)
                t.matmul(psY[0:8, 512:1024], diagE2[:, :], S_in[:, 512:1024], start=False, stop=True)
                t.matmul(psM[0:8, 137:138], Mlast[:, :], ones_col[:, :], start=True, stop=False)
                i = t.matmul(psM[0:8, 137:138], diagE2[:, :], Zin_col[:, :], start=False, stop=True)
                i.then_inc(sT, 1)
            P.tsec(t_carry)
            m["carry"][ck] = P.inc("sT")

            def s_cev(s, ck=ck):
                s.wait_ge(sT, m["carry"][ck])
                s.activation(S_in[:, :], psY[0:8, 0:1024], AF.Identity)
                s.activation(Zin_col[:, :], psM[0:8, 137:138], AF.Identity).then_inc(sS, 1)
            P.ssec(s_cev)
            m["cev"][ck] = P.inc("sS")

            def v_zb(v, ck=ck):
                v.wait_ge(sS, m["cev"][ck])
                v.tensor_scalar(diagZ[:, :], s_id8[:, :], Zin_col[:, 0:1], None, ALU.mult).then_inc(sV, 1)
            P.vsec(v_zb)
            vzb = P.inc("sV")

            def t_zb(t, ck=ck, vzb=vzb):
                t.wait_ge(sV, vzb)
                t.matmul(psM[:, 0:8], ones8[:, :], diagZ[:, :], start=True, stop=True).then_inc(sT, 1)
            P.tsec(t_zb)
            tzb = P.inc("sT")

            def s_zb(s, ck=ck, tzb=tzb):
                s.wait_ge(sT, tzb)
                s.activation(Zbc[:, :], psM[:, 0:8], AF.Identity).then_inc(sS, 1)
            P.ssec(s_zb)
            m["zbev"][ck] = P.inc("sS")

        for oc in range(NCH):
            op = oc % 2

            def t_o(t, g=g, oc=oc, pb=pb):
                t.wait_ge(sS, m["yev"][g * 4 + 3])
                if oc == 0:
                    t.wait_ge(sS, m["vev"][(g * 4 + 3, 1)])
                else:
                    t.wait_ge(sS, m["oev"][(g, oc - 1)])
                for dc in range(NCH):
                    i = t.matmul(psB[:, 0:512],
                                 swo[:, dc * 1024 + oc * 128: dc * 1024 + oc * 128 + 128],
                                 yTg[pb][:, dc * G:(dc + 1) * G],
                                 start=(dc == 0), stop=(dc == NCH - 1))
                i.then_inc(sT, 1)
            P.tsec(t_o)
            m["wo"][(g, oc)] = P.inc("sT")

            def s_oev(s, g=g, oc=oc, op=op):
                s.wait_ge(sT, m["wo"][(g, oc)])
                if (g, oc - 2) in m["xadd"]:
                    s.wait_ge(sV, m["xadd"][(g, oc - 2)])
                elif g >= 1:
                    s.wait_ge(sV, m["xadd"][(g - 1, NCH - 2 + oc)])
                s.activation(tO[op][:, :], psB[:, 0:512], AF.Identity).then_inc(sS, 1)
            P.ssec(s_oev)
            m["oev"][(g, oc)] = P.inc("sS")

            def v_add(v, g=g, oc=oc, op=op, pb=pb):
                v.wait_ge(sS, m["oev"][(g, oc)])
                v.tensor_tensor(xt[pb][:, _chsl(oc)], xt[pb][:, _chsl(oc)],
                                tO[op][:, :], ALU.add).then_inc(sV, 1)
            P.vsec(v_add)
            m["xadd"][(g, oc)] = P.inc("sV")

        def g_store(g_, g=g, pb=pb):
            g_.wait_ge(sV, m["xadd"][(g, NCH - 1)])
            g_.dma_start(xout[:, :, g * G:(g + 1) * G].rearrange("c p t -> p c t"),
                         xt[pb][:, :].rearrange("p (c t) -> p c t", c=NCH)
                         ).then_inc(qS[g % 2], 16)
        P.gsec(g_store)
        store_done[g] = (qS[g % 2], P.inc("qS%d" % (g % 2), 16))

    def g_fin(g_):
        g_.wait_ge(qS[0], P.cnt["qS0"])
        g_.wait_ge(qS[1], P.cnt["qS1"])
    P.gsec(g_fin)


    P.pop_scope()
    P.emit_barrier()


def build_ssm_p2():
    key = "ssm2"
    if key in _PROG_CACHE:
        return _PROG_CACHE[key]
    P = Prog(key)
    nc = P.nc
    ext = {}
    ext["xin"] = nc.dram_tensor("xin", [NCH, 128, TPC], F32, kind="ExternalInput")
    ext["abT_in"] = nc.dram_tensor("abT_in", [16, TPC], F32, kind="ExternalInput")
    ext["slocall"] = nc.dram_tensor("slocall", [4, 8, 1026], F32, kind="ExternalInput")
    ext["psel"] = nc.dram_tensor("psel", [8, 3], F32, kind="ExternalInput")
    ext["wvT"] = nc.dram_tensor("wvT", [DD, DD], F32, kind="ExternalInput")
    ext["woT"] = nc.dram_tensor("woT", [DD, DD], F32, kind="ExternalInput")
    ext["cut"] = nc.dram_tensor("cut", [128, 128], F32, kind="ExternalInput")
    ext["cutb"] = nc.dram_tensor("cutb", [128, 128], BF16, kind="ExternalInput")
    ext["cdsel"] = nc.dram_tensor("cdsel", [8, 1024], F32, kind="ExternalInput")
    ext["cid8"] = nc.dram_tensor("cid8", [8, 8], F32, kind="ExternalInput")
    ext["cid128"] = nc.dram_tensor("cid128", [128, 128], F32, kind="ExternalInput")
    ext["cid128b"] = nc.dram_tensor("cid128b", [128, 128], BF16, kind="ExternalInput")
    ext["clneg"] = nc.dram_tensor("clneg", [128, 128], BF16, kind="ExternalInput")
    ext["xout"] = nc.dram_tensor("xout", [NCH, 128, TPC], F32, kind="ExternalOutput")
    P.std_sems()
    P.open()
    emit_ssm_p2(P, ext)
    P.close()
    _PROG_CACHE[key] = P
    return P


def run_ssm(xs, W_q, W_k, W_v, W_alpha_w, W_alpha_b, W_o, C):
    P1 = build_ssm_p1()
    com = {
        "wqT": np.ascontiguousarray(W_q.T), "wkT": np.ascontiguousarray(W_k.T),
        "waT": np.ascontiguousarray(W_alpha_w.T), "wvT": np.ascontiguousarray(W_v.T),
        "abt": np.ascontiguousarray(W_alpha_b.reshape(4, 128).T),
        "cut": C["ut_f32"], "chs": C["hsel"], "cnhs": C["nhsel"],
        "cid8": np.ascontiguousarray(C["ident_f32"][0:8, 0:8]),
        "cid8b": np.eye(8, dtype=ml_dtypes.bfloat16),
    }
    in_maps = [dict(xin=xs[c], **com) for c in range(NCORE)]
    res = run_bass_kernel_spmd(P1.nc, in_maps, core_ids=list(range(NCORE)))
    slocs = [np.asarray(res.results[c]["sloc"]) for c in range(NCORE)]
    abTs = [res.results[c]["abT_out"] for c in range(NCORE)]

    P2 = build_ssm_p2()
    sl_b = [np.stack(slocs[0:4]), np.stack(slocs[4:8])]
    com2 = {
        "wvT": com["wvT"], "woT": np.ascontiguousarray(W_o.T),
        "cut": C["ut_f32"], "cutb": C["ut_bf"], "cdsel": C["dsel"],
        "cid8": com["cid8"], "cid128": C["ident_f32"],
        "cid128b": C["ident_f32"].astype(ml_dtypes.bfloat16),
        "clneg": C["lneg_bf"],
    }
    in_maps = []
    for c in range(NCORE):
        mchunk = c % 4
        sel = np.zeros((8, 3), np.float32)
        for k in range(3):
            if k < mchunk:
                sel[:, k] = 1.0
        in_maps.append(dict(xin=xs[c], abT_in=abTs[c], slocall=sl_b[c // 4],
                            psel=sel, **com2))
    res = run_bass_kernel_spmd(P2.nc, in_maps, core_ids=list(range(NCORE)))
    return [res.results[c]["xout"] for c in range(NCORE)]


def l0_host_consts(C, inp_tabs=None):
    hc = {}
    hc["cm0"] = C["mask_b0"].astype(ml_dtypes.bfloat16)
    hc["cm4"] = C["mask_b4"].astype(ml_dtypes.bfloat16)
    hc["cid128b"] = C["ident_f32"].astype(ml_dtypes.bfloat16)
    hc["chs"] = C["hsel"]
    chsT = np.zeros((8, 512), np.float32)
    for oc in range(4):
        chsT[:, oc * 128:(oc + 1) * 128] = C["hsel"][:, oc * 8:(oc + 1) * 8].T
    hc["chsT"] = np.ascontiguousarray(chsT)
    hc["crot"] = C["rot_lhsT_bf"]
    return hc


def l0_smask(C, chunk0):
    sm = np.zeros((128, 16 * 128), np.float32)
    for m_ in range(4):
        for b in range(4):
            blk = sm[:, (m_ * 4 + b) * 128:(m_ * 4 + b + 1) * 128]
            if chunk0 and (m_ - 4 + b) < 0:
                blk[:] = NEG
            elif b == 0:
                blk[:] = C["mask_b0"]
    return sm.astype(ml_dtypes.bfloat16)


def l0_tables(C, c):
    t0 = (c % 4) * TPC
    out = {}
    for nm, key in [("tqc", "tab_qcos"), ("tqs", "tab_qsin"),
                    ("tkc", "tab_kcos"), ("tks", "tab_ksin")]:
        full = C[key]
        lo = t0 - 512
        if lo < 0:
            sl = np.concatenate([np.zeros((128, -lo), full.dtype), full[:, 0:t0 + TPC]], axis=1)
        else:
            sl = full[:, lo:t0 + TPC]
        out[nm] = np.ascontiguousarray(sl)
    return out


def _halo_slices(xs):
    halos = []
    for c in range(NCORE):
        if c % 4 == 0:
            halos.append(np.zeros((NCH, 128, 513), np.float32))
        else:
            halos.append(np.ascontiguousarray(np.asarray(xs[c - 1])[:, :, TPC - 513:TPC]))
    return halos


def _halo_col(xs):
    cols = []
    for c in range(NCORE):
        if c % 4 == 0:
            cols.append(np.zeros((NCH, 128, 1), np.float32))
        else:
            cols.append(np.ascontiguousarray(np.asarray(xs[c - 1])[:, :, TPC - 1:TPC]))
    return cols


def run_l0attn(l, xs, inp, C):
    P = build_l0attn(l)
    hc = l0_host_consts(C)
    wqkv = np.ascontiguousarray(inp["L0_ffn_wqkv"][l])
    wo = np.ascontiguousarray(inp["L0_ffn_wo"][l])
    qsc = np.ascontiguousarray(np.tile(inp["L0_ffn_q_scale"][l], 2)[:, None])
    ksc = np.ascontiguousarray(np.tile(inp["L0_ffn_k_scale"][l], 2)[:, None])
    halos = _halo_slices(xs)
    in_maps = []
    for c in range(NCORE):
        m_ = dict(xin=np.asarray(xs[c]), xhalo=halos[c], wqkv=wqkv, wo=wo,
                  qscol=qsc, kscol=ksc, smask=l0_smask(C, c % 4 == 0), **hc,
                  **l0_tables(C, c))
        in_maps.append(m_)
    res = run_bass_kernel_spmd(P.nc, in_maps, core_ids=list(range(NCORE)))
    return [res.results[c]["xout"] for c in range(NCORE)]


def _l0ffn_bias_tables(inp, l):
    b1 = inp["L0_ffn_b1"][l]
    b1a = np.zeros((128, 22), np.float32)
    b1g = np.zeros((128, 22), np.float32)
    for half in range(2):
        for c, (off, w) in enumerate(_hchunks(half)):
            b1a[0:w, half * 11 + c] = b1[off:off + w]
            b1g[0:w, half * 11 + c] = b1[HID + off: HID + off + w]
    lgt = np.ascontiguousarray(inp["L0_ffn_ln_g"][l].reshape(8, 128).T)
    lbt = np.ascontiguousarray(inp["L0_ffn_ln_b"][l].reshape(8, 128).T)
    b2t = np.ascontiguousarray(inp["L0_ffn_b2"][l].reshape(8, 128).T)
    return b1a, b1g, b2t, lgt, lbt


def run_l0ffn(l, final, xs, inp, C, xres=None):
    P = build_l0ffn(final)
    w1 = np.ascontiguousarray(inp["L0_ffn_w1"][l])
    w2 = np.ascontiguousarray(inp["L0_ffn_w2"][l])
    b1a, b1g, b2t, lgt, lbt = _l0ffn_bias_tables(inp, l)
    cols = _halo_col(xs)
    in_maps = []
    for c in range(NCORE):
        m_ = dict(xin=np.asarray(xs[c]), xh1=cols[c], w1=w1, w2=w2,
                  b1at=b1a, b1gt=b1g, b2t=b2t, lgt=lgt, lbt=lbt)
        if final:
            m_["xres"] = np.asarray(xres[c])
        in_maps.append(m_)
    res = run_bass_kernel_spmd(P.nc, in_maps, core_ids=list(range(NCORE)))
    return [res.results[c]["xout"] for c in range(NCORE)]


def run_l0call(xs, inp, C):
    x1_ = run_l0attn(0, xs, inp, C)
    x2_ = run_l0ffn(0, False, x1_, inp, C)
    x3_ = run_l0attn(1, x2_, inp, C)
    return run_l0ffn(1, True, x3_, inp, C, xres=xs)


def build_fused():
    key = "fused"
    if key in _PROG_CACHE:
        return _PROG_CACHE[key]
    P = Prog(key)
    nc = P.nc
    E = {}

    def ein(nm, shape, dt):
        E[nm] = nc.dram_tensor(nm, shape, dt, kind="ExternalInput")
        return E[nm]

    x0in = ein("x0in", [NCH, 128, TPC], F32)
    yout = nc.dram_tensor("yout", [NCH, 128, TPC], F32, kind="ExternalOutput")
    for l in range(2):
        ein(f"wqkv{l}", [DD, 1536], F32)
        ein(f"wo{l}", [512, DD], F32)
        ein(f"qscol{l}", [128, 1], F32)
        ein(f"kscol{l}", [128, 1], F32)
        ein(f"w1f{l}", [DD, 2 * HID], F32)
        ein(f"w2f{l}", [HID, DD], F32)
        ein(f"b1at{l}", [128, 22], F32)
        ein(f"b1gt{l}", [128, 22], F32)
        ein(f"b2tf{l}", [128, 8], F32)
        ein(f"lgt{l}", [128, 8], F32)
        ein(f"lbt{l}", [128, 8], F32)
    for nm in ("tqc", "tqs", "tkc", "tks"):
        ein(nm, [128, 4608], BF16)
    ein("smask", [128, 16 * 128], BF16)
    ein("cm0", [128, 128], BF16)
    ein("cm4", [128, 128], BF16)
    ein("cid128b", [128, 128], BF16)
    ein("chs", [128, 32], F32)
    ein("chsT", [8, 512], F32)
    ein("crot", [128, 128], BF16)
    ein("wqTs", [DD, 512], F32)
    ein("wkTs", [DD, 512], F32)
    ein("waTs", [DD, 512], F32)
    ein("wvTs", [DD, DD], F32)
    ein("abt", [128, 4], F32)
    ein("cut", [128, 128], F32)
    ein("cutb", [128, 128], BF16)
    ein("cnhs", [128, 32], F32)
    ein("cid8", [8, 8], F32)
    ein("cid8b", [8, 8], BF16)
    ein("cdsel", [8, 1024], F32)
    ein("cid128", [128, 128], F32)
    ein("clneg", [128, 128], BF16)
    ein("woTs", [DD, DD], F32)
    ein("psel", [8, 3], F32)
    ein("prevrank", [1, 1], mybir.dt.int32)
    ein("w1l1", [DD, 4096], F32)
    ein("w2l1", [4096, DD], F32)
    ein("b1tl1", [128, 32], F32)
    ein("b2tl1", [128, 8], F32)
    ein("wqTa", [DD, DD], F32)
    ein("wkTa", [DD, DD], F32)
    ein("wvTa", [DD, DD], F32)
    ein("woTa", [DD, DD], F32)

    xA = nc.dram_tensor("xA", [NCH, 128, TPC], F32)
    xB = nc.dram_tensor("xB", [NCH, 128, TPC], F32)
    xC = nc.dram_tensor("xC", [NCH, 128, TPC], F32)
    hsrc = nc.dram_tensor("hsrc", [1024, SW], F32)
    hall = nc.dram_tensor("hall", [5 * 1024, SW], F32)
    hcsrc = nc.dram_tensor("hcsrc", [1024, 1], F32)
    hcall = nc.dram_tensor("hcall", [5 * 1024, 1], F32)
    xhalo_d = nc.dram_tensor("xhalo_d", [NCH, 128, SW], F32)
    xh1_d = nc.dram_tensor("xh1_d", [NCH, 128, 1], F32)
    sloc_i = nc.dram_tensor("sloc_i", [8, 1026], F32)
    slall = nc.dram_tensor("slall", [4, 8, 1026], F32)
    abT_i = nc.dram_tensor("abT_i", [16, TPC], F32)
    qt_i = nc.dram_tensor("qt_i", [NCH, 128, TPC], BF16)
    kvp_i = nc.dram_tensor("kvp_i", [64, 1024], F32)
    kvr_i = nc.dram_tensor("kvr_i", [64, 1024], F32)

    P.std_sems()
    sC = P.sem("sC")
    GROUPS = [[0, 1, 2, 3], [4, 5, 6, 7]]
    P.open()
    qW = P.handles["qW"]
    sG = P.handles["sG"]

    P.push_scope()
    rk_t = P.sb("rk", [1, 1], mybir.dt.int32)
    zt = P.sb("zt", [128, NCH * SW], F32)
    rk = nc.gpsimd.alloc_register("rkreg")
    roff = nc.gpsimd.alloc_register("roffreg")

    def g_pre(g_):
        g_.dma_start(rk_t[:, :], E["prevrank"][:, :]).then_inc(qW, 16)
        g_.memset(zt[:, :], 0.0)
        g_.wait_ge(qW, P.cnt["qW"] + 16)
        g_.reg_load(rk, rk_t[0:1, 0:1])
        g_.dma_start(hall[4 * 1024:5 * 1024, :].rearrange("(c p) t -> p c t", p=128),
                     zt[:, :].rearrange("p (c t) -> p c t", c=NCH)).then_inc(qW, 16)
        g_.dma_start(hcall[4 * 1024:5 * 1024, :].rearrange("(c p) t -> p c t", p=128),
                     zt[:, 0:NCH].rearrange("p (c t) -> p c t", c=NCH),
                     allow_slow_non_contiguous=True).then_inc(qW, 16)
        g_.wait_ge(qW, P.cnt["qW"] + 48)
    P.gsec(g_pre)
    P.inc("qW", 48)
    P.pop_scope()
    P.emit_barrier()

    def halo_ex(src, width):

        def g_h(g_):
            g_.dma_start(hsrc[:, :].rearrange("(c p) t -> c p t", p=128),
                         src[:, :, TPC - SW:TPC]).then_inc(qW, 16)
            g_.wait_ge(qW, P.cnt["qW"] + 16)
            g_.collective_compute(
                "AllGather", mybir.AluOpType.bypass, GROUPS,
                ins=[hsrc.ap().opt()], outs=[hall[0:4 * 1024, :].opt()],
            ).then_inc(sC, 1)
            g_.wait_ge(sC, P.cnt["sC"] + 1)
            g_.reg_mov(roff, rk)
            g_.reg_mul(roff, roff, 1024 * SW)
            g_.dma_start(
                xhalo_d[:, :, :],
                bass.AP(hall, roff, [[128 * SW, NCH], [SW, 128], [1, SW]]),
            ).then_inc(qW, 16)
            g_.wait_ge(qW, P.cnt["qW"] + 32)
            if width == 1:
                g_.dma_start(xh1_d[:, :, :], xhalo_d[:, :, SW - 1:SW],
                             allow_slow_non_contiguous=True).then_inc(qW, 16)
                g_.wait_ge(qW, P.cnt["qW"] + 48)
        P.gsec(g_h)
        P.inc("qW", 32 + (16 if width == 1 else 0))
        P.inc("sC", 1)

        tgt = P.cnt["qW"]

        def y_h(y, tgt=tgt):
            y.wait_ge(qW, tgt)
        P.ysec(y_h)

    lw = {}
    for l in range(2):
        lw[l] = dict(wqkv=E[f"wqkv{l}"], wo=E[f"wo{l}"], qscol=E[f"qscol{l}"],
                     kscol=E[f"kscol{l}"],
                     tqc=E["tqc"], tqs=E["tqs"], tkc=E["tkc"], tks=E["tks"],
                     smask=E["smask"], cm0=E["cm0"], cm4=E["cm4"],
                     cid128b=E["cid128b"], chs=E["chs"], chsT=E["chsT"],
                     crot=E["crot"])
    fw = {}
    for l in range(2):
        fw[l] = dict(w1=E[f"w1f{l}"], w2=E[f"w2f{l}"], b1at=E[f"b1at{l}"],
                     b1gt=E[f"b1gt{l}"], b2t=E[f"b2tf{l}"], lgt=E[f"lgt{l}"],
                     lbt=E[f"lbt{l}"])

    def l0call(xres, t1, t2):
        halo_ex(xres, SW)
        emit_l0attn(P, dict(xin=xres, xhalo=xhalo_d, xout=t1, **lw[0]), 0)
        halo_ex(t1, 1)
        emit_l0ffn(P, dict(xin=t1, xh1=xh1_d, xout=t2, **fw[0]), False)
        halo_ex(t2, SW)
        emit_l0attn(P, dict(xin=t2, xhalo=xhalo_d, xout=t1, **lw[1]), 1)
        halo_ex(t1, 1)
        emit_l0ffn(P, dict(xin=t1, xh1=xh1_d, xres=xres, xout=t2, **fw[1]), True)
        return t2

    xa = l0call(x0in, xA, xB)
    emit_ssm_p1(P, dict(xin=xa, wqT=E["wqTs"], wkT=E["wkTs"], waT=E["waTs"],
                        wvT=E["wvTs"], abt=E["abt"], cut=E["cut"], chs=E["chs"],
                        cnhs=E["cnhs"], cid8=E["cid8"], cid8b=E["cid8b"],
                        sloc=sloc_i, abT_out=abT_i))

    def g_sl(g_):
        g_.collective_compute(
            "AllGather", mybir.AluOpType.bypass, GROUPS,
            ins=[sloc_i.ap().opt()],
            outs=[slall[:, :, :].rearrange("a p n -> (a p) n").opt()],
        ).then_inc(sC, 1)
        g_.wait_ge(sC, P.cnt["sC"] + 1)
    P.gsec(g_sl)
    P.inc("sC", 1)

    emit_ssm_p2(P, dict(xin=xa, abT_in=abT_i, slocall=slall, psel=E["psel"],
                        wvT=E["wvTs"], woT=E["woTs"],
                        cut=E["cut"], cutb=E["cutb"], cdsel=E["cdsel"],
                        cid8=E["cid8"], cid128=E["cid128"], cid128b=E["cid128b"],
                        clneg=E["clneg"], xout=xC))
    xc = l0call(xC, xA, xB)
    emit_l1ffn(P, dict(xin=xc, w1=E["w1l1"], w2=E["w2l1"], b1t=E["b1tl1"],
                       b2t=E["b2tl1"], xout=xA))
    emit_l1attn_p1(P, dict(xin=xA, wqT=E["wqTa"], wkT=E["wkTa"], wvT=E["wvTa"],
                           qt_out=qt_i, kvp=kvp_i))

    def g_kv(g_):
        g_.collective_compute(
            "AllReduce", mybir.AluOpType.add, GROUPS,
            ins=[kvp_i.ap().opt()], outs=[kvr_i.ap().opt()],
        ).then_inc(sC, 1)
        g_.wait_ge(sC, P.cnt["sC"] + 1)
    P.gsec(g_kv)
    P.inc("sC", 1)

    emit_l1attn_p2(P, dict(xin=xA, qt_in=qt_i, kvp4=kvr_i, woT=E["woTa"], xout=xC),
                   fused_kv=True)
    emit_l1ffn(P, dict(xin=xC, w1=E["w1l1"], w2=E["w2l1"], b1t=E["b1tl1"],
                       b2t=E["b2tl1"], xout=yout))
    P.close()
    _PROG_CACHE[key] = P
    return P


def build_fused_mini():
    key = "fusedmini"
    if key in _PROG_CACHE:
        return _PROG_CACHE[key]
    P = Prog(key)
    nc = P.nc
    E = {}

    def ein(nm, shape, dt):
        E[nm] = nc.dram_tensor(nm, shape, dt, kind="ExternalInput")
        return E[nm]

    x0in = ein("x0in", [NCH, 128, TPC], F32)
    yout = nc.dram_tensor("yout", [NCH, 128, TPC], F32, kind="ExternalOutput")
    ein("wqkv0", [DD, 1536], F32); ein("wo0", [512, DD], F32)
    ein("qscol0", [128, 1], F32); ein("kscol0", [128, 1], F32)
    for nm in ("tqc", "tqs", "tkc", "tks"):
        ein(nm, [128, 4608], BF16)
    ein("smask", [128, 16 * 128], BF16)
    ein("cm0", [128, 128], BF16); ein("cm4", [128, 128], BF16)
    ein("cid128b", [128, 128], BF16); ein("chs", [128, 32], F32)
    ein("chsT", [8, 512], F32); ein("crot", [128, 128], BF16)
    ein("prevrank", [1, 1], mybir.dt.int32)

    hsrc = nc.dram_tensor("hsrc", [1024, SW], F32)
    hall = nc.dram_tensor("hall", [5 * 1024, SW], F32)
    xhalo_d = nc.dram_tensor("xhalo_d", [NCH, 128, SW], F32)
    xhalo_dbg = nc.dram_tensor("xhalo_dbg", [NCH, 128, SW], F32, kind="ExternalOutput")

    P.std_sems()
    sC = P.sem("sC")
    GROUPS = [[0, 1, 2, 3], [4, 5, 6, 7]]
    P.open()
    qW = P.handles["qW"]

    P.push_scope()
    rk_t = P.sb("rk", [1, 1], mybir.dt.int32)
    zt = P.sb("zt", [128, NCH * SW], F32)
    rk = nc.gpsimd.alloc_register("rkreg")
    roff = nc.gpsimd.alloc_register("roffreg")

    def g_pre(g_):
        g_.dma_start(rk_t[:, :], E["prevrank"][:, :]).then_inc(qW, 16)
        g_.memset(zt[:, :], 0.0)
        g_.wait_ge(qW, P.cnt["qW"] + 16)
        g_.reg_load(rk, rk_t[0:1, 0:1])
        g_.dma_start(hall[4 * 1024:5 * 1024, :].rearrange("(c p) t -> p c t", p=128),
                     zt[:, :].rearrange("p (c t) -> p c t", c=NCH)).then_inc(qW, 16)
        g_.wait_ge(qW, P.cnt["qW"] + 32)
    P.gsec(g_pre)
    P.inc("qW", 32)
    P.pop_scope()
    P.emit_barrier()

    def g_h(g_):
        g_.dma_start(hsrc[:, :].rearrange("(c p) t -> c p t", p=128),
                     x0in[:, :, TPC - SW:TPC]).then_inc(qW, 16)
        g_.wait_ge(qW, P.cnt["qW"] + 16)
        g_.collective_compute(
            "AllGather", mybir.AluOpType.bypass, GROUPS,
            ins=[hsrc.ap().opt()], outs=[hall[0:4 * 1024, :].opt()],
        ).then_inc(sC, 1)
        g_.wait_ge(sC, P.cnt["sC"] + 1)
        g_.reg_mov(roff, rk)
        g_.reg_mul(roff, roff, 1024 * SW)
        g_.dma_start(xhalo_d[:, :, :],
                     bass.AP(hall, roff, [[128 * SW, NCH], [SW, 128], [1, SW]])
                     ).then_inc(qW, 16)
        g_.wait_ge(qW, P.cnt["qW"] + 32)
        g_.dma_start(xhalo_dbg[:, :, :], xhalo_d[:, :, :]).then_inc(qW, 16)
        g_.wait_ge(qW, P.cnt["qW"] + 48)
    P.gsec(g_h)
    P.inc("qW", 48)
    P.inc("sC", 1)

    lw0 = dict(wqkv=E["wqkv0"], wo=E["wo0"], qscol=E["qscol0"], kscol=E["kscol0"],
               tqc=E["tqc"], tqs=E["tqs"], tkc=E["tkc"], tks=E["tks"],
               smask=E["smask"], cm0=E["cm0"], cm4=E["cm4"],
               cid128b=E["cid128b"], chs=E["chs"], chsT=E["chsT"], crot=E["crot"])
    ein("w1f0", [DD, 2 * HID], F32)
    ein("w2f0", [HID, DD], F32)
    ein("b1at0", [128, 22], F32)
    ein("b1gt0", [128, 22], F32)
    ein("b2tf0", [128, 8], F32)
    ein("lgt0", [128, 8], F32)
    ein("lbt0", [128, 8], F32)
    xA = nc.dram_tensor("xA", [NCH, 128, TPC], F32)
    hcsrc = nc.dram_tensor("hcsrc", [1024, 1], F32)
    hcall = nc.dram_tensor("hcall", [5 * 1024, 1], F32)
    xh1_d = nc.dram_tensor("xh1_d", [NCH, 128, 1], F32)
    emit_l0attn(P, dict(xin=x0in, xhalo=xhalo_d, xout=xA, **lw0), 0)

    def g_z2(g_):
        g_.dma_start(hcall[4 * 1024:5 * 1024, :].rearrange("(c p) t -> p c t", p=128),
                     x0in[:, :, 0:1].rearrange("c p t -> p c t"),
                     allow_slow_non_contiguous=True).then_inc(qW, 16)
        g_.wait_ge(qW, P.cnt["qW"] + 16)
    P.gsec(g_z2)
    P.inc("qW", 16)

    def g_h1(g_):
        g_.dma_start(hcsrc[:, :].rearrange("(c p) t -> c p t", p=128),
                     xA[:, :, TPC - 1:TPC],
                     allow_slow_non_contiguous=True).then_inc(qW, 16)
        g_.wait_ge(qW, P.cnt["qW"] + 16)
        g_.collective_compute(
            "AllGather", mybir.AluOpType.bypass, GROUPS,
            ins=[hcsrc.ap().opt()], outs=[hcall[0:4 * 1024, :].opt()],
        ).then_inc(sC, 1)
        g_.wait_ge(sC, P.cnt["sC"] + 1)
        g_.reg_mov(roff, rk)
        g_.reg_mul(roff, roff, 1024)
        g_.dma_start(xh1_d[:, :, :],
                     bass.AP(hcall, roff, [[128, NCH], [1, 128], [1, 1]]),
                     allow_slow_non_contiguous=True).then_inc(qW, 16)
        g_.wait_ge(qW, P.cnt["qW"] + 32)
    P.gsec(g_h1)
    P.inc("qW", 32)
    P.inc("sC", 1)

    fw0 = dict(w1=E["w1f0"], w2=E["w2f0"], b1at=E["b1at0"], b1gt=E["b1gt0"],
               b2t=E["b2tf0"], lgt=E["lgt0"], lbt=E["lbt0"])
    emit_l0ffn(P, dict(xin=xA, xh1=xh1_d, xout=yout, **fw0), False)
    P.close()
    _PROG_CACHE[key] = P
    return P


def run_fused(inputs):
    C = host_constants()
    P = build_fused()
    xs = prep_x(np.ascontiguousarray(inputs["L0_attn_x"], dtype=np.float32))
    com = {}
    for l in range(2):
        com[f"wqkv{l}"] = np.ascontiguousarray(inputs["L0_ffn_wqkv"][l])
        com[f"wo{l}"] = np.ascontiguousarray(inputs["L0_ffn_wo"][l])
        com[f"qscol{l}"] = np.ascontiguousarray(np.tile(inputs["L0_ffn_q_scale"][l], 2)[:, None])
        com[f"kscol{l}"] = np.ascontiguousarray(np.tile(inputs["L0_ffn_k_scale"][l], 2)[:, None])
        com[f"w1f{l}"] = np.ascontiguousarray(inputs["L0_ffn_w1"][l])
        com[f"w2f{l}"] = np.ascontiguousarray(inputs["L0_ffn_w2"][l])
        b1a, b1g, b2t, lgt, lbt = _l0ffn_bias_tables(inputs, l)
        com[f"b1at{l}"] = b1a
        com[f"b1gt{l}"] = b1g
        com[f"b2tf{l}"] = b2t
        com[f"lgt{l}"] = lgt
        com[f"lbt{l}"] = lbt
    hc = l0_host_consts(C)
    com.update(cm0=hc["cm0"], cm4=hc["cm4"], cid128b=hc["cid128b"], chs=hc["chs"],
               chsT=hc["chsT"], crot=hc["crot"])
    com.update(wqTs=np.ascontiguousarray(inputs["L0_attn_W_q"].T),
               wkTs=np.ascontiguousarray(inputs["L0_attn_W_k"].T),
               waTs=np.ascontiguousarray(inputs["L0_attn_W_alpha_w"].T),
               wvTs=np.ascontiguousarray(inputs["L0_attn_W_v"].T),
               woTs=np.ascontiguousarray(inputs["L0_attn_W_o"].T),
               abt=np.ascontiguousarray(inputs["L0_attn_W_alpha_b"].reshape(4, 128).T),
               cut=C["ut_f32"], cutb=C["ut_bf"], cnhs=C["nhsel"],
               cid8=np.ascontiguousarray(C["ident_f32"][0:8, 0:8]),
               cid8b=np.eye(8, dtype=ml_dtypes.bfloat16),
               cdsel=C["dsel"], cid128=C["ident_f32"],
               clneg=C["lneg_bf"])
    com.update(w1l1=np.ascontiguousarray(inputs["L1_ffn_w1"]),
               w2l1=np.ascontiguousarray(inputs["L1_ffn_w2"]),
               b1tl1=np.ascontiguousarray(inputs["L1_ffn_b1"].reshape(32, 128).T),
               b2tl1=np.ascontiguousarray(inputs["L1_ffn_b2"].reshape(8, 128).T),
               wqTa=np.ascontiguousarray(inputs["L1_attn_W_q"].T),
               wkTa=np.ascontiguousarray(inputs["L1_attn_W_k"].T),
               wvTa=np.ascontiguousarray(inputs["L1_attn_W_v"].T),
               woTa=np.ascontiguousarray(inputs["L1_attn_W_o"].T))
    in_maps = []
    for c in range(NCORE):
        mchunk = c % 4
        sel = np.zeros((8, 3), np.float32)
        for k in range(3):
            if k < mchunk:
                sel[:, k] = 1.0
        pr = np.array([[mchunk - 1 if mchunk > 0 else 4]], np.int32)
        m_ = dict(com)
        m_.update(l0_tables(C, c))
        m_.update(x0in=xs[c], smask=l0_smask(C, mchunk == 0), psel=sel, prevrank=pr)
        in_maps.append(m_)
    res = run_bass_kernel_spmd(P.nc, in_maps, core_ids=list(range(NCORE)))
    return gather_x([np.asarray(res.results[c]["yout"]) for c in range(NCORE)])


def prep_x(L0_attn_x):
    xs = []
    for c in range(NCORE):
        b, t0 = c // 4, (c % 4) * TPC
        xs.append(to_fm(np.ascontiguousarray(L0_attn_x[b, t0:t0 + TPC, :])))
    return xs


def gather_x(xs):
    out = np.empty((BB, SS, DD), np.float32)
    for c in range(NCORE):
        b, t0 = c // 4, (c % 4) * TPC
        out[b, t0:t0 + TPC, :] = from_fm(xs[c])
    return out


FUSED = True

def kernel(**inputs):
    inputs = {k: np.asarray(v) for k, v in inputs.items()}
    if FUSED:
        return run_fused(inputs)
    C = host_constants()
    xs = prep_x(np.ascontiguousarray(inputs["L0_attn_x"], dtype=np.float32))
    xs = run_l0call(xs, inputs, C)
    xs = [np.ascontiguousarray(np.asarray(v)) for v in xs]
    xs = run_ssm(xs, inputs["L0_attn_W_q"], inputs["L0_attn_W_k"],
                 inputs["L0_attn_W_v"], inputs["L0_attn_W_alpha_w"],
                 inputs["L0_attn_W_alpha_b"], inputs["L0_attn_W_o"], C)
    xs = [np.ascontiguousarray(np.asarray(v)) for v in xs]
    xs = run_l0call(xs, inputs, C)
    xs = [np.ascontiguousarray(np.asarray(v)) for v in xs]
    xs = run_l1ffn(xs, inputs["L1_ffn_w1"], inputs["L1_ffn_w2"],
                   inputs["L1_ffn_b1"], inputs["L1_ffn_b2"])
    xs = [np.ascontiguousarray(np.asarray(v)) for v in xs]
    xs = run_l1attn(xs, inputs["L1_attn_W_q"], inputs["L1_attn_W_k"],
                    inputs["L1_attn_W_v"], inputs["L1_attn_W_o"])
    xs = [np.ascontiguousarray(np.asarray(v)) for v in xs]
    xs = run_l1ffn(xs, inputs["L1_ffn_w1"], inputs["L1_ffn_w2"],
                   inputs["L1_ffn_b1"], inputs["L1_ffn_b2"])
    xs = [np.asarray(v) for v in xs]
    return gather_x(xs)
